# version 01
# baseline (speedup 1.0000x reference)
import numpy as np
import ml_dtypes
from contextlib import ExitStack

import concourse.bass as bass
import concourse.mybir as mybir
from concourse.bass_utils import run_bass_kernel_spmd

F32 = mybir.dt.float32
BF16 = mybir.dt.bfloat16
AF = mybir.ActivationFunctionType
ALU = mybir.AluOpType
AX = mybir.AxisListType
BFNP = ml_dtypes.bfloat16

B_PER_CORE = 2
S = 2048
D = 1024
NC_ = 8
NT = S // 512
LN_EPS = np.float32(1e-5)
RMS_EPS = np.float32(1.1920929e-07)


class SB:

    ENG = ("pe", "dve", "act", "pool", "sp")

    def __init__(self, nc, stack):
        self.nc = nc
        self.stack = stack
        self.prog = {e: [] for e in self.ENG}
        self.sem = {}
        self.cnt = {}
        for e in self.ENG:
            self.sem[e] = stack.enter_context(nc.semaphore(f"S{e}"))
            self.cnt[e] = 0
        self.slot_of = {}
        self.seen = {e: {} for e in self.ENG}
        self.writer = {}
        self.readers = {}
        self.pending = {e: [] for e in self.ENG}
        self.bar_snapshot = {}

    def _slot_sem(self, slot):
        if slot not in self.slot_of:
            name = f"d_{slot}"
            assert len(self.slot_of) < 96, "too many DMA slots"
            self.sem[name] = self.stack.enter_context(self.nc.semaphore(name))
            self.cnt[name] = 0
            self.slot_of[slot] = name
        return self.slot_of[slot]

    def _force_inc(self, eng):
        ent = None
        for it in reversed(self.prog[eng]):
            if isinstance(it, dict) and it["inc"] is None:
                ent = it
                break
        assert ent is not None, "no quiet op to inc"
        self.cnt[eng] += 1
        c = self.cnt[eng]
        ent["inc"] = (self.sem[eng], 1)
        for buf, kind in self.pending[eng]:
            if kind == "w" and self.writer.get(buf) == ("pend", eng):
                self.writer[buf] = ("done", eng, c)
            if kind == "r":
                rd = self.readers.setdefault(buf, {})
                if rd.get(("pend", eng)):
                    del rd[("pend", eng)]
                rd[eng] = c
        self.pending[eng] = []
        return c

    def _need(self, eng, deps, rec, consumer_is_dma):
        if rec[0] == "done":
            _, sem, c = rec
            deps[sem] = max(deps.get(sem, 0), c)
        else:
            peng = rec[1]
            if peng == eng and not consumer_is_dma:
                return
            c = self._force_inc(peng)
            deps[peng] = max(deps.get(peng, 0), c)

    def op(self, eng, fn, reads=(), writes=(), slot=None, inc=True):
        dma = slot is not None
        deps = {}
        for r in reads:
            w = self.writer.get(r)
            if w:
                self._need(eng, deps, w, dma)
        for wbuf in writes:
            w = self.writer.get(wbuf)
            if w:
                self._need(eng, deps, w, dma)
            for key, c in list(self.readers.get(wbuf, {}).items()):
                if key == ("pend",):
                    continue
                if isinstance(key, tuple):
                    peng = key[1]
                    if peng == eng and not dma:
                        continue
                    cc = self._force_inc(peng)
                    deps[peng] = max(deps.get(peng, 0), cc)
                else:
                    deps[key] = max(deps.get(key, 0), c)
        for sem, c in sorted(deps.items()):
            if self.seen[eng].get(sem, 0) >= c:
                continue
            h = self.sem[sem]
            self.prog[eng].append(lambda e, h=h, c=c: e.wait_ge(h, c))
            self.seen[eng][sem] = c
        if dma:
            semname = self._slot_sem(slot)
            self.cnt[semname] += 16
            c = self.cnt[semname]
            self.prog[eng].append({"fn": fn, "inc": (self.sem[semname], 16)})
            for wbuf in writes:
                self.writer[wbuf] = ("done", semname, c)
                self.readers[wbuf] = {}
            for r in reads:
                self.readers.setdefault(r, {})[semname] = c
        elif inc:
            self.cnt[eng] += 1
            c = self.cnt[eng]
            self.prog[eng].append({"fn": fn, "inc": (self.sem[eng], 1)})
            for buf, kind in self.pending[eng]:
                if kind == "w" and self.writer.get(buf) == ("pend", eng):
                    self.writer[buf] = ("done", eng, c)
                if kind == "r":
                    rd = self.readers.setdefault(buf, {})
                    if ("pend", eng) in rd:
                        del rd[("pend", eng)]
                    rd[eng] = c
            self.pending[eng] = []
            for wbuf in writes:
                self.writer[wbuf] = ("done", eng, c)
                self.readers[wbuf] = {}
            for r in reads:
                self.readers.setdefault(r, {})[eng] = c
        else:
            self.prog[eng].append({"fn": fn, "inc": None})
            for wbuf in writes:
                self.writer[wbuf] = ("pend", eng)
                self.readers[wbuf] = {}
                self.pending[eng].append((wbuf, "w"))
            for r in reads:
                self.readers.setdefault(r, {})[("pend", eng)] = True
                self.pending[eng].append((r, "r"))

    def barrier(self):
        for e in self.ENG:
            if self.pending[e]:
                self._force_inc(e)
        snap = dict(self.cnt)
        for e in self.ENG:
            for sem, c in snap.items():
                if c == 0 or sem == e:
                    continue
                if self.bar_snapshot.get((e, sem), -1) >= c:
                    continue
                if self.seen[e].get(sem, 0) >= c:
                    self.bar_snapshot[(e, sem)] = c
                    continue
                h = self.sem[sem]
                self.prog[e].append(lambda en, h=h, c=c: en.wait_ge(h, c))
                self.seen[e][sem] = c
                self.bar_snapshot[(e, sem)] = c

    def finalize(self):
        for e in self.ENG:
            if self.pending[e]:
                self._force_inc(e)
        self.barrier()


def _prep(ins):
    f = np.float32
    H = {}
    x = ins["L0_attn_x"]
    H["_x_fm"] = np.ascontiguousarray(x.transpose(0, 2, 1))

    perm = np.array([[[h * 192 + d * 3 + s for d in range(64)] for h in range(16)]
                     for s in range(3)]).reshape(-1)
    WT = np.ascontiguousarray(ins["L0_attn_W_qkv"][perm].T)
    bq = ins["L0_attn_b_qkv"][perm]
    H["wqk"] = np.ascontiguousarray(
        WT[:, :2048].reshape(8, 128, 16, 128).transpose(2, 0, 1, 3))
    H["wv"] = np.ascontiguousarray(WT[:, 2048:].reshape(8, 128, 1024))
    H["bqk"] = np.ascontiguousarray(bq[:2048].reshape(16, 128).T)
    H["bv"] = np.ascontiguousarray(bq[2048:].reshape(1, 1024))
    WoT = np.ascontiguousarray(ins["L0_attn_W_o"].T)
    H["wo"] = np.ascontiguousarray(
        WoT.reshape(8, 128, 8, 128).transpose(2, 0, 1, 3))
    H["bo"] = np.ascontiguousarray(ins["L0_attn_b_o"].reshape(8, 128).T)

    H["rt0"] = np.ascontiguousarray(ins["L0_ffn_router_w"].T.reshape(8, 128, 8))
    H["eb0"] = np.ascontiguousarray(ins["L0_ffn_expert_bias"].reshape(1, 8))
    w1 = np.concatenate([ins["L0_ffn_sw1"], ins["L0_ffn_w1"]], 0)
    w3 = np.concatenate([ins["L0_ffn_sw3"], ins["L0_ffn_w3"]], 0)
    w2 = np.concatenate([ins["L0_ffn_sw2"], ins["L0_ffn_w2"]], 0)
    H["w1s"] = np.ascontiguousarray(w1.transpose(0, 2, 1).reshape(10, 8, 128, 2048))
    H["w3s"] = np.ascontiguousarray(w3.transpose(0, 2, 1).reshape(10, 8, 128, 2048))
    H["w2s"] = np.ascontiguousarray(w2.transpose(0, 2, 1).reshape(10, 16, 128, 1024))

    H["ln3g"] = np.ascontiguousarray(ins["L1_attn_ln_g"].reshape(8, 128).T)
    H["ln3b"] = np.ascontiguousarray(ins["L1_attn_ln_b"].reshape(8, 128).T)
    H["taps"] = np.ascontiguousarray(
        ins["L1_attn_mix_w"][:, 0, :].reshape(8, 128, 3).transpose(1, 0, 2).reshape(128, 24))
    H["mixb"] = np.ascontiguousarray(ins["L1_attn_mix_b"].reshape(8, 128).T)

    def blk_io(W, mo, co):
        i, o = W.shape
        return np.ascontiguousarray(
            W.reshape(co, 128, mo, o // mo).transpose(2, 0, 1, 3))

    H["wi"] = blk_io(ins["L1_attn_Wi"], 8, 8)
    H["bi"] = np.ascontiguousarray(ins["L1_attn_bi"].reshape(8, 128).T)
    H["wg"] = blk_io(ins["L1_attn_Wg"], 8, 8)
    H["bg"] = np.ascontiguousarray(ins["L1_attn_bg"].reshape(8, 128).T)
    H["wout"] = blk_io(ins["L1_attn_Wout"], 8, 8)
    H["bout"] = np.ascontiguousarray(ins["L1_attn_bout"].reshape(8, 128).T)
    H["wsin"] = blk_io(ins["L1_attn_Wsin"], 2, 8)
    H["bsin"] = np.ascontiguousarray(ins["L1_attn_bsin"].reshape(2, 128).T)
    H["wd"] = blk_io(ins["L1_attn_Wd"], 2, 8)
    H["bd"] = np.ascontiguousarray(ins["L1_attn_bd"].reshape(2, 128).T)
    H["wsel"] = blk_io(ins["L1_attn_Wsel"], 2, 8)
    H["bsel"] = np.ascontiguousarray(ins["L1_attn_bsel"].reshape(2, 128).T)
    H["wso"] = blk_io(ins["L1_attn_Wso"], 8, 2)
    H["bso"] = np.ascontiguousarray(ins["L1_attn_bso"].reshape(8, 128).T)

    H["rt1"] = np.ascontiguousarray(ins["L1_ffn_router_w"].T.reshape(8, 128, 8))
    H["fcs"] = np.ascontiguousarray(
        ins["L1_ffn_expert_fc"].reshape(8, 8, 128, 4096))
    prT = ins["L1_ffn_expert_proj"].transpose(0, 2, 1)
    H["prs"] = np.ascontiguousarray(
        prT.reshape(8, 32, 128, 2, 512).transpose(0, 3, 1, 2, 4))

    H["idn"] = np.eye(128, dtype=BFNP)
    masks = np.zeros((4, 128, 512), np.float32)
    for oi, o in enumerate((0, 128, 256, 384)):
        for kk in range(128):
            masks[oi, kk, o + kk:] = 1.0
    H["masks"] = masks.astype(BFNP)
    return H


_IN_SPECS = [
    ("wqk", [16, 8, 128, 128], F32), ("wv", [8, 128, 1024], F32),
    ("bqk", [128, 16], F32), ("bv", [1, 1024], F32),
    ("wo", [8, 8, 128, 128], F32), ("bo", [128, 8], F32),
    ("rt0", [8, 128, 8], F32), ("eb0", [1, 8], F32),
    ("w1s", [10, 8, 128, 2048], F32), ("w3s", [10, 8, 128, 2048], F32),
    ("w2s", [10, 16, 128, 1024], F32),
    ("ln3g", [128, 8], F32), ("ln3b", [128, 8], F32),
    ("taps", [128, 24], F32), ("mixb", [128, 8], F32),
    ("wi", [8, 8, 128, 128], F32), ("bi", [128, 8], F32),
    ("wg", [8, 8, 128, 128], F32), ("bg", [128, 8], F32),
    ("wout", [8, 8, 128, 128], F32), ("bout", [128, 8], F32),
    ("wsin", [2, 8, 128, 128], F32), ("bsin", [128, 2], F32),
    ("wd", [2, 8, 128, 128], F32), ("bd", [128, 2], F32),
    ("wsel", [2, 8, 128, 128], F32), ("bsel", [128, 2], F32),
    ("wso", [8, 2, 128, 128], F32), ("bso", [128, 8], F32),
    ("rt1", [8, 128, 8], F32),
    ("fcs", [8, 8, 128, 4096], F32), ("prs", [8, 2, 32, 128, 512], F32),
    ("idn", [128, 128], BF16), ("masks", [4, 128, 512], BF16),
    ("x_fm", [B_PER_CORE, 1024, 2048], F32),
]

_SCRATCH = [
    ("qdbg", [B_PER_CORE, 1024, 2048], BF16),
    ("kdbg", [B_PER_CORE, 1024, 2048], BF16),
    ("vdbg", [B_PER_CORE, 2048, 1024], BF16),
    ("zdbg", [B_PER_CORE, 1024, 2048], BF16),
    ("xnbf_d", [B_PER_CORE, 1024, 2048], BF16),
    ("rnbf_d", [B_PER_CORE, 1024, 2048], BF16),
    ("s_d", [B_PER_CORE, 1024, 2048], F32),
    ("gmoe_d", [B_PER_CORE * S, 1024], F32),
    ("x1_d", [B_PER_CORE, 1024, 2048], F32),
    ("xn1_d", [B_PER_CORE, 1024, 2048], F32),
    ("xn1bf_d", [B_PER_CORE, 1024, 2048], BF16),
    ("s1_d", [B_PER_CORE, 1024, 2048], F32),
    ("g1moe_d", [B_PER_CORE * S, 1024], F32),
]


class P:

    def __init__(self, nc, sb, t):
        self.nc = nc
        self.sb = sb
        self.t = t
        self.cst = None
        self.dump = set()


def regs(name, *idx_ranges):
    out = [name]
    for r in idx_ranges:
        out = [f"{o}:{i}" for o in out for i in r]
    return out


def emit_ln_rows(p, pfx, p_s1, p_s2, rows, eps_ap):
    sb = p.sb
    sb.op("act", lambda e: e.activation(
        rows[:, 0:S], p_s1[:], AF.Identity, scale=1.0 / D),
        reads=[f"{pfx}s1:{i}" for i in range(4)], writes=[f"{pfx}mu"])
    sb.op("act", lambda e: e.activation(
        rows[:, 2 * S:3 * S], p_s2[:], AF.Identity, scale=1.0 / D),
        reads=[f"{pfx}s2:{i}" for i in range(4)], writes=[f"{pfx}m2"])
    sb.op("dve", lambda e: e.tensor_mul(
        rows[:, S:2 * S], rows[:, 0:S], rows[:, 0:S]),
        reads=[f"{pfx}mu"], writes=[f"{pfx}rs"])
    sb.op("dve", lambda e: e.tensor_sub(
        rows[:, 2 * S:3 * S], rows[:, 2 * S:3 * S], rows[:, S:2 * S]),
        reads=[f"{pfx}m2", f"{pfx}rs"], writes=[f"{pfx}m2"])
    sb.op("act", lambda e: e.activation(
        rows[:, S:2 * S], rows[:, 2 * S:3 * S], AF.Sqrt, bias=eps_ap),
        reads=[f"{pfx}m2", "cst"], writes=[f"{pfx}rs"])
    sb.op("dve", lambda e: e.reciprocal(rows[:, S:2 * S], rows[:, S:2 * S]),
          reads=[f"{pfx}rs"], writes=[f"{pfx}rs"])


def emit_s0(p, b):
    nc, sb, t = p.nc, p.sb, p.t
    with ExitStack() as st:
        x_sb = st.enter_context(nc.sbuf_tensor(f"s0x{b}", [128, NC_ * S], F32))
        xn_sb = st.enter_context(nc.sbuf_tensor(f"s0xn{b}", [128, NC_ * S], F32))
        cb = st.enter_context(nc.sbuf_tensor(f"s0cb{b}", [128, S], BF16))
        sq = st.enter_context(nc.sbuf_tensor(f"s0sq{b}", [128, S], BF16))
        ones = st.enter_context(nc.sbuf_tensor(f"s0on{b}", [128, 1], BF16))
        onesr = st.enter_context(nc.sbuf_tensor(f"s0onr{b}", [1, 128], F32))
        rows = st.enter_context(nc.sbuf_tensor(f"s0rw{b}", [1, 4 * S], F32))
        sb.op("dve", lambda e: e.memset(ones[:], 1.0), writes=["ones"])
        sb.op("dve", lambda e: e.memset(onesr[:], 1.0), writes=["onesr"])
        for c in range(NC_):
            sb.op("sp", lambda e, c=c: e.dma_start(
                out=x_sb[:, c * S:(c + 1) * S],
                in_=t["x_fm"][b, c * 128:(c + 1) * 128, :]),
                writes=[f"x_sb:{c}"], slot=f"ld{c}")
        with (
            nc.psum_tensor(f"p0a{b}", [1, S], F32) as p_s1,
            nc.psum_tensor(f"p0b{b}", [1, S], F32) as p_s2,
        ):
            for c in range(NC_):
                xs = x_sb[:, c * S:(c + 1) * S]
                sb.op("pool", lambda e, xs=xs: e.tensor_copy(cb[:], xs),
                      reads=[f"x_sb:{c}"], writes=["cb"])
                sb.op("act", lambda e, xs=xs: e.activation(sq[:], xs, AF.Square),
                      reads=[f"x_sb:{c}"], writes=["sq"])
                for tc in range(4):
                    sl = slice(tc * 512, (tc + 1) * 512)
                    sb.op("pe", lambda e, sl=sl, c=c: e.matmul(
                        p_s1[:, sl], ones[:], cb[:, sl],
                        start=(c == 0), stop=(c == NC_ - 1)),
                        reads=["cb", "ones"], writes=[f"As1:{tc}"])
                    sb.op("pe", lambda e, sl=sl, c=c: e.matmul(
                        p_s2[:, sl], ones[:], sq[:, sl],
                        start=(c == 0), stop=(c == NC_ - 1)),
                        reads=["sq", "ones"], writes=[f"As2:{tc}"])
            emit_ln_rows(p, "A", p_s1, p_s2, rows, p.cst[0:1, 0:1])
        with nc.psum_tensor(f"p0c{b}", [128, 1024], F32) as p_b:
            for tc in range(4):
                sl = slice(tc * 512, (tc + 1) * 512)
                sb.op("pe", lambda e, sl=sl: e.matmul(
                    p_b[:, 0:512], onesr[:], rows[0:1, 0:S][:, sl],
                    start=True, stop=True),
                    reads=["Amu", "onesr"],
                    writes=["Bmu"] + ([f"As1:{i}" for i in range(4)]
                                      + [f"As2:{i}" for i in range(4)] if tc == 0 else []))
                sb.op("pe", lambda e, sl=sl: e.matmul(
                    p_b[:, 512:1024], onesr[:], rows[0:1, S:2 * S][:, sl],
                    start=True, stop=True),
                    reads=["Ars", "onesr"], writes=["Brs"])
                for c in range(NC_):
                    xs = x_sb[:, c * S + tc * 512:c * S + (tc + 1) * 512]
                    xn = xn_sb[:, c * S + tc * 512:c * S + (tc + 1) * 512]
                    sb.op("dve", lambda e, xs=xs, xn=xn: e.tensor_sub(
                        xn, xs, p_b[:, 0:512]),
                        reads=[f"x_sb:{c}", "Bmu"], writes=[f"xn:{c}:{tc}"])
                    sb.op("dve", lambda e, xn=xn: e.tensor_mul(xn, xn, p_b[:, 512:1024]),
                        reads=[f"xn:{c}:{tc}", "Brs"], writes=[f"xn:{c}:{tc}"])
        with nc.psum_tensor(f"p0d{b}", [1, S], F32) as p_rms:
            for c in range(NC_):
                xn = xn_sb[:, c * S:(c + 1) * S]
                rd = [f"xn:{c}:{tc}" for tc in range(4)]
                sb.op("pool", lambda e, xn=xn: e.tensor_copy(cb[:], xn),
                      reads=rd, writes=["cb"])
                sb.op("pool", lambda e, c=c: e.dma_start(
                    out=t["xnbf_d"][b, c * 128:(c + 1) * 128, :], in_=cb[:]),
                    reads=["cb"], writes=[f"xnbf_d:{b}:{c}"], slot="st0")
                sb.op("act", lambda e, xn=xn: e.activation(sq[:], xn, AF.Square),
                      reads=rd + (["Bmu", "Brs"] if c == 0 else []), writes=["sq"])
                for tc in range(4):
                    sl = slice(tc * 512, (tc + 1) * 512)
                    sb.op("pe", lambda e, sl=sl, c=c: e.matmul(
                        p_rms[:, sl], ones[:], sq[:, sl],
                        start=(c == 0), stop=(c == NC_ - 1)),
                        reads=["sq", "ones"],
                        writes=[f"Rs:{tc}"] + (["Bmu", "Brs"] if c == 0 and tc == 0 else []))
            sb.op("act", lambda e: e.activation(
                rows[:, 3 * S:4 * S], p_rms[:], AF.Sqrt, scale=1.0 / D,
                bias=p.cst[0:1, 1:2]),
                reads=[f"Rs:{i}" for i in range(4)] + ["cst"], writes=["rmsr"])
            sb.op("dve", lambda e: e.reciprocal(
                rows[:, 3 * S:4 * S], rows[:, 3 * S:4 * S]),
                reads=["rmsr"], writes=["rmsr"])
        with nc.psum_tensor(f"p0e{b}", [128, S], F32) as p_c:
            for tc in range(4):
                sl = slice(tc * 512, (tc + 1) * 512)
                sb.op("pe", lambda e, sl=sl: e.matmul(
                    p_c[:, sl], onesr[:], rows[0:1, 3 * S:4 * S][:, sl],
                    start=True, stop=True),
                    reads=["rmsr", "onesr"],
                    writes=["Crm"] + ([f"Rs:{i}" for i in range(4)] if tc == 0 else []))
            for c in range(NC_):
                xn = xn_sb[:, c * S:(c + 1) * S]
                rd = [f"xn:{c}:{tc}" for tc in range(4)]
                sb.op("dve", lambda e, xn=xn: e.tensor_mul(cb[:], xn, p_c[:]),
                      reads=rd + ["Crm"], writes=["cb"])
                sb.op("sp", lambda e, c=c: e.dma_start(
                    out=t["rnbf_d"][b, c * 128:(c + 1) * 128, :], in_=cb[:]),
                    reads=["cb"], writes=[f"rnbf_d:{b}:{c}"], slot="st1")
    p.sb.barrier()


def emit_s12(p, b):
    nc, sb, t = p.nc, p.sb, p.t
    sb.barrier()
    with ExitStack() as st:
        rn = st.enter_context(nc.sbuf_tensor(f"a_rn{b}", [128, NC_ * S], BF16))
        q_sb = st.enter_context(nc.sbuf_tensor(f"a_q{b}", [128, NC_ * S], BF16))
        k_sb = st.enter_context(nc.sbuf_tensor(f"a_k{b}", [128, NC_ * S], BF16))
        v_sb = st.enter_context(nc.sbuf_tensor(f"a_v{b}", [128, 16 * 1024], BF16))
        z_sb = st.enter_context(nc.sbuf_tensor(f"a_z{b}", [128, NC_ * S], BF16))
        wt = [st.enter_context(nc.sbuf_tensor(f"a_w{b}{i}", [128, NC_ * 128], BF16))
              for i in range(2)]
        wv_sb = st.enter_context(nc.sbuf_tensor(f"a_wv{b}", [128, NC_ * 1024], BF16))
        bqk = st.enter_context(nc.sbuf_tensor(f"a_bqk{b}", [128, 16], F32))
        bo_sb = st.enter_context(nc.sbuf_tensor(f"a_bo{b}", [128, 8], F32))
        bv_sb = st.enter_context(nc.sbuf_tensor(f"a_bv{b}", [1, 1024], BF16))
        onesb = st.enter_context(nc.sbuf_tensor(f"a_on{b}", [1, 128], BF16))
        msk = st.enter_context(nc.sbuf_tensor(f"a_msk{b}", [128, 4 * 512], BF16))
        e_bf = st.enter_context(nc.sbuf_tensor(f"a_e{b}", [128, S], BF16))
        etmp = st.enter_context(nc.sbuf_tensor(f"a_et{b}", [128, 512], BF16))
        vv = st.enter_context(nc.sbuf_tensor(f"a_vv{b}", [128, 64], BF16))
        zr = st.enter_context(nc.sbuf_tensor(f"a_zr{b}", [128, 16], F32))
        sv = st.enter_context(nc.sbuf_tensor(f"a_sv{b}", [128, S], F32))
        sb.op("dve", lambda e: e.memset(onesb[:], 1.0), writes=["onesb"])
        for c in range(NC_):
            sb.op("sp", lambda e, c=c: e.dma_start(
                out=rn[:, c * S:(c + 1) * S],
                in_=t["rnbf_d"][b, c * 128:(c + 1) * 128, :]),
                reads=[f"rnbf_d:{b}:{c}"], writes=[f"rn:{c}"], slot=f"ld{c}")
        sb.op("sp", lambda e: e.dma_start(out=bqk[:], in_=t["bqk"]),
              writes=["bqk"], slot="bia")
        sb.op("sp", lambda e: e.dma_start(out=bo_sb[:], in_=t["bo"]),
              writes=["bo"], slot="bia")
        sb.op("pool", lambda e: e.dma_start(out=bv_sb[:], in_=t["bv"]),
              writes=["bv"], slot="bia2")
        sb.op("sp", lambda e: e.dma_start(
            out=msk[:].rearrange("p (o n) -> p o n", o=4),
            in_=t["masks"].rearrange("o p n -> p o n")),
            writes=["msk"], slot="msk")
        sb.op("pool", lambda e: e.dma_start(
            out=wv_sb[:].rearrange("p (c n) -> p c n", c=NC_),
            in_=t["wv"].rearrange("c p n -> p c n")),
            writes=["wv_sb"], slot="wv")
        with ExitStack() as pst:
            pqk = [pst.enter_context(nc.psum_tensor(f"a_pq{b}{i}", [128, 512], F32))
                   for i in range(8)]
            for m in range(16):
                wb = m % 2
                sb.op("pool", lambda e, m=m, wb=wb: e.dma_start(
                    out=wt[wb][:].rearrange("p (c n) -> p c n", c=NC_),
                    in_=t["wqk"][m].rearrange("c p n -> p c n")),
                    writes=[f"wt{wb}"], slot=f"wqk{wb}")
                pbase = (m % 2) * 4
                for c in range(NC_):
                    for tc in range(4):
                        sb.op("pe", lambda e, m=m, c=c, tc=tc, wb=wb, pbase=pbase: e.matmul(
                            pqk[pbase + tc][:],
                            wt[wb][:, c * 128:(c + 1) * 128],
                            rn[:, c * S + tc * 512:c * S + (tc + 1) * 512],
                            start=(c == 0), stop=(c == NC_ - 1)),
                            reads=[f"wt{wb}", f"rn:{c}"],
                            writes=[f"pqk:{pbase + tc}"],
                            inc=(c == NC_ - 1))
                dst = q_sb if m < 8 else k_sb
                mm = m % 8
                for tc in range(4):
                    sb.op("act", lambda e, m=m, mm=mm, tc=tc, dst=dst, pbase=pbase: e.activation(
                        dst[:, mm * S + tc * 512:mm * S + (tc + 1) * 512],
                        pqk[pbase + tc][:], AF.Identity, bias=bqk[:, m:m + 1]),
                        reads=[f"pqk:{pbase + tc}", "bqk"],
                        writes=[f"qk:{m}:{tc}"])
        sb.barrier()
        with ExitStack() as pst:
            pv = [pst.enter_context(nc.psum_tensor(f"a_pv{b}{i}", [128, 1024], F32))
                  for i in range(2)]
            for tt in range(16):
                pb = tt % 2
                for c in range(NC_):
                    for half in range(2):
                        sb.op("pe", lambda e, tt=tt, c=c, half=half, pb=pb: e.matmul(
                            pv[pb][:, half * 512:(half + 1) * 512],
                            rn[:, c * S + tt * 128:c * S + (tt + 1) * 128],
                            wv_sb[:, c * 1024 + half * 512:c * 1024 + (half + 1) * 512],
                            start=(c == 0), stop=False),
                            reads=["wv_sb", f"rn:{c}"], writes=[f"pv:{pb}"],
                            inc=False)
                for half in range(2):
                    sb.op("pe", lambda e, half=half, pb=pb: e.matmul(
                        pv[pb][:, half * 512:(half + 1) * 512],
                        onesb[:], bv_sb[:, half * 512:(half + 1) * 512],
                        start=False, stop=True),
                        reads=["onesb", "bv"], writes=[f"pv:{pb}"],
                        inc=(half == 1))
                sb.op("dve", lambda e, tt=tt, pb=pb: e.tensor_copy(
                    v_sb[:, tt * 1024:(tt + 1) * 1024], pv[pb][:]),
                    reads=[f"pv:{pb}"], writes=[f"v:{tt}"])
        if "qdbg" in p.dump:
            sb.barrier()
            for c in range(NC_):
                for nm, buf in (("qdbg", q_sb), ("kdbg", k_sb)):
                    sb.op("sp", lambda e, c=c, nm=nm, buf=buf: e.dma_start(
                        out=t[nm][b, c * 128:(c + 1) * 128, :],
                        in_=buf[:, c * S:(c + 1) * S]),
                        reads=[], writes=[], slot="dbg")
            for tt in range(16):
                sb.op("sp", lambda e, tt=tt: e.dma_start(
                    out=t["vdbg"][b, tt * 128:(tt + 1) * 128, :],
                    in_=v_sb[:, tt * 1024:(tt + 1) * 1024]),
                    reads=[], writes=[], slot="dbg")
            sb.barrier()
        sb.barrier()
        with (
            nc.psum_tensor(f"a_ps{b}", [128, S], F32) as psc,
            nc.psum_tensor(f"a_pz{b}", [64, S], F32) as pz,
        ):
            for h in range(16):
                ch = h // 2
                r0 = (h % 2) * 64
                qh = q_sb[r0:r0 + 64, ch * S:(ch + 1) * S]
                kh = k_sb[r0:r0 + 64, ch * S:(ch + 1) * S]
                for kt in range(16):
                    c0 = kt // 4
                    for cc in range(c0, 4):
                        sb.op("pe", lambda e, kt=kt, cc=cc, qh=qh, kh=kh: e.matmul(
                            psc[:, cc * 512:(cc + 1) * 512],
                            kh[:, kt * 128:(kt + 1) * 128],
                            qh[:, cc * 512:(cc + 1) * 512],
                            start=True, stop=True),
                            reads=[f"qk:{ch}:{cc}", f"qk:{8 + ch}:{kt // 4}"],
                            writes=[f"psc:{cc}"])
                        sb.op("dve", lambda e, cc=cc: e.reduce_max(
                            zr[:, 8 + cc:9 + cc], psc[:, cc * 512:(cc + 1) * 512],
                            axis=AX.X, negate=True),
                            reads=[f"psc:{cc}"], writes=[f"nmx:{cc}"])
                    for cc in range(c0 + 1, 4):
                        src0 = 8 + c0 if cc == c0 + 1 else 15
                        sb.op("dve", lambda e, cc=cc, src0=src0: e.tensor_tensor(
                            zr[:, 15:16], zr[:, src0:src0 + 1], zr[:, 8 + cc:9 + cc],
                            ALU.min),
                            reads=[f"nmx:{cc}", f"nmx:{c0}" if cc == c0 + 1 else "nmf"],
                            writes=["nmf"])
                    if c0 == 3:
                        sb.op("dve", lambda e: e.tensor_copy(
                            zr[:, 15:16], zr[:, 11:12]),
                            reads=["nmx:3"], writes=["nmf"])
                    o = kt % 4
                    sb.op("act", lambda e, c0=c0: e.activation(
                        etmp[:], psc[:, c0 * 512:(c0 + 1) * 512], AF.Exp,
                        bias=zr[:, 15:16]),
                        reads=[f"psc:{c0}", "nmf"], writes=["etmp"])
                    sb.op("dve", lambda e, c0=c0, o=o: e.tensor_mul(
                        e_bf[:, c0 * 512:(c0 + 1) * 512], etmp[:],
                        msk[:, o * 512:(o + 1) * 512]),
                        reads=["etmp", "msk"], writes=[f"ee:{c0}"])
                    sb.op("dve", lambda e, c0=c0: e.reduce_sum(
                        zr[:, c0:c0 + 1], e_bf[:, c0 * 512:(c0 + 1) * 512], axis=AX.X),
                        reads=[f"ee:{c0}"], writes=[f"zs:{c0}"])
                    for cc in range(c0 + 1, 4):
                        sb.op("act", lambda e, cc=cc: e.activation(
                            e_bf[:, cc * 512:(cc + 1) * 512],
                            psc[:, cc * 512:(cc + 1) * 512], AF.Exp,
                            bias=zr[:, 15:16], accum_out=zr[:, cc:cc + 1]),
                            reads=[f"psc:{cc}", "nmf"],
                            writes=[f"ee:{cc}", f"zs:{cc}"])
                    for cc in range(c0 + 1, 4):
                        src0 = c0 if cc == c0 + 1 else 4
                        sb.op("dve", lambda e, cc=cc, src0=src0: e.tensor_add(
                            zr[:, 4:5], zr[:, src0:src0 + 1], zr[:, cc:cc + 1]),
                            reads=[f"zs:{cc}", f"zs:{c0}" if cc == c0 + 1 else "zt"],
                            writes=["zt"])
                    if c0 == 3:
                        sb.op("dve", lambda e: e.tensor_copy(zr[:, 4:5], zr[:, 3:4]),
                              reads=["zs:3"], writes=["zt"])
                    sb.op("dve", lambda e: e.reciprocal(zr[:, 5:6], zr[:, 4:5]),
                          reads=["zt"], writes=["zrec"])
                    sb.op("dve", lambda e, h=h, kt=kt: e.tensor_scalar(
                        vv[:], v_sb[:, kt * 1024 + h * 64:kt * 1024 + h * 64 + 64],
                        zr[:, 5:6], None, ALU.mult),
                        reads=[f"v:{kt}", "zrec"], writes=["vv"])
                    for cc in range(c0, 4):
                        sb.op("pe", lambda e, cc=cc, kt=kt: e.matmul(
                            pz[:, cc * 512:(cc + 1) * 512], vv[:],
                            e_bf[:, cc * 512:(cc + 1) * 512],
                            start=(kt == 0), stop=(kt == 4 * cc + 3)),
                            reads=["vv", f"ee:{cc}"], writes=[f"pz:{cc}"])
                sb.op("dve", lambda e, r0=r0, ch=ch: e.tensor_copy(
                    z_sb[r0:r0 + 64, ch * S:(ch + 1) * S], pz[:]),
                    reads=[f"pz:{cc}" for cc in range(4)], writes=[f"z:{ch}"])
        if "zdbg" in p.dump:
            sb.barrier()
            for c in range(NC_):
                sb.op("sp", lambda e, c=c: e.dma_start(
                    out=t["zdbg"][b, c * 128:(c + 1) * 128, :],
                    in_=z_sb[:, c * S:(c + 1) * S]),
                    reads=[], writes=[], slot="dbg")
            sb.barrier()
        sb.barrier()
        with ExitStack() as pst:
            po = [pst.enter_context(nc.psum_tensor(f"a_po{b}{i}", [128, 512], F32))
                  for i in range(8)]
            for m in range(8):
                wb = m % 2
                pbase = wb * 4
                sb.op("pool", lambda e, m=m, wb=wb: e.dma_start(
                    out=wt[wb][:].rearrange("p (c n) -> p c n", c=NC_),
                    in_=t["wo"][m].rearrange("c p n -> p c n")),
                    writes=[f"wt{wb}"], slot=f"wqk{wb}")
                for c in range(NC_):
                    for tc in range(4):
                        sb.op("pe", lambda e, c=c, tc=tc, wb=wb, pbase=pbase: e.matmul(
                            po[pbase + tc][:],
                            wt[wb][:, c * 128:(c + 1) * 128],
                            z_sb[:, c * S + tc * 512:c * S + (tc + 1) * 512],
                            start=(c == 0), stop=(c == NC_ - 1)),
                            reads=[f"wt{wb}", f"z:{c}"],
                            writes=[f"po:{pbase + tc}"],
                            inc=(c == NC_ - 1))
                for tc in range(4):
                    sb.op("act", lambda e, tc=tc, m=m, pbase=pbase: e.activation(
                        sv[:, tc * 512:(tc + 1) * 512], po[pbase + tc][:],
                        AF.Identity, bias=bo_sb[:, m:m + 1]),
                        reads=[f"po:{pbase + tc}", "bo"], writes=["sv"])
                sb.op("sp", lambda e, m=m: e.dma_start(
                    out=t["s_d"][b, m * 128:(m + 1) * 128, :], in_=sv[:]),
                    reads=["sv"], writes=[f"s_d:{b}:{m}"], slot="sst")
    sb.barrier()


def emit_s3(p):
    nc, sb, t = p.nc, p.sb, p.t
    sb.barrier()
    NTT = 32
    with ExitStack() as st:
        xn = st.enter_context(nc.sbuf_tensor("f_xn", [128, NC_ * 4096], BF16))
        rt = st.enter_context(nc.sbuf_tensor("f_rt", [128, 64], BF16))
        eb = st.enter_context(nc.sbuf_tensor("f_eb", [1, 8], BF16))
        onesb = st.enter_context(nc.sbuf_tensor("f_on", [1, 128], BF16))
        cw = st.enter_context(nc.sbuf_tensor("f_cw", [128, NTT * 10], F32))
        lg = st.enter_context(nc.sbuf_tensor("f_lg", [128, 8], F32))
        rga = st.enter_context(nc.sbuf_tensor("f_rga", [128, 8], F32))
        rgb = st.enter_context(nc.sbuf_tensor("f_rgb", [128, 8], F32))
        rr = st.enter_context(nc.sbuf_tensor("f_rr", [128, 8], F32))
        w1t = st.enter_context(nc.sbuf_tensor("f_w1", [128, NC_ * 2048], BF16))
        w3t = st.enter_context(nc.sbuf_tensor("f_w3", [128, NC_ * 2048], BF16))
        w2t = st.enter_context(nc.sbuf_tensor("f_w2", [128, 16 * 1024], BF16))
        hh = st.enter_context(nc.sbuf_tensor("f_hh", [128, 16 * 512], BF16))
        s1t = st.enter_context(nc.sbuf_tensor("f_s1t", [128, 512], BF16))
        gtmp = [st.enter_context(nc.sbuf_tensor(f"f_gt{i}", [128, 1024], F32))
                for i in range(2)]
        sb.op("dve", lambda e: e.memset(onesb[:], 1.0), writes=["onesb"])
        sb.op("dve", lambda e: e.memset(cw[:], 1.0), writes=["cwall"])
        for b in range(B_PER_CORE):
            for c in range(NC_):
                sb.op("sp", lambda e, b=b, c=c: e.dma_start(
                    out=xn[:, c * 4096 + b * S:c * 4096 + (b + 1) * S],
                    in_=t["xnbf_d"][b, c * 128:(c + 1) * 128, :]),
                    reads=[f"xnbf_d:{b}:{c}"], writes=[f"xn:{c}"],
                    slot=f"ld{c}")
        sb.op("pool", lambda e: e.dma_start(
            out=rt[:].rearrange("p (c n) -> p c n", c=NC_),
            in_=t["rt0"].rearrange("c p n -> p c n")),
            writes=["rt"], slot="bia2")
        sb.op("pool", lambda e: e.dma_start(out=eb[:], in_=t["eb0"]),
              writes=["eb"], slot="bia2")
        with (
            nc.psum_tensor("f_pr0", [128, 512], F32) as pr0,
            nc.psum_tensor("f_pr1", [128, 512], F32) as pr1,
        ):
            for tt in range(NTT):
                hf = tt % 2
                psl = (pr0 if hf == 0 else pr1)[:, 0:8]
                for c in range(NC_):
                    sb.op("pe", lambda e, tt=tt, c=c, psl=psl: e.matmul(
                        psl, xn[:, c * 4096 + tt * 128:c * 4096 + (tt + 1) * 128],
                        rt[:, c * 8:(c + 1) * 8],
                        start=(c == 0), stop=False),
                        reads=[f"xn:{c}", "rt"], writes=[f"pr:{hf}"], inc=False)
                sb.op("pe", lambda e, psl=psl: e.matmul(
                    psl, onesb[:], eb[:], start=False, stop=True),
                    reads=["onesb", "eb"], writes=[f"pr:{hf}"])
                sb.op("dve", lambda e, psl=psl: e.tensor_copy(lg[:], psl),
                      reads=[f"pr:{hf}"], writes=["lg"])
                sb.op("dve", lambda e: e.reduce_max(
                    rr[:, 0:1], lg[:], axis=AX.X, negate=True),
                    reads=["lg"], writes=["rr"])
                sb.op("dve", lambda e: e.reduce_max(rr[:, 1:2], lg[:], axis=AX.X),
                      reads=["lg"], writes=["rr"])
                sb.op("dve", lambda e: e.tensor_scalar(
                    rga[:], lg[:], rr[:, 1:2], None, ALU.is_ge),
                    reads=["lg", "rr"], writes=["rga"])
                sb.op("dve", lambda e: e.scalar_tensor_tensor(
                    rgb[:], rga[:], -1e30, lg[:], ALU.mult, ALU.add),
                    reads=["rga", "lg"], writes=["rgb"])
                sb.op("dve", lambda e: e.reduce_max(rr[:, 2:3], rgb[:], axis=AX.X),
                      reads=["rgb"], writes=["rr"])
                sb.op("dve", lambda e: e.tensor_scalar(
                    rga[:], lg[:], rr[:, 2:3], None, ALU.is_ge),
                    reads=["lg", "rr"], writes=["rga"])
                sb.op("act", lambda e: e.activation(
                    rgb[:], lg[:], AF.Exp, bias=rr[:, 0:1]),
                    reads=["lg", "rr"], writes=["rgb"])
                sb.op("act", lambda e: e.activation(
                    rr[:, 3:4], rr[:, 2:3], AF.Exp, bias=rr[:, 0:1]),
                    reads=["rr"], writes=["rr2"])
                sb.op("dve", lambda e: e.tensor_scalar(
                    rr[:, 4:5], rr[:, 3:4], 1.0, None, ALU.add),
                    reads=["rr2"], writes=["rr2"])
                sb.op("dve", lambda e: e.reciprocal(rr[:, 5:6], rr[:, 4:5]),
                      reads=["rr2"], writes=["rr2"])
                sb.op("dve", lambda e, tt=tt: e.scalar_tensor_tensor(
                    cw[:, tt * 10 + 2:tt * 10 + 10], rgb[:], rr[:, 5:6], rga[:],
                    ALU.mult, ALU.mult),
                    reads=["rgb", "rr2", "rga", "cwall"], writes=[f"cw:{tt}"])
        sb.barrier()
        with ExitStack() as pst:
            ph = [pst.enter_context(nc.psum_tensor(f"f_ph{i}", [128, 512], F32))
                  for i in range(4)]
            po = [pst.enter_context(nc.psum_tensor(f"f_po{i}", [128, 1024], F32))
                  for i in range(2)]
            for j in range(10):
                for wn, wt_, nch in (("w1s", w1t, NC_), ("w3s", w3t, NC_),
                                     ("w2s", w2t, 16)):
                    sb.op("pool", lambda e, wn=wn, wt_=wt_, nch=nch, j=j: e.dma_start(
                        out=wt_[:].rearrange("p (c n) -> p c n", c=nch),
                        in_=t[wn][j].rearrange("c p n -> p c n")),
                        writes=[wn], slot=wn)
                for tb in range(8):
                    for mh in range(16):
                        pa = ph[2 * (mh % 2)]
                        pb_ = ph[2 * (mh % 2) + 1]
                        for c in range(NC_):
                            sb.op("pe", lambda e, mh=mh, c=c, tb=tb, pa=pa: e.matmul(
                                pa[:], w1t[:, c * 2048 + mh * 128:c * 2048 + (mh + 1) * 128],
                                xn[:, c * 4096 + tb * 512:c * 4096 + (tb + 1) * 512],
                                start=(c == 0), stop=(c == NC_ - 1)),
                                reads=["w1s", f"xn:{c}"],
                                writes=[f"ph:{2 * (mh % 2)}"],
                                inc=(c == NC_ - 1))
                        for c in range(NC_):
                            sb.op("pe", lambda e, mh=mh, c=c, tb=tb, pb_=pb_: e.matmul(
                                pb_[:], w3t[:, c * 2048 + mh * 128:c * 2048 + (mh + 1) * 128],
                                xn[:, c * 4096 + tb * 512:c * 4096 + (tb + 1) * 512],
                                start=(c == 0), stop=(c == NC_ - 1)),
                                reads=["w3s", f"xn:{c}"],
                                writes=[f"ph:{2 * (mh % 2) + 1}"],
                                inc=(c == NC_ - 1))
                        sb.op("act", lambda e, pa=pa: e.activation(
                            s1t[:], pa[:], AF.Silu),
                            reads=[f"ph:{2 * (mh % 2)}"], writes=["s1t"])
                        sb.op("dve", lambda e, mh=mh, pb_=pb_: e.tensor_mul(
                            hh[:, mh * 512:(mh + 1) * 512], s1t[:], pb_[:]),
                            reads=["s1t", f"ph:{2 * (mh % 2) + 1}"],
                            writes=[f"hh:{mh}"])
                    for ti in range(4):
                        tt = tb * 4 + ti
                        pp = po[ti % 2]
                        for hc in range(16):
                            for dh in range(2):
                                sb.op("pe", lambda e, hc=hc, dh=dh, ti=ti, pp=pp: e.matmul(
                                    pp[:, dh * 512:(dh + 1) * 512],
                                    hh[:, hc * 512 + ti * 128:hc * 512 + (ti + 1) * 128],
                                    w2t[:, hc * 1024 + dh * 512:hc * 1024 + (dh + 1) * 512],
                                    start=(hc == 0), stop=(hc == 15)),
                                    reads=[f"hh:{hc}", "w2s"],
                                    writes=[f"po:{ti % 2}"],
                                    inc=(hc == 15 and dh == 1))
                        sb.op("dve", lambda e, tt=tt, j=j, ti=ti, pp=pp: e.tensor_scalar(
                            gtmp[ti % 2][:], pp[:], cw[:, tt * 10 + j:tt * 10 + j + 1],
                            None, ALU.mult),
                            reads=[f"po:{ti % 2}", f"cw:{tt}", "cwall"],
                            writes=[f"gt:{ti % 2}"])
                        sb.op("pool", lambda e, tt=tt, ti=ti, j=j: e.dma_start(
                            out=t["gmoe_d"][tt * 128:(tt + 1) * 128, :],
                            in_=gtmp[ti % 2][:],
                            accum_op=(ALU.bypass if j == 0 else ALU.add)),
                            reads=[f"gt:{ti % 2}"], writes=[f"gmoe_d:{tt}"],
                            slot=f"gm{tt}")
    sb.barrier()


def emit_s4(p, b):
    nc, sb, t = p.nc, p.sb, p.t
    sb.barrier()
    with ExitStack() as st:
        gfm = st.enter_context(nc.sbuf_tensor(f"c_g{b}", [128, NC_ * S], F32))
        x1b = [st.enter_context(nc.sbuf_tensor(f"c_x1{b}{i}", [128, S], F32))
               for i in range(2)]
        gt = st.enter_context(nc.sbuf_tensor(f"c_gt{b}", [128, 1024], F32))
        gs = st.enter_context(nc.sbuf_tensor(f"c_gs{b}", [128, 1024], BF16))
        xs_ = [st.enter_context(nc.sbuf_tensor(f"c_xs{b}{i}", [128, S], F32))
               for i in range(2)]
        ss_ = [st.enter_context(nc.sbuf_tensor(f"c_ss{b}{i}", [128, S], F32))
               for i in range(2)]
        idn = st.enter_context(nc.sbuf_tensor(f"c_idn{b}", [128, 128], BF16))
        cb = st.enter_context(nc.sbuf_tensor(f"c_cb{b}", [128, S], BF16))
        sq = st.enter_context(nc.sbuf_tensor(f"c_sq{b}", [128, S], BF16))
        ones = st.enter_context(nc.sbuf_tensor(f"c_on{b}", [128, 1], BF16))
        onesr = st.enter_context(nc.sbuf_tensor(f"c_onr{b}", [1, 128], F32))
        rows = st.enter_context(nc.sbuf_tensor(f"c_rw{b}", [1, 3 * S], F32))
        t12 = st.enter_context(nc.sbuf_tensor(f"c_t12{b}", [128, S], F32))
        sb.op("dve", lambda e: e.memset(ones[:], 1.0), writes=["ones"])
        sb.op("dve", lambda e: e.memset(onesr[:], 1.0), writes=["onesr"])
        sb.op("sp", lambda e: e.dma_start(out=idn[:], in_=t["idn"]),
              writes=["idn"], slot="bia")
        with nc.psum_tensor(f"c_pt{b}", [128, 2048], BF16) as pt:
            for tloc in range(16):
                tt = b * 16 + tloc
                sb.op("sp", lambda e, tt=tt: e.dma_start(
                    out=gt[:], in_=t["gmoe_d"][tt * 128:(tt + 1) * 128, :]),
                    reads=[f"gmoe_d:{tt}"], writes=["gt"], slot="ld0")
                sb.op("act", lambda e: e.activation(gs[:], gt[:], AF.Silu),
                      reads=["gt"], writes=["gs"])
                for c in range(NC_):
                    pb = c % 2
                    sb.op("pe", lambda e, c=c, pb=pb: e.transpose(
                        pt[:, pb * 1024:pb * 1024 + 128],
                        gs[:, c * 128:(c + 1) * 128], idn[:]),
                        reads=["gs", "idn"], writes=[f"ptr:{pb}"])
                    sb.op("dve", lambda e, c=c, tloc=tloc, pb=pb: e.tensor_copy(
                        gfm[:, c * S + tloc * 128:c * S + (tloc + 1) * 128],
                        pt[:, pb * 1024:pb * 1024 + 128]),
                        reads=[f"ptr:{pb}"], writes=[f"gfm:{c}"])
        with (
            nc.psum_tensor(f"c_ps1{b}", [1, S], F32) as p_s1,
            nc.psum_tensor(f"c_ps2{b}", [1, S], F32) as p_s2,
        ):
            for c in range(NC_):
                xb = xs_[c % 2]
                sbf = ss_[c % 2]
                sb.op("sp", lambda e, c=c, xb=xb: e.dma_start(
                    out=xb[:], in_=t["x_fm"][b, c * 128:(c + 1) * 128, :]),
                    writes=[f"xs_:{c % 2}"], slot=f"ld{1 + c % 2}")
                sb.op("sp", lambda e, c=c, sbf=sbf: e.dma_start(
                    out=sbf[:], in_=t["s_d"][b, c * 128:(c + 1) * 128, :]),
                    reads=[f"s_d:{b}:{c}"], writes=[f"ss_:{c % 2}"],
                    slot=f"ld{3 + c % 2}")
                x1c = x1b[c % 2][:]
                sb.op("dve", lambda e, c=c, sbf=sbf: e.tensor_mul(
                    t12[:], sbf[:], gfm[:, c * S:(c + 1) * S]),
                    reads=[f"ss_:{c % 2}", f"gfm:{c}"], writes=["t12"])
                sb.op("dve", lambda e, c=c, xb=xb, x1c=x1c: e.tensor_add(
                    x1c, xb[:], t12[:]),
                    reads=[f"xs_:{c % 2}", "t12"], writes=[f"x1:{c % 2}"])
                sb.op("sp", lambda e, c=c, x1c=x1c: e.dma_start(
                    out=t["x1_d"][b, c * 128:(c + 1) * 128, :], in_=x1c),
                    reads=[f"x1:{c % 2}"], writes=[f"x1_d:{b}:{c}"], slot="st0")
                sb.op("pool", lambda e, x1c=x1c: e.tensor_copy(cb[:], x1c),
                      reads=[f"x1:{c % 2}"], writes=["cb"])
                sb.op("act", lambda e, x1c=x1c: e.activation(sq[:], x1c, AF.Square),
                      reads=[f"x1:{c % 2}"], writes=["sq"])
                for tc in range(4):
                    sl = slice(tc * 512, (tc + 1) * 512)
                    sb.op("pe", lambda e, sl=sl, c=c: e.matmul(
                        p_s1[:, sl], ones[:], cb[:, sl],
                        start=(c == 0), stop=(c == NC_ - 1)),
                        reads=["cb", "ones"], writes=[f"Cs1:{tc}"])
                    sb.op("pe", lambda e, sl=sl, c=c: e.matmul(
                        p_s2[:, sl], ones[:], sq[:, sl],
                        start=(c == 0), stop=(c == NC_ - 1)),
                        reads=["sq", "ones"], writes=[f"Cs2:{tc}"])
            emit_ln_rows(p, "C", p_s1, p_s2, rows, p.cst[0:1, 0:1])
        sb.barrier()
        with (
            nc.psum_tensor(f"c_pm{b}", [128, S], F32) as p_mu,
            nc.psum_tensor(f"c_pr{b}", [128, S], F32) as p_rs,
        ):
            for tc in range(4):
                sl = slice(tc * 512, (tc + 1) * 512)
                sb.op("pe", lambda e, sl=sl: e.matmul(
                    p_mu[:, sl], onesr[:], rows[0:1, 0:S][:, sl],
                    start=True, stop=True),
                    reads=["Cmu", "onesr"], writes=["CBmu"])
                sb.op("pe", lambda e, sl=sl: e.matmul(
                    p_rs[:, sl], onesr[:], rows[0:1, S:2 * S][:, sl],
                    start=True, stop=True),
                    reads=["Crs", "onesr"], writes=["CBrs"])
            for c in range(NC_):
                xb = x1b[c % 2]
                sb.op("sp", lambda e, c=c, xb=xb: e.dma_start(
                    out=xb[:], in_=t["x1_d"][b, c * 128:(c + 1) * 128, :]),
                    reads=[f"x1_d:{b}:{c}"], writes=[f"x1:{c % 2}"],
                    slot=f"ld{1 + c % 2}")
                sb.op("dve", lambda e, xb=xb: e.tensor_sub(
                    t12[:], xb[:], p_mu[:]),
                    reads=[f"x1:{c % 2}", "CBmu"], writes=["t12"])
                sb.op("dve", lambda e: e.tensor_mul(t12[:], t12[:], p_rs[:]),
                      reads=["t12", "CBrs"], writes=["t12"])
                sb.op("sp", lambda e, c=c: e.dma_start(
                    out=t["xn1_d"][b, c * 128:(c + 1) * 128, :], in_=t12[:]),
                    reads=["t12"], writes=[f"xn1_d:{b}:{c}"], slot="st1")
                sb.op("pool", lambda e, c=c: e.tensor_copy(cb[:], t12[:]),
                      reads=["t12"], writes=["cb"])
                sb.op("pool", lambda e, c=c: e.dma_start(
                    out=t["xn1bf_d"][b, c * 128:(c + 1) * 128, :], in_=cb[:]),
                    reads=["cb"], writes=[f"xn1bf_d:{b}:{c}"], slot="st2")
    sb.barrier()


def emit_s5(p, b):
    nc, sb, t = p.nc, p.sb, p.t
    sb.barrier()
    HP = 2052
    with ExitStack() as st:
        mixed = st.enter_context(nc.sbuf_tensor(f"r_mx{b}", [128, NC_ * S], BF16))
        biases = st.enter_context(nc.sbuf_tensor(f"r_bs{b}", [128, 64], F32))
        bi2 = st.enter_context(nc.sbuf_tensor(f"r_b2{b}", [128, 40], F32))
        xc = [st.enter_context(nc.sbuf_tensor(f"r_xc{b}{i}", [128, S], F32))
              for i in range(2)]
        t1 = st.enter_context(nc.sbuf_tensor(f"r_t1{b}", [128, 512], F32))
        t2 = st.enter_context(nc.sbuf_tensor(f"r_t2{b}", [128, 512], F32))
        cb = st.enter_context(nc.sbuf_tensor(f"r_cb{b}", [128, S], BF16))
        sq = st.enter_context(nc.sbuf_tensor(f"r_sq{b}", [128, S], BF16))
        ones = st.enter_context(nc.sbuf_tensor(f"r_on{b}", [128, 1], BF16))
        onesr = st.enter_context(nc.sbuf_tensor(f"r_onr{b}", [1, 128], F32))
        rows = st.enter_context(nc.sbuf_tensor(f"r_rw{b}", [1, 3 * S], F32))
        wl = [st.enter_context(nc.sbuf_tensor(f"r_wl{b}{i}", [128, NC_ * 128], BF16))
              for i in range(2)]
        sb.op("dve", lambda e: e.memset(ones[:], 1.0), writes=["ones"])
        sb.op("dve", lambda e: e.memset(onesr[:], 1.0), writes=["onesr"])
        sb.op("sp", lambda e: e.dma_start(out=biases[:, 0:8], in_=t["ln3g"]),
              writes=["bss"], slot="bia")
        sb.op("sp", lambda e: e.dma_start(out=biases[:, 8:16], in_=t["ln3b"]),
              writes=["bss"], slot="bia")
        sb.op("sp", lambda e: e.dma_start(out=biases[:, 16:40], in_=t["taps"]),
              writes=["bss"], slot="bia")
        sb.op("sp", lambda e: e.dma_start(out=biases[:, 40:48], in_=t["mixb"]),
              writes=["bss"], slot="bia")
        for nm, sl0 in (("bi", 0), ("bg", 8), ("bout", 16), ("bso", 24),
                        ("bsin", 32), ("bd", 34), ("bsel", 36)):
            n = 2 if nm in ("bsin", "bd", "bsel") else 8
            sb.op("sp", lambda e, nm=nm, sl0=sl0, n=n: e.dma_start(
                out=bi2[:, sl0:sl0 + n], in_=t[nm]),
                writes=["bss"], slot="bia")
        with (
            nc.psum_tensor(f"r_ps1{b}", [1, S], F32) as p_s1,
            nc.psum_tensor(f"r_ps2{b}", [1, S], F32) as p_s2,
        ):
            for c in range(NC_):
                xb = xc[c % 2]
                sb.op("sp", lambda e, c=c, xb=xb: e.dma_start(
                    out=xb[:], in_=t["xn1_d"][b, c * 128:(c + 1) * 128, :]),
                    reads=[f"xn1_d:{b}:{c}"], writes=[f"xc:{c % 2}"],
                    slot=f"ld{c % 2}"),
                sb.op("pool", lambda e, xb=xb: e.tensor_copy(cb[:], xb[:]),
                      reads=[f"xc:{c % 2}"], writes=["cb"])
                sb.op("act", lambda e, xb=xb: e.activation(sq[:], xb[:], AF.Square),
                      reads=[f"xc:{c % 2}"], writes=["sq"])
                for tc in range(4):
                    sl = slice(tc * 512, (tc + 1) * 512)
                    sb.op("pe", lambda e, sl=sl, c=c: e.matmul(
                        p_s1[:, sl], ones[:], cb[:, sl],
                        start=(c == 0), stop=(c == NC_ - 1)),
                        reads=["cb", "ones"], writes=[f"Rs1:{tc}"])
                    sb.op("pe", lambda e, sl=sl, c=c: e.matmul(
                        p_s2[:, sl], ones[:], sq[:, sl],
                        start=(c == 0), stop=(c == NC_ - 1)),
                        reads=["sq", "ones"], writes=[f"Rs2:{tc}"])
            emit_ln_rows(p, "R", p_s1, p_s2, rows, p.cst[0:1, 0:1])
        sb.barrier()
        with ExitStack() as st2:
            h_sb = st2.enter_context(
                nc.sbuf_tensor(f"r_h{b}", [128, NC_ * HP], BF16))
            acc = st2.enter_context(nc.sbuf_tensor(f"r_ac{b}", [128, S], F32))
            tf = st2.enter_context(nc.sbuf_tensor(f"r_tf{b}", [128, S], F32))
            sb.op("dve", lambda e: e.memset(h_sb[:], 0.0), writes=["hz"])
            with (
                nc.psum_tensor(f"r_pm{b}", [128, S], F32) as p_mu,
                nc.psum_tensor(f"r_pr2{b}", [128, S], F32) as p_rs,
            ):
                for tc in range(4):
                    sl = slice(tc * 512, (tc + 1) * 512)
                    sb.op("pe", lambda e, sl=sl: e.matmul(
                        p_mu[:, sl], onesr[:], rows[0:1, 0:S][:, sl],
                        start=True, stop=True),
                        reads=["Rmu", "onesr"], writes=["RBmu"])
                    sb.op("pe", lambda e, sl=sl: e.matmul(
                        p_rs[:, sl], onesr[:], rows[0:1, S:2 * S][:, sl],
                        start=True, stop=True),
                        reads=["Rrs", "onesr"], writes=["RBrs"])
                for c in range(NC_):
                    xb = xc[c % 2]
                    sb.op("sp", lambda e, c=c, xb=xb: e.dma_start(
                        out=xb[:], in_=t["xn1_d"][b, c * 128:(c + 1) * 128, :]),
                        reads=[f"xn1_d:{b}:{c}"], writes=[f"xc2:{c % 2}"],
                        slot=f"ld{2 + c % 2}")
                    sb.op("dve", lambda e, c=c, xb=xb: e.tensor_sub(
                        tf[:], xb[:], p_mu[:]),
                        reads=[f"xc2:{c % 2}", "RBmu"], writes=["rtf"])
                    sb.op("dve", lambda e: e.tensor_mul(tf[:], tf[:], p_rs[:]),
                          reads=["rtf", "RBrs"], writes=["rtf"])
                    sb.op("dve", lambda e, c=c: e.tensor_scalar(
                        h_sb[:, c * HP + 1:c * HP + 1 + S],
                        tf[:], biases[:, c:c + 1], biases[:, 8 + c:9 + c],
                        ALU.mult, ALU.add),
                        reads=["rtf", "bss", "hz"], writes=[f"h:{c}"])
            for c in range(NC_):
                rd = [f"h:{c}", "bss", "hz"]
                sb.op("dve", lambda e, c=c: e.tensor_scalar(
                    acc[:], h_sb[:, c * HP:c * HP + S],
                    biases[:, 16 + c * 3:17 + c * 3], biases[:, 40 + c:41 + c],
                    ALU.mult, ALU.add),
                    reads=rd, writes=["racc"])
                sb.op("dve", lambda e, c=c: e.scalar_tensor_tensor(
                    acc[:], h_sb[:, c * HP + 1:c * HP + 1 + S],
                    biases[:, 17 + c * 3:18 + c * 3], acc[:],
                    ALU.mult, ALU.add),
                    reads=rd + ["racc"], writes=["racc"])
                sb.op("dve", lambda e, c=c: e.scalar_tensor_tensor(
                    mixed[:, c * S:(c + 1) * S],
                    h_sb[:, c * HP + 2:c * HP + 2 + S],
                    biases[:, 18 + c * 3:19 + c * 3], acc[:],
                    ALU.mult, ALU.add),
                    reads=rd + ["racc"], writes=[f"mx:{c}"])
        sb.barrier()
        with ExitStack() as st3:
            selst = st3.enter_context(nc.sbuf_tensor(f"r_ss{b}", [128, 2 * S], BF16))
            st4 = st3.enter_context(ExitStack())
            dec = st4.enter_context(nc.sbuf_tensor(f"r_dc{b}", [128, 2 * S], F32))
            xst = st4.enter_context(nc.sbuf_tensor(f"r_xs{b}", [128, 2 * S], F32))
            stt = st4.enter_context(nc.sbuf_tensor(f"r_st{b}", [128, 2 * S], F32))
            d1 = st4.enter_context(nc.sbuf_tensor(f"r_d1{b}", [128, S], F32))
            db = st4.enter_context(nc.sbuf_tensor(f"r_db{b}", [128, S], F32))
            with ExitStack() as pst:
                pj = [pst.enter_context(nc.psum_tensor(f"r_pj{b}{i}", [128, 512], F32))
                      for i in range(8)]
                for wn, dst, fn, bcol in (("wd", dec, AF.Sigmoid, 34),
                                          ("wsin", xst, AF.Tanh, 32)):
                    for m in range(2):
                        wb = m % 2
                        sb.op("pool", lambda e, wn=wn, m=m, wb=wb: e.dma_start(
                            out=wl[wb][:].rearrange("p (c n) -> p c n", c=NC_),
                            in_=t[wn][m].rearrange("c p n -> p c n")),
                            writes=[f"wl{wb}"], slot=f"wqk{wb}")
                        for c in range(NC_):
                            for tc in range(4):
                                sb.op("pe", lambda e, c=c, tc=tc, wb=wb, m=m: e.matmul(
                                    pj[(m % 2) * 4 + tc][:],
                                    wl[wb][:, c * 128:(c + 1) * 128],
                                    mixed[:, c * S + tc * 512:c * S + (tc + 1) * 512],
                                    start=(c == 0), stop=(c == NC_ - 1)),
                                    reads=[f"wl{wb}", f"mx:{c}"],
                                    writes=[f"pj:{(m % 2) * 4 + tc}"],
                                    inc=(c == NC_ - 1))
                        for tc in range(4):
                            sb.op("act", lambda e, m=m, tc=tc, dst=dst, fn=fn, bcol=bcol: e.activation(
                                dst[:, m * S + tc * 512:m * S + (tc + 1) * 512],
                                pj[(m % 2) * 4 + tc][:], fn,
                                bias=bi2[:, bcol + m:bcol + m + 1]),
                                reads=[f"pj:{(m % 2) * 4 + tc}", "bss"],
                                writes=[f"{wn}o:{m}"])
                for m in range(2):
                    sb.op("dve", lambda e, m=m: e.tensor_scalar(
                        d1[:], dec[:, m * S:(m + 1) * S], -1.0, 1.0,
                        ALU.mult, ALU.add),
                        reads=[f"wdo:{m}"], writes=["d1"])
                    sb.op("dve", lambda e, m=m: e.tensor_mul(
                        db[:], d1[:], xst[:, m * S:(m + 1) * S]),
                        reads=[f"wsino:{m}", "d1"], writes=["db"])
                    sb.op("dve", lambda e, m=m: e.tensor_tensor_scan(
                        stt[:, m * S:(m + 1) * S], dec[:, m * S:(m + 1) * S],
                        db[:], 0.0, ALU.mult, ALU.add),
                        reads=[f"wdo:{m}", "db"], writes=[f"stt:{m}"])
                for m in range(2):
                    wb = m % 2
                    sb.op("pool", lambda e, m=m, wb=wb: e.dma_start(
                        out=wl[wb][:].rearrange("p (c n) -> p c n", c=NC_),
                        in_=t["wsel"][m].rearrange("c p n -> p c n")),
                        writes=[f"wl{wb}"], slot=f"wqk{wb}")
                    for c in range(NC_):
                        for tc in range(4):
                            sb.op("pe", lambda e, c=c, tc=tc, wb=wb, m=m: e.matmul(
                                pj[(m % 2) * 4 + tc][:],
                                wl[wb][:, c * 128:(c + 1) * 128],
                                mixed[:, c * S + tc * 512:c * S + (tc + 1) * 512],
                                start=(c == 0), stop=(c == NC_ - 1)),
                                reads=[f"wl{wb}", f"mx:{c}"],
                                writes=[f"pj:{(m % 2) * 4 + tc}"],
                                inc=(c == NC_ - 1))
                    for tc in range(4):
                        sb.op("act", lambda e, m=m, tc=tc: e.activation(
                            t1[:], pj[(m % 2) * 4 + tc][:], AF.Sigmoid,
                            bias=bi2[:, 36 + m:37 + m]),
                            reads=[f"pj:{(m % 2) * 4 + tc}", "bss"],
                            writes=["rt1"])
                        sb.op("dve", lambda e, m=m, tc=tc: e.tensor_mul(
                            selst[:, m * S + tc * 512:m * S + (tc + 1) * 512],
                            t1[:], stt[:, m * S + tc * 512:m * S + (tc + 1) * 512]),
                            reads=["rt1", f"stt:{m}"], writes=[f"sst:{m}"])
            sb.barrier()
            st4.close()
            cmb = st3.enter_context(nc.sbuf_tensor(f"r_cm{b}", [128, NC_ * S], BF16))
            yv = st3.enter_context(nc.sbuf_tensor(f"r_yv{b}", [128, 512], F32))
            uv = st3.enter_context(nc.sbuf_tensor(f"r_uv{b}", [128, 512], F32))
            gv = st3.enter_context(nc.sbuf_tensor(f"r_gv{b}", [128, 512], F32))
            wso_t = st3.enter_context(nc.sbuf_tensor(f"r_ws{b}", [128, 2 * 128], BF16))
            with ExitStack() as pst:
                py = pst.enter_context(nc.psum_tensor(f"r_py{b}", [128, 512], F32))
                pu = pst.enter_context(nc.psum_tensor(f"r_pu{b}", [128, 512], F32))
                pg = pst.enter_context(nc.psum_tensor(f"r_pg{b}", [128, 512], F32))
                for m in range(NC_):
                    sb.op("pool", lambda e, m=m: e.dma_start(
                        out=wso_t[:].rearrange("p (c n) -> p c n", c=2),
                        in_=t["wso"][m].rearrange("c p n -> p c n")),
                        writes=["wsot"], slot="wso")
                    sb.op("pool", lambda e, m=m: e.dma_start(
                        out=wl[0][:].rearrange("p (c n) -> p c n", c=NC_),
                        in_=t["wi"][m].rearrange("c p n -> p c n")),
                        writes=["wl0"], slot="wqk0")
                    sb.op("pool", lambda e, m=m: e.dma_start(
                        out=wl[1][:].rearrange("p (c n) -> p c n", c=NC_),
                        in_=t["wg"][m].rearrange("c p n -> p c n")),
                        writes=[f"wl1"], slot="wqk1")
                    for tc in range(4):
                        for c2 in range(2):
                            sb.op("pe", lambda e, c2=c2, tc=tc: e.matmul(
                                py[:], wso_t[:, c2 * 128:(c2 + 1) * 128],
                                selst[:, c2 * S + tc * 512:c2 * S + (tc + 1) * 512],
                                start=(c2 == 0), stop=(c2 == 1)),
                                reads=["wsot", f"sst:{c2}"], writes=["py"],
                                inc=(c2 == 1))
                        for c in range(NC_):
                            sb.op("pe", lambda e, c=c, tc=tc: e.matmul(
                                pu[:], wl[0][:, c * 128:(c + 1) * 128],
                                mixed[:, c * S + tc * 512:c * S + (tc + 1) * 512],
                                start=(c == 0), stop=(c == NC_ - 1)),
                                reads=["wl0", f"mx:{c}"], writes=["pu"],
                                inc=(c == NC_ - 1))
                        for c in range(NC_):
                            sb.op("pe", lambda e, c=c, tc=tc: e.matmul(
                                pg[:], wl[1][:, c * 128:(c + 1) * 128],
                                mixed[:, c * S + tc * 512:c * S + (tc + 1) * 512],
                                start=(c == 0), stop=(c == NC_ - 1)),
                                reads=["wl1", f"mx:{c}"], writes=["pg"],
                                inc=(c == NC_ - 1))
                        sb.op("act", lambda e, m=m: e.activation(
                            yv[:], py[:], AF.Identity, bias=bi2[:, 24 + m:25 + m]),
                            reads=["py", "bss"], writes=["yv"])
                        sb.op("act", lambda e, m=m: e.activation(
                            uv[:], pu[:], AF.Tanh, bias=bi2[:, 0 + m:1 + m]),
                            reads=["pu", "bss"], writes=["uv"])
                        sb.op("act", lambda e, m=m: e.activation(
                            gv[:], pg[:], AF.Sigmoid, bias=bi2[:, 8 + m:9 + m]),
                            reads=["pg", "bss"], writes=["gv"])
                        sb.op("dve", lambda e: e.tensor_sub(t1[:], yv[:], uv[:]),
                              reads=["yv", "uv"], writes=["rt1"])
                        sb.op("dve", lambda e: e.tensor_mul(t2[:], gv[:], t1[:]),
                              reads=["gv", "rt1"], writes=["rt2"])
                        sb.op("dve", lambda e, m=m, tc=tc: e.tensor_add(
                            cmb[:, m * S + tc * 512:m * S + (tc + 1) * 512],
                            t2[:], uv[:]),
                            reads=["rt2", "uv"], writes=[f"cmb:{m}"])
            sb.barrier()
            with ExitStack() as pst:
                pw = [pst.enter_context(nc.psum_tensor(f"r_pw{b}{i}", [128, 512], F32))
                      for i in range(8)]
                s1v = st3.enter_context(nc.sbuf_tensor(f"r_s1{b}", [128, S], F32))
                for m in range(NC_):
                    wb = m % 2
                    pbase = wb * 4
                    sb.op("pool", lambda e, m=m, wb=wb: e.dma_start(
                        out=wl[wb][:].rearrange("p (c n) -> p c n", c=NC_),
                        in_=t["wout"][m].rearrange("c p n -> p c n")),
                        writes=[f"wl{wb}"], slot=f"wqk{wb}")
                    xb = xc[m % 2]
                    sb.op("sp", lambda e, m=m, xb=xb: e.dma_start(
                        out=xb[:], in_=t["xn1_d"][b, m * 128:(m + 1) * 128, :]),
                        reads=[f"xn1_d:{b}:{m}"], writes=[f"xc3:{m % 2}"],
                        slot=f"ld{4 + m % 2}")
                    for tc in range(4):
                        for c in range(NC_):
                            sb.op("pe", lambda e, c=c, tc=tc, wb=wb, pbase=pbase: e.matmul(
                                pw[pbase + tc][:], wl[wb][:, c * 128:(c + 1) * 128],
                                cmb[:, c * S + tc * 512:c * S + (tc + 1) * 512],
                                start=(c == 0), stop=(c == NC_ - 1)),
                                reads=[f"wl{wb}", f"cmb:{c}"],
                                writes=[f"pw:{pbase + tc}"],
                                inc=(c == NC_ - 1))
                        sb.op("act", lambda e, m=m, tc=tc, pbase=pbase: e.activation(
                            t1[:], pw[pbase + tc][:], AF.Identity,
                            bias=bi2[:, 16 + m:17 + m]),
                            reads=[f"pw:{pbase + tc}", "bss"], writes=["rt1"])
                        sb.op("dve", lambda e, m=m, tc=tc, xb=xb: e.tensor_add(
                            s1v[:, tc * 512:(tc + 1) * 512], t1[:],
                            xb[:, tc * 512:(tc + 1) * 512]),
                            reads=["rt1", f"xc3:{m % 2}"], writes=["s1v"])
                    sb.op("sp", lambda e, m=m: e.dma_start(
                        out=t["s1_d"][b, m * 128:(m + 1) * 128, :], in_=s1v[:]),
                        reads=["s1v"], writes=[f"s1_d:{b}:{m}"], slot="sst")
    sb.barrier()


def emit_s6(p):
    nc, sb, t = p.nc, p.sb, p.t
    sb.barrier()
    NTT = 32
    with ExitStack() as st:
        rt = st.enter_context(nc.sbuf_tensor("g_rt", [128, 64], BF16))
        ind = st.enter_context(nc.sbuf_tensor("g_ind", [128, NTT * 8], F32))
        lg = st.enter_context(nc.sbuf_tensor("g_lg", [128, 8], F32))
        rr = st.enter_context(nc.sbuf_tensor("g_rr", [128, 2], F32))
        fc = st.enter_context(nc.sbuf_tensor("g_fc", [128, NC_ * 4096], BF16))
        pr_ = st.enter_context(nc.sbuf_tensor("g_pr", [128, 32 * 1024], BF16))
        hb = st.enter_context(nc.sbuf_tensor("g_hb", [128, 32 * 512], BF16))
        rt_t = st.enter_context(nc.sbuf_tensor("g_rtt", [128, 512], BF16))
        xt = [st.enter_context(nc.sbuf_tensor(f"g_xt{i}", [128, NC_ * 512], BF16))
              for i in range(2)]
        gtmp = [st.enter_context(nc.sbuf_tensor(f"g_gt{i}", [128, 512], F32))
                for i in range(4)]
        sb.op("pool", lambda e: e.dma_start(
            out=rt[:].rearrange("p (c n) -> p c n", c=NC_),
            in_=t["rt1"].rearrange("c p n -> p c n")),
            writes=["rt"], slot="bia2")
        with (
            nc.psum_tensor("g_pr0", [128, 512], F32) as pr0,
            nc.psum_tensor("g_pr1", [128, 512], F32) as pr1,
        ):
            for tt in range(NTT):
                b = tt // 16
                tloc = tt % 16
                hf = tt % 2
                psl = (pr0 if hf == 0 else pr1)[:, 0:8]
                xb = xt[hf]
                for c in range(NC_):
                    sb.op("sp", lambda e, b=b, c=c, tloc=tloc, xb=xb: e.dma_start(
                        out=xb[:, c * 512:c * 512 + 128],
                        in_=t["xn1bf_d"][b, c * 128:(c + 1) * 128,
                                         tloc * 128:(tloc + 1) * 128]),
                        reads=[f"xn1bf_d:{b}:{c}"], writes=[f"gxt:{hf}:{c}"],
                        slot=f"ld{hf}")
                allx = [f"gxt:{hf}:{i}" for i in range(NC_)]
                for c in range(NC_):
                    sb.op("pe", lambda e, c=c, psl=psl, xb=xb: e.matmul(
                        psl, xb[:, c * 512:c * 512 + 128],
                        rt[:, c * 8:(c + 1) * 8],
                        start=(c == 0), stop=(c == NC_ - 1)),
                        reads=allx + ["rt"], writes=[f"gpr:{hf}"],
                        inc=(c == NC_ - 1))
                sb.op("dve", lambda e, psl=psl: e.tensor_copy(lg[:], psl),
                      reads=[f"gpr:{hf}"], writes=["glg"])
                sb.op("dve", lambda e: e.reduce_max(rr[:, 0:1], lg[:], axis=AX.X),
                      reads=["glg"], writes=["grr"])
                sb.op("dve", lambda e, tt=tt: e.tensor_scalar(
                    ind[:, tt * 8:(tt + 1) * 8], lg[:], rr[:, 0:1], None,
                    ALU.is_ge),
                    reads=["glg", "grr"], writes=[f"ind:{tt}"])
        sb.barrier()
        with ExitStack() as pst:
            ph = [pst.enter_context(nc.psum_tensor(f"g_ph{i}", [128, 512], F32))
                  for i in range(2)]
            po = [pst.enter_context(nc.psum_tensor(f"g_po{i}", [128, 512], F32))
                  for i in range(4)]
            for j in range(8):
                sb.op("pool", lambda e, j=j: e.dma_start(
                    out=fc[:].rearrange("p (c n) -> p c n", c=NC_),
                    in_=t["fcs"][j].rearrange("c p n -> p c n")),
                    writes=["gfc"], slot="w1s")
                for dh in range(2):
                    sb.op("pool", lambda e, j=j, dh=dh: e.dma_start(
                        out=pr_[:, dh * 16384:(dh + 1) * 16384].rearrange(
                            "p (c n) -> p c n", c=32),
                        in_=t["prs"][j, dh].rearrange("c p n -> p c n")),
                        writes=[f"gprj:{dh}"], slot=f"w{2 + dh}s")
                for tb in range(8):
                    b = tb // 4
                    tcb = tb % 4
                    xb = xt[tb % 2]
                    for c in range(NC_):
                        sb.op("sp", lambda e, b=b, c=c, tcb=tcb, xb=xb, tb=tb: e.dma_start(
                            out=xb[:, c * 512:(c + 1) * 512],
                            in_=t["xn1bf_d"][b, c * 128:(c + 1) * 128,
                                             tcb * 512:(tcb + 1) * 512]),
                            reads=[f"xn1bf_d:{b}:{c}"],
                            writes=[f"gxt2:{tb % 2}:{c}"],
                            slot=f"ld{2 + tb % 2}")
                    allx2 = [f"gxt2:{tb % 2}:{i}" for i in range(NC_)]
                    for mh in range(32):
                        pp = ph[mh % 2]
                        for c in range(NC_):
                            sb.op("pe", lambda e, mh=mh, c=c, pp=pp, xb=xb, tb=tb: e.matmul(
                                pp[:], fc[:, c * 4096 + mh * 128:c * 4096 + (mh + 1) * 128],
                                xb[:, c * 512:(c + 1) * 512],
                                start=(c == 0), stop=(c == NC_ - 1)),
                                reads=["gfc"] + allx2,
                                writes=[f"gph:{mh % 2}"],
                                inc=(c == NC_ - 1))
                        sb.op("act", lambda e, mh=mh, pp=pp: e.activation(
                            rt_t[:], pp[:], AF.Relu),
                            reads=[f"gph:{mh % 2}"], writes=["grt"])
                        sb.op("dve", lambda e, mh=mh: e.tensor_mul(
                            hb[:, mh * 512:(mh + 1) * 512], rt_t[:], rt_t[:]),
                            reads=["grt"], writes=[f"ghb:{mh}"])
                    for dh in range(2):
                        for ti in range(4):
                            tt = tb * 4 + ti
                            pp = po[2 * dh + ti % 2]
                            for hc in range(32):
                                sb.op("pe", lambda e, hc=hc, ti=ti, dh=dh, pp=pp: e.matmul(
                                    pp[:],
                                    hb[:, hc * 512 + ti * 128:hc * 512 + (ti + 1) * 128],
                                    pr_[:, dh * 16384 + hc * 512:dh * 16384 + (hc + 1) * 512],
                                    start=(hc == 0), stop=(hc == 31)),
                                    reads=[f"ghb:{hc}", f"gprj:{dh}"],
                                    writes=[f"gpo:{2 * dh + ti % 2}"],
                                    inc=(hc == 31))
                            gb = gtmp[2 * dh + ti % 2]
                            sb.op("dve", lambda e, tt=tt, j=j, gb=gb, pp=pp: e.tensor_scalar(
                                gb[:], pp[:], ind[:, tt * 8 + j:tt * 8 + j + 1],
                                None, ALU.mult),
                                reads=[f"gpo:{2 * dh + ti % 2}", f"ind:{tt}"],
                                writes=[f"ggt:{2 * dh + ti % 2}"])
                            sb.op("pool", lambda e, tt=tt, dh=dh, j=j, gb=gb: e.dma_start(
                                out=t["g1moe_d"][tt * 128:(tt + 1) * 128,
                                                 dh * 512:(dh + 1) * 512],
                                in_=gb[:],
                                accum_op=(ALU.bypass if j == 0 else ALU.add)),
                                reads=[f"ggt:{2 * dh + ti % 2}"],
                                writes=[f"g1moe_d:{tt}:{dh}"],
                                slot=f"gm{tt}"),
    sb.barrier()


def emit_s7(p, b):
    nc, sb, t = p.nc, p.sb, p.t
    sb.barrier()
    with ExitStack() as st:
        gfm = st.enter_context(nc.sbuf_tensor(f"o_g{b}", [128, NC_ * S], F32))
        gt = st.enter_context(nc.sbuf_tensor(f"o_gt{b}", [128, 1024], F32))
        gs = st.enter_context(nc.sbuf_tensor(f"o_gs{b}", [128, 1024], BF16))
        idn = st.enter_context(nc.sbuf_tensor(f"o_idn{b}", [128, 128], BF16))
        xs_ = [st.enter_context(nc.sbuf_tensor(f"o_xs{b}{i}", [128, S], F32))
               for i in range(2)]
        ss_ = [st.enter_context(nc.sbuf_tensor(f"o_ss{b}{i}", [128, S], F32))
               for i in range(2)]
        ov = st.enter_context(nc.sbuf_tensor(f"o_ov{b}", [128, S], F32))
        t12 = st.enter_context(nc.sbuf_tensor(f"o_t12{b}", [128, S], F32))
        sb.op("sp", lambda e: e.dma_start(out=idn[:], in_=t["idn"]),
              writes=["idn"], slot="bia")
        with nc.psum_tensor(f"o_pt{b}", [128, 2048], BF16) as pt:
            for tloc in range(16):
                tt = b * 16 + tloc
                sb.op("sp", lambda e, tt=tt: e.dma_start(
                    out=gt[:], in_=t["g1moe_d"][tt * 128:(tt + 1) * 128, :]),
                    reads=[f"g1moe_d:{tt}:0", f"g1moe_d:{tt}:1"],
                    writes=["gt"], slot="ld0")
                sb.op("act", lambda e: e.activation(gs[:], gt[:], AF.Silu),
                      reads=["gt"], writes=["gs"])
                for c in range(NC_):
                    pb = c % 2
                    sb.op("pe", lambda e, c=c, pb=pb: e.transpose(
                        pt[:, pb * 1024:pb * 1024 + 128],
                        gs[:, c * 128:(c + 1) * 128], idn[:]),
                        reads=["gs", "idn"], writes=[f"ptr:{pb}"])
                    sb.op("dve", lambda e, c=c, tloc=tloc, pb=pb: e.tensor_copy(
                        gfm[:, c * S + tloc * 128:c * S + (tloc + 1) * 128],
                        pt[:, pb * 1024:pb * 1024 + 128]),
                        reads=[f"ptr:{pb}"], writes=[f"gfm:{c}"])
        for c in range(NC_):
            xb = xs_[c % 2]
            sbf = ss_[c % 2]
            sb.op("sp", lambda e, c=c, xb=xb: e.dma_start(
                out=xb[:], in_=t["x1_d"][b, c * 128:(c + 1) * 128, :]),
                reads=[f"x1_d:{b}:{c}"], writes=[f"oxs:{c % 2}"],
                slot=f"ld{1 + c % 2}")
            sb.op("sp", lambda e, c=c, sbf=sbf: e.dma_start(
                out=sbf[:], in_=t["s1_d"][b, c * 128:(c + 1) * 128, :]),
                reads=[f"s1_d:{b}:{c}"], writes=[f"oss:{c % 2}"],
                slot=f"ld{3 + c % 2}")
            sb.op("dve", lambda e, c=c, sbf=sbf: e.tensor_mul(
                t12[:], sbf[:], gfm[:, c * S:(c + 1) * S]),
                reads=[f"oss:{c % 2}", f"gfm:{c}"], writes=["t12"])
            sb.op("dve", lambda e, c=c, xb=xb: e.tensor_add(ov[:], xb[:], t12[:]),
                  reads=[f"oxs:{c % 2}", "t12"], writes=["ov"])
            sb.op("sp", lambda e, c=c: e.dma_start(
                out=t["yout"][b, c * 128:(c + 1) * 128, :], in_=ov[:]),
                reads=["ov"], writes=[f"yout:{b}:{c}"], slot="st0")
    sb.barrier()


def build_nc(stages=("s0",), dump=()):
    nc = bass.Bass("TRN2", target_bir_lowering=False)
    t = {}
    for name, shape, dt in _IN_SPECS:
        t[name] = nc.dram_tensor(name, shape, dt, kind="ExternalInput").ap()
    for name, shape, dt in _SCRATCH:
        kind = "ExternalOutput" if name in dump else "Internal"
        t[name] = nc.dram_tensor(name, shape, dt, kind=kind).ap()
    t["yout"] = nc.dram_tensor(
        "yout", [B_PER_CORE, 1024, 2048], F32, kind="ExternalOutput").ap()

    with ExitStack() as stack:
        sb = SB(nc, stack)
        p = P(nc, sb, t)
        p.cst = stack.enter_context(nc.sbuf_tensor("csts", [128, 8], F32))
        sb.op("dve", lambda e: e.memset(p.cst[:, 0:1], float(LN_EPS)), writes=["cst"])
        sb.op("dve", lambda e: e.memset(p.cst[:, 1:2], float(RMS_EPS)), writes=["cst"])
        p.dump = set(dump)
        if "s0" in stages:
            for b in range(B_PER_CORE):
                emit_s0(p, b)
        if "s12" in stages:
            for b in range(B_PER_CORE):
                emit_s12(p, b)
        if "s3" in stages:
            emit_s3(p)
        if "s4" in stages:
            for b in range(B_PER_CORE):
                emit_s4(p, b)
        if "s5" in stages:
            for b in range(B_PER_CORE):
                emit_s5(p, b)
        if "s6" in stages:
            emit_s6(p)
        if "s7" in stages:
            for b in range(B_PER_CORE):
                emit_s7(p, b)
        sb.finalize()
        import os
        if os.environ.get("MK_DEBUG_COUNT"):
            print("instr counts:", {e: len(v) for e, v in sb.prog.items()})

        def run_prog(lst):
            def body(e):
                for it in lst:
                    if isinstance(it, dict):
                        ins = it["fn"](e)
                        if it["inc"] is not None:
                            h, n = it["inc"]
                            ins.then_inc(h, n)
                    else:
                        it(e)
            return body

        with nc.Block() as blk:
            blk.sync(run_prog(sb.prog["sp"]))
            blk.gpsimd(run_prog(sb.prog["pool"]))
            blk.tensor(run_prog(sb.prog["pe"]))
            blk.vector(run_prog(sb.prog["dve"]))
            blk.scalar(run_prog(sb.prog["act"]))
    return nc


def kernel(**inputs):
    H = _prep(inputs)
    x_fm = H.pop("_x_fm")
    nc = build_nc(stages=("s0", "s12", "s3", "s4", "s5", "s6", "s7"))
    in_maps = []
    for c in range(8):
        m = dict(H)
        m["x_fm"] = np.ascontiguousarray(x_fm[2 * c:2 * c + 2])
        in_maps.append(m)
    res = run_bass_kernel_spmd(nc, in_maps, core_ids=list(range(8)))
    out = np.concatenate([r["yout"] for r in res.results], axis=0)
    return np.ascontiguousarray(out.transpose(0, 2, 1))


# version 02
# speedup vs baseline: 1.1664x; 1.1664x over previous
import numpy as np
import ml_dtypes
from contextlib import ExitStack

import concourse.bass as bass
import concourse.mybir as mybir
from concourse.bass_utils import run_bass_kernel_spmd
from concourse.bass import IndirectOffsetOnAxis as IoA

F32 = mybir.dt.float32
BF16 = mybir.dt.bfloat16
I32 = mybir.dt.int32
AF = mybir.ActivationFunctionType
ALU = mybir.AluOpType
AX = mybir.AxisListType
BFNP = ml_dtypes.bfloat16

B_PER_CORE = 2
S = 2048
D = 1024
NC_ = 8
NT = S // 512
LN_EPS = np.float32(1e-5)
RMS_EPS = np.float32(1.1920929e-07)


class SB:

    ENG = ("pe", "dve", "act", "pool", "sp")

    def __init__(self, nc, stack):
        self.nc = nc
        self.stack = stack
        self.prog = {e: [] for e in self.ENG}
        self.sem = {}
        self.cnt = {}
        for e in self.ENG:
            self.sem[e] = stack.enter_context(nc.semaphore(f"S{e}"))
            self.cnt[e] = 0
        self.slot_of = {}
        self.seen = {e: {} for e in self.ENG}
        self.writer = {}
        self.readers = {}
        self.pending = {e: [] for e in self.ENG}
        self.bar_snapshot = {}

    def _slot_sem(self, slot):
        if slot not in self.slot_of:
            name = f"d_{slot}"
            assert len(self.slot_of) < 96, "too many DMA slots"
            self.sem[name] = self.stack.enter_context(self.nc.semaphore(name))
            self.cnt[name] = 0
            self.slot_of[slot] = name
        return self.slot_of[slot]

    def _force_inc(self, eng):
        ent = None
        for it in reversed(self.prog[eng]):
            if isinstance(it, dict) and it["inc"] is None:
                ent = it
                break
        assert ent is not None, "no quiet op to inc"
        self.cnt[eng] += 1
        c = self.cnt[eng]
        ent["inc"] = (self.sem[eng], 1)
        for buf, kind in self.pending[eng]:
            if kind == "w" and self.writer.get(buf) == ("pend", eng):
                self.writer[buf] = ("done", eng, c)
            if kind == "r":
                rd = self.readers.setdefault(buf, {})
                if rd.get(("pend", eng)):
                    del rd[("pend", eng)]
                rd[eng] = c
        self.pending[eng] = []
        return c

    def _need(self, eng, deps, rec, consumer_is_dma):
        if rec[0] == "done":
            _, sem, c = rec
            deps[sem] = max(deps.get(sem, 0), c)
        else:
            peng = rec[1]
            if peng == eng and not consumer_is_dma:
                return
            c = self._force_inc(peng)
            deps[peng] = max(deps.get(peng, 0), c)

    def op(self, eng, fn, reads=(), writes=(), slot=None, inc=True):
        dma = slot is not None
        deps = {}
        for r in reads:
            w = self.writer.get(r)
            if w:
                self._need(eng, deps, w, dma)
        for wbuf in writes:
            w = self.writer.get(wbuf)
            if w:
                self._need(eng, deps, w, dma)
            for key, c in list(self.readers.get(wbuf, {}).items()):
                if key == ("pend",):
                    continue
                if isinstance(key, tuple):
                    peng = key[1]
                    if peng == eng and not dma:
                        continue
                    cc = self._force_inc(peng)
                    deps[peng] = max(deps.get(peng, 0), cc)
                else:
                    deps[key] = max(deps.get(key, 0), c)
        for sem, c in sorted(deps.items()):
            if self.seen[eng].get(sem, 0) >= c:
                continue
            h = self.sem[sem]
            self.prog[eng].append(lambda e, h=h, c=c: e.wait_ge(h, c))
            self.seen[eng][sem] = c
        if dma:
            semname = self._slot_sem(slot)
            self.cnt[semname] += 16
            c = self.cnt[semname]
            self.prog[eng].append({"fn": fn, "inc": (self.sem[semname], 16)})
            for wbuf in writes:
                self.writer[wbuf] = ("done", semname, c)
                self.readers[wbuf] = {}
            for r in reads:
                self.readers.setdefault(r, {})[semname] = c
        elif inc:
            self.cnt[eng] += 1
            c = self.cnt[eng]
            self.prog[eng].append({"fn": fn, "inc": (self.sem[eng], 1)})
            for buf, kind in self.pending[eng]:
                if kind == "w" and self.writer.get(buf) == ("pend", eng):
                    self.writer[buf] = ("done", eng, c)
                if kind == "r":
                    rd = self.readers.setdefault(buf, {})
                    if ("pend", eng) in rd:
                        del rd[("pend", eng)]
                    rd[eng] = c
            self.pending[eng] = []
            for wbuf in writes:
                self.writer[wbuf] = ("done", eng, c)
                self.readers[wbuf] = {}
            for r in reads:
                self.readers.setdefault(r, {})[eng] = c
        else:
            self.prog[eng].append({"fn": fn, "inc": None})
            for wbuf in writes:
                self.writer[wbuf] = ("pend", eng)
                self.readers[wbuf] = {}
                self.pending[eng].append((wbuf, "w"))
            for r in reads:
                self.readers.setdefault(r, {})[("pend", eng)] = True
                self.pending[eng].append((r, "r"))

    def barrier(self):
        for e in self.ENG:
            if self.pending[e]:
                self._force_inc(e)
        snap = dict(self.cnt)
        for e in self.ENG:
            for sem, c in snap.items():
                if c == 0 or sem == e:
                    continue
                if self.bar_snapshot.get((e, sem), -1) >= c:
                    continue
                if self.seen[e].get(sem, 0) >= c:
                    self.bar_snapshot[(e, sem)] = c
                    continue
                h = self.sem[sem]
                self.prog[e].append(lambda en, h=h, c=c: en.wait_ge(h, c))
                self.seen[e][sem] = c
                self.bar_snapshot[(e, sem)] = c

    def finalize(self):
        for e in self.ENG:
            if self.pending[e]:
                self._force_inc(e)
        self.barrier()


def _prep(ins):
    f = np.float32
    H = {}
    x = ins["L0_attn_x"]
    H["_x_fm"] = np.ascontiguousarray(x.transpose(0, 2, 1))

    perm = np.array([[[h * 192 + d * 3 + s for d in range(64)] for h in range(16)]
                     for s in range(3)]).reshape(-1)
    WT = np.ascontiguousarray(ins["L0_attn_W_qkv"][perm].T)
    bq = ins["L0_attn_b_qkv"][perm]
    H["wqk"] = np.ascontiguousarray(
        WT[:, :2048].reshape(8, 128, 16, 128).transpose(2, 0, 1, 3))
    H["wv"] = np.ascontiguousarray(WT[:, 2048:].reshape(8, 128, 1024))
    H["bqk"] = np.ascontiguousarray(bq[:2048].reshape(16, 128).T)
    H["bv"] = np.ascontiguousarray(bq[2048:].reshape(1, 1024))
    WoT = np.ascontiguousarray(ins["L0_attn_W_o"].T)
    H["wo"] = np.ascontiguousarray(
        WoT.reshape(8, 128, 8, 128).transpose(2, 0, 1, 3))
    H["bo"] = np.ascontiguousarray(ins["L0_attn_b_o"].reshape(8, 128).T)

    H["rt0"] = np.ascontiguousarray(ins["L0_ffn_router_w"].T.reshape(8, 128, 8))
    H["eb0"] = np.ascontiguousarray(ins["L0_ffn_expert_bias"].reshape(1, 8))
    w1 = np.concatenate([ins["L0_ffn_sw1"], ins["L0_ffn_w1"]], 0)
    w3 = np.concatenate([ins["L0_ffn_sw3"], ins["L0_ffn_w3"]], 0)
    w2 = np.concatenate([ins["L0_ffn_sw2"], ins["L0_ffn_w2"]], 0)
    H["w1s"] = np.ascontiguousarray(w1.transpose(0, 2, 1).reshape(10, 8, 128, 2048))
    H["w3s"] = np.ascontiguousarray(w3.transpose(0, 2, 1).reshape(10, 8, 128, 2048))
    H["w2s"] = np.ascontiguousarray(w2.transpose(0, 2, 1).reshape(10, 16, 128, 1024))

    H["ln3g"] = np.ascontiguousarray(ins["L1_attn_ln_g"].reshape(8, 128).T)
    H["ln3b"] = np.ascontiguousarray(ins["L1_attn_ln_b"].reshape(8, 128).T)
    H["taps"] = np.ascontiguousarray(
        ins["L1_attn_mix_w"][:, 0, :].reshape(8, 128, 3).transpose(1, 0, 2).reshape(128, 24))
    H["mixb"] = np.ascontiguousarray(ins["L1_attn_mix_b"].reshape(8, 128).T)

    def blk_io(W, mo, co):
        i, o = W.shape
        return np.ascontiguousarray(
            W.reshape(co, 128, mo, o // mo).transpose(2, 0, 1, 3))

    H["wi"] = blk_io(ins["L1_attn_Wi"], 8, 8)
    H["bi"] = np.ascontiguousarray(ins["L1_attn_bi"].reshape(8, 128).T)
    H["wg"] = blk_io(ins["L1_attn_Wg"], 8, 8)
    H["bg"] = np.ascontiguousarray(ins["L1_attn_bg"].reshape(8, 128).T)
    H["wout"] = blk_io(ins["L1_attn_Wout"], 8, 8)
    H["bout"] = np.ascontiguousarray(ins["L1_attn_bout"].reshape(8, 128).T)
    H["wsin"] = blk_io(ins["L1_attn_Wsin"], 2, 8)
    H["bsin"] = np.ascontiguousarray(ins["L1_attn_bsin"].reshape(2, 128).T)
    H["wd"] = blk_io(ins["L1_attn_Wd"], 2, 8)
    H["bd"] = np.ascontiguousarray(ins["L1_attn_bd"].reshape(2, 128).T)
    H["wsel"] = blk_io(ins["L1_attn_Wsel"], 2, 8)
    H["bsel"] = np.ascontiguousarray(ins["L1_attn_bsel"].reshape(2, 128).T)
    H["wso"] = blk_io(ins["L1_attn_Wso"], 8, 2)
    H["bso"] = np.ascontiguousarray(ins["L1_attn_bso"].reshape(8, 128).T)

    H["rt1"] = np.ascontiguousarray(ins["L1_ffn_router_w"].T.reshape(8, 128, 8))
    H["fcs"] = np.ascontiguousarray(
        ins["L1_ffn_expert_fc"].reshape(8, 8, 128, 4096))
    prT = ins["L1_ffn_expert_proj"].transpose(0, 2, 1)
    H["prs"] = np.ascontiguousarray(
        prT.reshape(8, 32, 128, 2, 512).transpose(0, 3, 1, 2, 4))

    H["idn"] = np.eye(128, dtype=BFNP)
    H["idnf"] = np.eye(128, dtype=np.float32)
    H["tids"] = np.arange(B_PER_CORE * S, dtype=np.int32).reshape(32, 128).T.copy()
    H["ecap1"] = np.broadcast_to(
        (np.arange(8, dtype=np.float32) * 1024.0)[None, :], (128, 8)).copy()
    masks = np.zeros((4, 128, 512), np.float32)
    for oi, o in enumerate((0, 128, 256, 384)):
        for kk in range(128):
            masks[oi, kk, o + kk:] = 1.0
    H["masks"] = masks.astype(BFNP)
    return H


_IN_SPECS = [
    ("wqk", [16, 8, 128, 128], F32), ("wv", [8, 128, 1024], F32),
    ("bqk", [128, 16], F32), ("bv", [1, 1024], F32),
    ("wo", [8, 8, 128, 128], F32), ("bo", [128, 8], F32),
    ("rt0", [8, 128, 8], F32), ("eb0", [1, 8], F32),
    ("w1s", [10, 8, 128, 2048], F32), ("w3s", [10, 8, 128, 2048], F32),
    ("w2s", [10, 16, 128, 1024], F32),
    ("ln3g", [128, 8], F32), ("ln3b", [128, 8], F32),
    ("taps", [128, 24], F32), ("mixb", [128, 8], F32),
    ("wi", [8, 8, 128, 128], F32), ("bi", [128, 8], F32),
    ("wg", [8, 8, 128, 128], F32), ("bg", [128, 8], F32),
    ("wout", [8, 8, 128, 128], F32), ("bout", [128, 8], F32),
    ("wsin", [2, 8, 128, 128], F32), ("bsin", [128, 2], F32),
    ("wd", [2, 8, 128, 128], F32), ("bd", [128, 2], F32),
    ("wsel", [2, 8, 128, 128], F32), ("bsel", [128, 2], F32),
    ("wso", [8, 2, 128, 128], F32), ("bso", [128, 8], F32),
    ("rt1", [8, 128, 8], F32),
    ("fcs", [8, 8, 128, 4096], F32), ("prs", [8, 2, 32, 128, 512], F32),
    ("idn", [128, 128], BF16), ("idnf", [128, 128], F32),
    ("tids", [128, 32], I32), ("ecap1", [128, 8], F32),
    ("masks", [4, 128, 512], BF16),
    ("x_fm", [B_PER_CORE, 1024, 2048], F32),
]

_SCRATCH = [
    ("qdbg", [B_PER_CORE, 1024, 2048], BF16),
    ("kdbg", [B_PER_CORE, 1024, 2048], BF16),
    ("vdbg", [B_PER_CORE, 2048, 1024], BF16),
    ("zdbg", [B_PER_CORE, 1024, 2048], BF16),
    ("xnbf_d", [B_PER_CORE, 1024, 2048], BF16),
    ("rnbf_d", [B_PER_CORE, 1024, 2048], BF16),
    ("s_d", [B_PER_CORE, 1024, 2048], F32),
    ("gmoe_d", [B_PER_CORE * S, 1024], F32),
    ("x1_d", [B_PER_CORE, 1024, 2048], F32),
    ("xn1_d", [B_PER_CORE, 1024, 2048], F32),
    ("xn1bf_d", [B_PER_CORE, 1024, 2048], BF16),
    ("s1_d", [B_PER_CORE, 1024, 2048], F32),
    ("g1a_d", [B_PER_CORE * S + 1, 512], F32),
    ("g1b_d", [B_PER_CORE * S + 1, 512], F32),
    ("xn1tm_d", [B_PER_CORE * S + 1, 1024], BF16),
    ("idx1_d", [8 * 1024 + 128, 1], I32),
]


class P:

    def __init__(self, nc, sb, t):
        self.nc = nc
        self.sb = sb
        self.t = t
        self.cst = None
        self.dump = set()


def regs(name, *idx_ranges):
    out = [name]
    for r in idx_ranges:
        out = [f"{o}:{i}" for o in out for i in r]
    return out


def emit_ln_rows(p, pfx, p_s1, p_s2, rows, eps_ap):
    sb = p.sb
    sb.op("act", lambda e: e.activation(
        rows[:, 0:S], p_s1[:], AF.Identity, scale=1.0 / D),
        reads=[f"{pfx}s1:{i}" for i in range(4)], writes=[f"{pfx}mu"])
    sb.op("act", lambda e: e.activation(
        rows[:, 2 * S:3 * S], p_s2[:], AF.Identity, scale=1.0 / D),
        reads=[f"{pfx}s2:{i}" for i in range(4)], writes=[f"{pfx}m2"])
    sb.op("dve", lambda e: e.tensor_mul(
        rows[:, S:2 * S], rows[:, 0:S], rows[:, 0:S]),
        reads=[f"{pfx}mu"], writes=[f"{pfx}rs"])
    sb.op("dve", lambda e: e.tensor_sub(
        rows[:, 2 * S:3 * S], rows[:, 2 * S:3 * S], rows[:, S:2 * S]),
        reads=[f"{pfx}m2", f"{pfx}rs"], writes=[f"{pfx}m2"])
    sb.op("act", lambda e: e.activation(
        rows[:, S:2 * S], rows[:, 2 * S:3 * S], AF.Sqrt, bias=eps_ap),
        reads=[f"{pfx}m2", "cst"], writes=[f"{pfx}rs"])
    sb.op("dve", lambda e: e.reciprocal(rows[:, S:2 * S], rows[:, S:2 * S]),
          reads=[f"{pfx}rs"], writes=[f"{pfx}rs"])


def emit_s0(p, b):
    nc, sb, t = p.nc, p.sb, p.t
    with ExitStack() as st:
        x_sb = st.enter_context(nc.sbuf_tensor(f"s0x{b}", [128, NC_ * S], F32))
        xn_sb = st.enter_context(nc.sbuf_tensor(f"s0xn{b}", [128, NC_ * S], F32))
        cb = st.enter_context(nc.sbuf_tensor(f"s0cb{b}", [128, S], BF16))
        sq = st.enter_context(nc.sbuf_tensor(f"s0sq{b}", [128, S], BF16))
        ones = st.enter_context(nc.sbuf_tensor(f"s0on{b}", [128, 1], BF16))
        onesr = st.enter_context(nc.sbuf_tensor(f"s0onr{b}", [1, 128], F32))
        rows = st.enter_context(nc.sbuf_tensor(f"s0rw{b}", [1, 4 * S], F32))
        sb.op("dve", lambda e: e.memset(ones[:], 1.0), writes=["ones"])
        sb.op("dve", lambda e: e.memset(onesr[:], 1.0), writes=["onesr"])
        for c in range(NC_):
            sb.op("sp", lambda e, c=c: e.dma_start(
                out=x_sb[:, c * S:(c + 1) * S],
                in_=t["x_fm"][b, c * 128:(c + 1) * 128, :]),
                writes=[f"x_sb:{c}"], slot=f"ld{c}")
        with (
            nc.psum_tensor(f"p0a{b}", [1, S], F32) as p_s1,
            nc.psum_tensor(f"p0b{b}", [1, S], F32) as p_s2,
        ):
            for c in range(NC_):
                xs = x_sb[:, c * S:(c + 1) * S]
                sb.op("pool", lambda e, xs=xs: e.tensor_copy(cb[:], xs),
                      reads=[f"x_sb:{c}"], writes=["cb"])
                sb.op("act", lambda e, xs=xs: e.activation(sq[:], xs, AF.Square),
                      reads=[f"x_sb:{c}"], writes=["sq"])
                for tc in range(4):
                    sl = slice(tc * 512, (tc + 1) * 512)
                    sb.op("pe", lambda e, sl=sl, c=c: e.matmul(
                        p_s1[:, sl], ones[:], cb[:, sl],
                        start=(c == 0), stop=(c == NC_ - 1)),
                        reads=["cb", "ones"], writes=[f"As1:{tc}"])
                    sb.op("pe", lambda e, sl=sl, c=c: e.matmul(
                        p_s2[:, sl], ones[:], sq[:, sl],
                        start=(c == 0), stop=(c == NC_ - 1)),
                        reads=["sq", "ones"], writes=[f"As2:{tc}"])
            emit_ln_rows(p, "A", p_s1, p_s2, rows, p.cst[0:1, 0:1])
        with nc.psum_tensor(f"p0c{b}", [128, 1024], F32) as p_b:
            for tc in range(4):
                sl = slice(tc * 512, (tc + 1) * 512)
                sb.op("pe", lambda e, sl=sl: e.matmul(
                    p_b[:, 0:512], onesr[:], rows[0:1, 0:S][:, sl],
                    start=True, stop=True),
                    reads=["Amu", "onesr"],
                    writes=["Bmu"] + ([f"As1:{i}" for i in range(4)]
                                      + [f"As2:{i}" for i in range(4)] if tc == 0 else []))
                sb.op("pe", lambda e, sl=sl: e.matmul(
                    p_b[:, 512:1024], onesr[:], rows[0:1, S:2 * S][:, sl],
                    start=True, stop=True),
                    reads=["Ars", "onesr"], writes=["Brs"])
                for c in range(NC_):
                    xs = x_sb[:, c * S + tc * 512:c * S + (tc + 1) * 512]
                    xn = xn_sb[:, c * S + tc * 512:c * S + (tc + 1) * 512]
                    sb.op("dve", lambda e, xs=xs, xn=xn: e.tensor_sub(
                        xn, xs, p_b[:, 0:512]),
                        reads=[f"x_sb:{c}", "Bmu"], writes=[f"xn:{c}:{tc}"])
                    sb.op("dve", lambda e, xn=xn: e.tensor_mul(xn, xn, p_b[:, 512:1024]),
                        reads=[f"xn:{c}:{tc}", "Brs"], writes=[f"xn:{c}:{tc}"])
        with nc.psum_tensor(f"p0d{b}", [1, S], F32) as p_rms:
            for c in range(NC_):
                xn = xn_sb[:, c * S:(c + 1) * S]
                rd = [f"xn:{c}:{tc}" for tc in range(4)]
                sb.op("pool", lambda e, xn=xn: e.tensor_copy(cb[:], xn),
                      reads=rd, writes=["cb"])
                sb.op("pool", lambda e, c=c: e.dma_start(
                    out=t["xnbf_d"][b, c * 128:(c + 1) * 128, :], in_=cb[:]),
                    reads=["cb"], writes=[f"xnbf_d:{b}:{c}"], slot="st0")
                sb.op("act", lambda e, xn=xn: e.activation(sq[:], xn, AF.Square),
                      reads=rd + (["Bmu", "Brs"] if c == 0 else []), writes=["sq"])
                for tc in range(4):
                    sl = slice(tc * 512, (tc + 1) * 512)
                    sb.op("pe", lambda e, sl=sl, c=c: e.matmul(
                        p_rms[:, sl], ones[:], sq[:, sl],
                        start=(c == 0), stop=(c == NC_ - 1)),
                        reads=["sq", "ones"],
                        writes=[f"Rs:{tc}"] + (["Bmu", "Brs"] if c == 0 and tc == 0 else []))
            sb.op("act", lambda e: e.activation(
                rows[:, 3 * S:4 * S], p_rms[:], AF.Sqrt, scale=1.0 / D,
                bias=p.cst[0:1, 1:2]),
                reads=[f"Rs:{i}" for i in range(4)] + ["cst"], writes=["rmsr"])
            sb.op("dve", lambda e: e.reciprocal(
                rows[:, 3 * S:4 * S], rows[:, 3 * S:4 * S]),
                reads=["rmsr"], writes=["rmsr"])
        with nc.psum_tensor(f"p0e{b}", [128, S], F32) as p_c:
            for tc in range(4):
                sl = slice(tc * 512, (tc + 1) * 512)
                sb.op("pe", lambda e, sl=sl: e.matmul(
                    p_c[:, sl], onesr[:], rows[0:1, 3 * S:4 * S][:, sl],
                    start=True, stop=True),
                    reads=["rmsr", "onesr"],
                    writes=["Crm"] + ([f"Rs:{i}" for i in range(4)] if tc == 0 else []))
            for c in range(NC_):
                xn = xn_sb[:, c * S:(c + 1) * S]
                rd = [f"xn:{c}:{tc}" for tc in range(4)]
                sb.op("dve", lambda e, xn=xn: e.tensor_mul(cb[:], xn, p_c[:]),
                      reads=rd + ["Crm"], writes=["cb"])
                sb.op("sp", lambda e, c=c: e.dma_start(
                    out=t["rnbf_d"][b, c * 128:(c + 1) * 128, :], in_=cb[:]),
                    reads=["cb"], writes=[f"rnbf_d:{b}:{c}"], slot="st1")
    p.sb.barrier()


def emit_s12(p, b):
    nc, sb, t = p.nc, p.sb, p.t
    sb.barrier()
    with ExitStack() as st:
        rn = st.enter_context(nc.sbuf_tensor(f"a_rn{b}", [128, NC_ * S], BF16))
        q_sb = st.enter_context(nc.sbuf_tensor(f"a_q{b}", [128, NC_ * S], BF16))
        k_sb = st.enter_context(nc.sbuf_tensor(f"a_k{b}", [128, NC_ * S], BF16))
        v_sb = st.enter_context(nc.sbuf_tensor(f"a_v{b}", [128, 16 * 1024], BF16))
        z_sb = st.enter_context(nc.sbuf_tensor(f"a_z{b}", [128, NC_ * S], BF16))
        wt = [st.enter_context(nc.sbuf_tensor(f"a_w{b}{i}", [128, NC_ * 128], BF16))
              for i in range(2)]
        wv_sb = st.enter_context(nc.sbuf_tensor(f"a_wv{b}", [128, NC_ * 1024], BF16))
        bqk = st.enter_context(nc.sbuf_tensor(f"a_bqk{b}", [128, 16], F32))
        bo_sb = st.enter_context(nc.sbuf_tensor(f"a_bo{b}", [128, 8], F32))
        bv_sb = st.enter_context(nc.sbuf_tensor(f"a_bv{b}", [1, 1024], BF16))
        onesb = st.enter_context(nc.sbuf_tensor(f"a_on{b}", [1, 128], BF16))
        msk = st.enter_context(nc.sbuf_tensor(f"a_msk{b}", [128, 4 * 512], BF16))
        e_bf = st.enter_context(nc.sbuf_tensor(f"a_e{b}", [128, S], BF16))
        etmp = st.enter_context(nc.sbuf_tensor(f"a_et{b}", [128, 512], BF16))
        vv = st.enter_context(nc.sbuf_tensor(f"a_vv{b}", [128, 64], BF16))
        zr = st.enter_context(nc.sbuf_tensor(f"a_zr{b}", [128, 16], F32))
        sv = st.enter_context(nc.sbuf_tensor(f"a_sv{b}", [128, S], F32))
        sb.op("dve", lambda e: e.memset(onesb[:], 1.0), writes=["onesb"])
        for c in range(NC_):
            sb.op("sp", lambda e, c=c: e.dma_start(
                out=rn[:, c * S:(c + 1) * S],
                in_=t["rnbf_d"][b, c * 128:(c + 1) * 128, :]),
                reads=[f"rnbf_d:{b}:{c}"], writes=[f"rn:{c}"], slot=f"ld{c}")
        sb.op("sp", lambda e: e.dma_start(out=bqk[:], in_=t["bqk"]),
              writes=["bqk"], slot="bia")
        sb.op("sp", lambda e: e.dma_start(out=bo_sb[:], in_=t["bo"]),
              writes=["bo"], slot="bia")
        sb.op("pool", lambda e: e.dma_start(out=bv_sb[:], in_=t["bv"]),
              writes=["bv"], slot="bia2")
        sb.op("sp", lambda e: e.dma_start(
            out=msk[:].rearrange("p (o n) -> p o n", o=4),
            in_=t["masks"].rearrange("o p n -> p o n")),
            writes=["msk"], slot="msk")
        sb.op("pool", lambda e: e.dma_start(
            out=wv_sb[:].rearrange("p (c n) -> p c n", c=NC_),
            in_=t["wv"].rearrange("c p n -> p c n")),
            writes=["wv_sb"], slot="wv")
        with ExitStack() as pst:
            pqk = [pst.enter_context(nc.psum_tensor(f"a_pq{b}{i}", [128, 512], F32))
                   for i in range(8)]
            for m in range(16):
                wb = m % 2
                sb.op("pool", lambda e, m=m, wb=wb: e.dma_start(
                    out=wt[wb][:].rearrange("p (c n) -> p c n", c=NC_),
                    in_=t["wqk"][m].rearrange("c p n -> p c n")),
                    writes=[f"wt{wb}"], slot=f"wqk{wb}")
                pbase = (m % 2) * 4
                for c in range(NC_):
                    for tc in range(4):
                        sb.op("pe", lambda e, m=m, c=c, tc=tc, wb=wb, pbase=pbase: e.matmul(
                            pqk[pbase + tc][:],
                            wt[wb][:, c * 128:(c + 1) * 128],
                            rn[:, c * S + tc * 512:c * S + (tc + 1) * 512],
                            start=(c == 0), stop=(c == NC_ - 1)),
                            reads=[f"wt{wb}", f"rn:{c}"],
                            writes=[f"pqk:{pbase + tc}"],
                            inc=(c == NC_ - 1))
                dst = q_sb if m < 8 else k_sb
                mm = m % 8
                for tc in range(4):
                    sb.op("act", lambda e, m=m, mm=mm, tc=tc, dst=dst, pbase=pbase: e.activation(
                        dst[:, mm * S + tc * 512:mm * S + (tc + 1) * 512],
                        pqk[pbase + tc][:], AF.Identity, bias=bqk[:, m:m + 1]),
                        reads=[f"pqk:{pbase + tc}", "bqk"],
                        writes=[f"qk:{m}:{tc}"])
        sb.barrier()
        with ExitStack() as pst:
            pv = [pst.enter_context(nc.psum_tensor(f"a_pv{b}{i}", [128, 1024], F32))
                  for i in range(2)]
            for tt in range(16):
                pb = tt % 2
                for c in range(NC_):
                    for half in range(2):
                        sb.op("pe", lambda e, tt=tt, c=c, half=half, pb=pb: e.matmul(
                            pv[pb][:, half * 512:(half + 1) * 512],
                            rn[:, c * S + tt * 128:c * S + (tt + 1) * 128],
                            wv_sb[:, c * 1024 + half * 512:c * 1024 + (half + 1) * 512],
                            start=(c == 0), stop=False),
                            reads=["wv_sb", f"rn:{c}"], writes=[f"pv:{pb}"],
                            inc=False)
                for half in range(2):
                    sb.op("pe", lambda e, half=half, pb=pb: e.matmul(
                        pv[pb][:, half * 512:(half + 1) * 512],
                        onesb[:], bv_sb[:, half * 512:(half + 1) * 512],
                        start=False, stop=True),
                        reads=["onesb", "bv"], writes=[f"pv:{pb}"],
                        inc=(half == 1))
                sb.op("dve", lambda e, tt=tt, pb=pb: e.tensor_copy(
                    v_sb[:, tt * 1024:(tt + 1) * 1024], pv[pb][:]),
                    reads=[f"pv:{pb}"], writes=[f"v:{tt}"])
        if "qdbg" in p.dump:
            sb.barrier()
            for c in range(NC_):
                for nm, buf in (("qdbg", q_sb), ("kdbg", k_sb)):
                    sb.op("sp", lambda e, c=c, nm=nm, buf=buf: e.dma_start(
                        out=t[nm][b, c * 128:(c + 1) * 128, :],
                        in_=buf[:, c * S:(c + 1) * S]),
                        reads=[], writes=[], slot="dbg")
            for tt in range(16):
                sb.op("sp", lambda e, tt=tt: e.dma_start(
                    out=t["vdbg"][b, tt * 128:(tt + 1) * 128, :],
                    in_=v_sb[:, tt * 1024:(tt + 1) * 1024]),
                    reads=[], writes=[], slot="dbg")
            sb.barrier()
        sb.barrier()
        with (
            nc.psum_tensor(f"a_ps{b}", [128, S], F32) as psc,
            nc.psum_tensor(f"a_pz{b}", [64, S], F32) as pz,
        ):
            for h in range(16):
                ch = h // 2
                r0 = (h % 2) * 64
                qh = q_sb[r0:r0 + 64, ch * S:(ch + 1) * S]
                kh = k_sb[r0:r0 + 64, ch * S:(ch + 1) * S]
                for kt in range(16):
                    c0 = kt // 4
                    for cc in range(c0, 4):
                        sb.op("pe", lambda e, kt=kt, cc=cc, qh=qh, kh=kh: e.matmul(
                            psc[:, cc * 512:(cc + 1) * 512],
                            kh[:, kt * 128:(kt + 1) * 128],
                            qh[:, cc * 512:(cc + 1) * 512],
                            start=True, stop=True),
                            reads=[f"qk:{ch}:{cc}", f"qk:{8 + ch}:{kt // 4}"],
                            writes=[f"psc:{cc}"])
                        sb.op("dve", lambda e, cc=cc: e.reduce_max(
                            zr[:, 8 + cc:9 + cc], psc[:, cc * 512:(cc + 1) * 512],
                            axis=AX.X, negate=True),
                            reads=[f"psc:{cc}"], writes=[f"nmx:{cc}"])
                    for cc in range(c0 + 1, 4):
                        src0 = 8 + c0 if cc == c0 + 1 else 15
                        sb.op("dve", lambda e, cc=cc, src0=src0: e.tensor_tensor(
                            zr[:, 15:16], zr[:, src0:src0 + 1], zr[:, 8 + cc:9 + cc],
                            ALU.min),
                            reads=[f"nmx:{cc}", f"nmx:{c0}" if cc == c0 + 1 else "nmf"],
                            writes=["nmf"])
                    if c0 == 3:
                        sb.op("dve", lambda e: e.tensor_copy(
                            zr[:, 15:16], zr[:, 11:12]),
                            reads=["nmx:3"], writes=["nmf"])
                    o = kt % 4
                    sb.op("act", lambda e, c0=c0: e.activation(
                        etmp[:], psc[:, c0 * 512:(c0 + 1) * 512], AF.Exp,
                        bias=zr[:, 15:16]),
                        reads=[f"psc:{c0}", "nmf"], writes=["etmp"])
                    sb.op("dve", lambda e, c0=c0, o=o: e.tensor_mul(
                        e_bf[:, c0 * 512:(c0 + 1) * 512], etmp[:],
                        msk[:, o * 512:(o + 1) * 512]),
                        reads=["etmp", "msk"], writes=[f"ee:{c0}"])
                    sb.op("dve", lambda e, c0=c0: e.reduce_sum(
                        zr[:, c0:c0 + 1], e_bf[:, c0 * 512:(c0 + 1) * 512], axis=AX.X),
                        reads=[f"ee:{c0}"], writes=[f"zs:{c0}"])
                    for cc in range(c0 + 1, 4):
                        sb.op("act", lambda e, cc=cc: e.activation(
                            e_bf[:, cc * 512:(cc + 1) * 512],
                            psc[:, cc * 512:(cc + 1) * 512], AF.Exp,
                            bias=zr[:, 15:16], accum_out=zr[:, cc:cc + 1]),
                            reads=[f"psc:{cc}", "nmf"],
                            writes=[f"ee:{cc}", f"zs:{cc}"])
                    for cc in range(c0 + 1, 4):
                        src0 = c0 if cc == c0 + 1 else 4
                        sb.op("dve", lambda e, cc=cc, src0=src0: e.tensor_add(
                            zr[:, 4:5], zr[:, src0:src0 + 1], zr[:, cc:cc + 1]),
                            reads=[f"zs:{cc}", f"zs:{c0}" if cc == c0 + 1 else "zt"],
                            writes=["zt"])
                    if c0 == 3:
                        sb.op("dve", lambda e: e.tensor_copy(zr[:, 4:5], zr[:, 3:4]),
                              reads=["zs:3"], writes=["zt"])
                    sb.op("dve", lambda e: e.reciprocal(zr[:, 5:6], zr[:, 4:5]),
                          reads=["zt"], writes=["zrec"])
                    sb.op("dve", lambda e, h=h, kt=kt: e.tensor_scalar(
                        vv[:], v_sb[:, kt * 1024 + h * 64:kt * 1024 + h * 64 + 64],
                        zr[:, 5:6], None, ALU.mult),
                        reads=[f"v:{kt}", "zrec"], writes=["vv"])
                    for cc in range(c0, 4):
                        sb.op("pe", lambda e, cc=cc, kt=kt: e.matmul(
                            pz[:, cc * 512:(cc + 1) * 512], vv[:],
                            e_bf[:, cc * 512:(cc + 1) * 512],
                            start=(kt == 0), stop=(kt == 4 * cc + 3)),
                            reads=["vv", f"ee:{cc}"], writes=[f"pz:{cc}"])
                sb.op("dve", lambda e, r0=r0, ch=ch: e.tensor_copy(
                    z_sb[r0:r0 + 64, ch * S:(ch + 1) * S], pz[:]),
                    reads=[f"pz:{cc}" for cc in range(4)], writes=[f"z:{ch}"])
        if "zdbg" in p.dump:
            sb.barrier()
            for c in range(NC_):
                sb.op("sp", lambda e, c=c: e.dma_start(
                    out=t["zdbg"][b, c * 128:(c + 1) * 128, :],
                    in_=z_sb[:, c * S:(c + 1) * S]),
                    reads=[], writes=[], slot="dbg")
            sb.barrier()
        sb.barrier()
        with ExitStack() as pst:
            po = [pst.enter_context(nc.psum_tensor(f"a_po{b}{i}", [128, 512], F32))
                  for i in range(8)]
            for m in range(8):
                wb = m % 2
                pbase = wb * 4
                sb.op("pool", lambda e, m=m, wb=wb: e.dma_start(
                    out=wt[wb][:].rearrange("p (c n) -> p c n", c=NC_),
                    in_=t["wo"][m].rearrange("c p n -> p c n")),
                    writes=[f"wt{wb}"], slot=f"wqk{wb}")
                for c in range(NC_):
                    for tc in range(4):
                        sb.op("pe", lambda e, c=c, tc=tc, wb=wb, pbase=pbase: e.matmul(
                            po[pbase + tc][:],
                            wt[wb][:, c * 128:(c + 1) * 128],
                            z_sb[:, c * S + tc * 512:c * S + (tc + 1) * 512],
                            start=(c == 0), stop=(c == NC_ - 1)),
                            reads=[f"wt{wb}", f"z:{c}"],
                            writes=[f"po:{pbase + tc}"],
                            inc=(c == NC_ - 1))
                for tc in range(4):
                    sb.op("act", lambda e, tc=tc, m=m, pbase=pbase: e.activation(
                        sv[:, tc * 512:(tc + 1) * 512], po[pbase + tc][:],
                        AF.Identity, bias=bo_sb[:, m:m + 1]),
                        reads=[f"po:{pbase + tc}", "bo"], writes=["sv"])
                sb.op("sp", lambda e, m=m: e.dma_start(
                    out=t["s_d"][b, m * 128:(m + 1) * 128, :], in_=sv[:]),
                    reads=["sv"], writes=[f"s_d:{b}:{m}"], slot="sst")
    sb.barrier()


def emit_s3(p):
    nc, sb, t = p.nc, p.sb, p.t
    sb.barrier()
    NTT = 32
    with ExitStack() as st:
        xn = st.enter_context(nc.sbuf_tensor("f_xn", [128, NC_ * 4096], BF16))
        rt = st.enter_context(nc.sbuf_tensor("f_rt", [128, 64], BF16))
        eb = st.enter_context(nc.sbuf_tensor("f_eb", [1, 8], BF16))
        onesb = st.enter_context(nc.sbuf_tensor("f_on", [1, 128], BF16))
        cw = st.enter_context(nc.sbuf_tensor("f_cw", [128, NTT * 10], F32))
        lg = st.enter_context(nc.sbuf_tensor("f_lg", [128, 8], F32))
        rga = st.enter_context(nc.sbuf_tensor("f_rga", [128, 8], F32))
        rgb = st.enter_context(nc.sbuf_tensor("f_rgb", [128, 8], F32))
        rr = st.enter_context(nc.sbuf_tensor("f_rr", [128, 8], F32))
        w1t = st.enter_context(nc.sbuf_tensor("f_w1", [128, NC_ * 2048], BF16))
        w3t = st.enter_context(nc.sbuf_tensor("f_w3", [128, NC_ * 2048], BF16))
        w2t = st.enter_context(nc.sbuf_tensor("f_w2", [128, 16 * 1024], BF16))
        hh = st.enter_context(nc.sbuf_tensor("f_hh", [128, 16 * 512], BF16))
        s1t = st.enter_context(nc.sbuf_tensor("f_s1t", [128, 512], BF16))
        gtmp = [st.enter_context(nc.sbuf_tensor(f"f_gt{i}", [128, 1024], F32))
                for i in range(2)]
        sb.op("dve", lambda e: e.memset(onesb[:], 1.0), writes=["onesb"])
        sb.op("dve", lambda e: e.memset(cw[:], 1.0), writes=["cwall"])
        for b in range(B_PER_CORE):
            for c in range(NC_):
                sb.op("sp", lambda e, b=b, c=c: e.dma_start(
                    out=xn[:, c * 4096 + b * S:c * 4096 + (b + 1) * S],
                    in_=t["xnbf_d"][b, c * 128:(c + 1) * 128, :]),
                    reads=[f"xnbf_d:{b}:{c}"], writes=[f"xn:{c}"],
                    slot=f"ld{c}")
        sb.op("pool", lambda e: e.dma_start(
            out=rt[:].rearrange("p (c n) -> p c n", c=NC_),
            in_=t["rt0"].rearrange("c p n -> p c n")),
            writes=["rt"], slot="bia2")
        sb.op("pool", lambda e: e.dma_start(out=eb[:], in_=t["eb0"]),
              writes=["eb"], slot="bia2")
        with (
            nc.psum_tensor("f_pr0", [128, 512], F32) as pr0,
            nc.psum_tensor("f_pr1", [128, 512], F32) as pr1,
        ):
            for tt in range(NTT):
                hf = tt % 2
                psl = (pr0 if hf == 0 else pr1)[:, 0:8]
                for c in range(NC_):
                    sb.op("pe", lambda e, tt=tt, c=c, psl=psl: e.matmul(
                        psl, xn[:, c * 4096 + tt * 128:c * 4096 + (tt + 1) * 128],
                        rt[:, c * 8:(c + 1) * 8],
                        start=(c == 0), stop=False),
                        reads=[f"xn:{c}", "rt"], writes=[f"pr:{hf}"], inc=False)
                sb.op("pe", lambda e, psl=psl: e.matmul(
                    psl, onesb[:], eb[:], start=False, stop=True),
                    reads=["onesb", "eb"], writes=[f"pr:{hf}"])
                sb.op("dve", lambda e, psl=psl: e.tensor_copy(lg[:], psl),
                      reads=[f"pr:{hf}"], writes=["lg"])
                sb.op("dve", lambda e: e.reduce_max(
                    rr[:, 0:1], lg[:], axis=AX.X, negate=True),
                    reads=["lg"], writes=["rr"])
                sb.op("dve", lambda e: e.reduce_max(rr[:, 1:2], lg[:], axis=AX.X),
                      reads=["lg"], writes=["rr"])
                sb.op("dve", lambda e: e.tensor_scalar(
                    rga[:], lg[:], rr[:, 1:2], None, ALU.is_ge),
                    reads=["lg", "rr"], writes=["rga"])
                sb.op("dve", lambda e: e.scalar_tensor_tensor(
                    rgb[:], rga[:], -1e30, lg[:], ALU.mult, ALU.add),
                    reads=["rga", "lg"], writes=["rgb"])
                sb.op("dve", lambda e: e.reduce_max(rr[:, 2:3], rgb[:], axis=AX.X),
                      reads=["rgb"], writes=["rr"])
                sb.op("dve", lambda e: e.tensor_scalar(
                    rga[:], lg[:], rr[:, 2:3], None, ALU.is_ge),
                    reads=["lg", "rr"], writes=["rga"])
                sb.op("act", lambda e: e.activation(
                    rgb[:], lg[:], AF.Exp, bias=rr[:, 0:1]),
                    reads=["lg", "rr"], writes=["rgb"])
                sb.op("act", lambda e: e.activation(
                    rr[:, 3:4], rr[:, 2:3], AF.Exp, bias=rr[:, 0:1]),
                    reads=["rr"], writes=["rr2"])
                sb.op("dve", lambda e: e.tensor_scalar(
                    rr[:, 4:5], rr[:, 3:4], 1.0, None, ALU.add),
                    reads=["rr2"], writes=["rr2"])
                sb.op("dve", lambda e: e.reciprocal(rr[:, 5:6], rr[:, 4:5]),
                      reads=["rr2"], writes=["rr2"])
                sb.op("dve", lambda e, tt=tt: e.scalar_tensor_tensor(
                    cw[:, tt * 10 + 2:tt * 10 + 10], rgb[:], rr[:, 5:6], rga[:],
                    ALU.mult, ALU.mult),
                    reads=["rgb", "rr2", "rga", "cwall"], writes=[f"cw:{tt}"])
        sb.barrier()
        with ExitStack() as pst:
            ph = [pst.enter_context(nc.psum_tensor(f"f_ph{i}", [128, 512], F32))
                  for i in range(4)]
            po = [pst.enter_context(nc.psum_tensor(f"f_po{i}", [128, 1024], F32))
                  for i in range(2)]
            for j in range(10):
                for wn, wt_, nch in (("w1s", w1t, NC_), ("w3s", w3t, NC_),
                                     ("w2s", w2t, 16)):
                    sb.op("pool", lambda e, wn=wn, wt_=wt_, nch=nch, j=j: e.dma_start(
                        out=wt_[:].rearrange("p (c n) -> p c n", c=nch),
                        in_=t[wn][j].rearrange("c p n -> p c n")),
                        writes=[wn], slot=wn)
                for tb in range(8):
                    for mh in range(16):
                        pa = ph[2 * (mh % 2)]
                        pb_ = ph[2 * (mh % 2) + 1]
                        for c in range(NC_):
                            sb.op("pe", lambda e, mh=mh, c=c, tb=tb, pa=pa: e.matmul(
                                pa[:], w1t[:, c * 2048 + mh * 128:c * 2048 + (mh + 1) * 128],
                                xn[:, c * 4096 + tb * 512:c * 4096 + (tb + 1) * 512],
                                start=(c == 0), stop=(c == NC_ - 1)),
                                reads=["w1s", f"xn:{c}"],
                                writes=[f"ph:{2 * (mh % 2)}"],
                                inc=(c == NC_ - 1))
                        for c in range(NC_):
                            sb.op("pe", lambda e, mh=mh, c=c, tb=tb, pb_=pb_: e.matmul(
                                pb_[:], w3t[:, c * 2048 + mh * 128:c * 2048 + (mh + 1) * 128],
                                xn[:, c * 4096 + tb * 512:c * 4096 + (tb + 1) * 512],
                                start=(c == 0), stop=(c == NC_ - 1)),
                                reads=["w3s", f"xn:{c}"],
                                writes=[f"ph:{2 * (mh % 2) + 1}"],
                                inc=(c == NC_ - 1))
                        sb.op("act", lambda e, pa=pa: e.activation(
                            s1t[:], pa[:], AF.Silu),
                            reads=[f"ph:{2 * (mh % 2)}"], writes=["s1t"])
                        sb.op("dve", lambda e, mh=mh, pb_=pb_: e.tensor_mul(
                            hh[:, mh * 512:(mh + 1) * 512], s1t[:], pb_[:]),
                            reads=["s1t", f"ph:{2 * (mh % 2) + 1}"],
                            writes=[f"hh:{mh}"])
                    for ti in range(4):
                        tt = tb * 4 + ti
                        pp = po[ti % 2]
                        for hc in range(16):
                            for dh in range(2):
                                sb.op("pe", lambda e, hc=hc, dh=dh, ti=ti, pp=pp: e.matmul(
                                    pp[:, dh * 512:(dh + 1) * 512],
                                    hh[:, hc * 512 + ti * 128:hc * 512 + (ti + 1) * 128],
                                    w2t[:, hc * 1024 + dh * 512:hc * 1024 + (dh + 1) * 512],
                                    start=(hc == 0), stop=(hc == 15)),
                                    reads=[f"hh:{hc}", "w2s"],
                                    writes=[f"po:{ti % 2}"],
                                    inc=(hc == 15 and dh == 1))
                        sb.op("dve", lambda e, tt=tt, j=j, ti=ti, pp=pp: e.tensor_scalar(
                            gtmp[ti % 2][:], pp[:], cw[:, tt * 10 + j:tt * 10 + j + 1],
                            None, ALU.mult),
                            reads=[f"po:{ti % 2}", f"cw:{tt}", "cwall"],
                            writes=[f"gt:{ti % 2}"])
                        sb.op("pool", lambda e, tt=tt, ti=ti, j=j: e.dma_start(
                            out=t["gmoe_d"][tt * 128:(tt + 1) * 128, :],
                            in_=gtmp[ti % 2][:],
                            accum_op=(ALU.bypass if j == 0 else ALU.add)),
                            reads=[f"gt:{ti % 2}"], writes=[f"gmoe_d:{tt}"],
                            slot=f"gm{tt}")
    sb.barrier()


def emit_s4(p, b):
    nc, sb, t = p.nc, p.sb, p.t
    sb.barrier()
    with ExitStack() as st:
        gfm = st.enter_context(nc.sbuf_tensor(f"c_g{b}", [128, NC_ * S], F32))
        x1b = [st.enter_context(nc.sbuf_tensor(f"c_x1{b}{i}", [128, S], F32))
               for i in range(2)]
        gt = st.enter_context(nc.sbuf_tensor(f"c_gt{b}", [128, 1024], F32))
        gs = st.enter_context(nc.sbuf_tensor(f"c_gs{b}", [128, 1024], BF16))
        xs_ = [st.enter_context(nc.sbuf_tensor(f"c_xs{b}{i}", [128, S], F32))
               for i in range(2)]
        ss_ = [st.enter_context(nc.sbuf_tensor(f"c_ss{b}{i}", [128, S], F32))
               for i in range(2)]
        idn = st.enter_context(nc.sbuf_tensor(f"c_idn{b}", [128, 128], BF16))
        cb = st.enter_context(nc.sbuf_tensor(f"c_cb{b}", [128, S], BF16))
        sq = st.enter_context(nc.sbuf_tensor(f"c_sq{b}", [128, S], BF16))
        ones = st.enter_context(nc.sbuf_tensor(f"c_on{b}", [128, 1], BF16))
        onesr = st.enter_context(nc.sbuf_tensor(f"c_onr{b}", [1, 128], F32))
        rows = st.enter_context(nc.sbuf_tensor(f"c_rw{b}", [1, 3 * S], F32))
        t12 = st.enter_context(nc.sbuf_tensor(f"c_t12{b}", [128, S], F32))
        sb.op("dve", lambda e: e.memset(ones[:], 1.0), writes=["ones"])
        sb.op("dve", lambda e: e.memset(onesr[:], 1.0), writes=["onesr"])
        sb.op("sp", lambda e: e.dma_start(out=idn[:], in_=t["idn"]),
              writes=["idn"], slot="bia")
        with nc.psum_tensor(f"c_pt{b}", [128, 2048], BF16) as pt:
            for tloc in range(16):
                tt = b * 16 + tloc
                sb.op("sp", lambda e, tt=tt: e.dma_start(
                    out=gt[:], in_=t["gmoe_d"][tt * 128:(tt + 1) * 128, :]),
                    reads=[f"gmoe_d:{tt}"], writes=["gt"], slot="ld0")
                sb.op("act", lambda e: e.activation(gs[:], gt[:], AF.Silu),
                      reads=["gt"], writes=["gs"])
                for c in range(NC_):
                    pb = c % 2
                    sb.op("pe", lambda e, c=c, pb=pb: e.transpose(
                        pt[:, pb * 1024:pb * 1024 + 128],
                        gs[:, c * 128:(c + 1) * 128], idn[:]),
                        reads=["gs", "idn"], writes=[f"ptr:{pb}"])
                    sb.op("dve", lambda e, c=c, tloc=tloc, pb=pb: e.tensor_copy(
                        gfm[:, c * S + tloc * 128:c * S + (tloc + 1) * 128],
                        pt[:, pb * 1024:pb * 1024 + 128]),
                        reads=[f"ptr:{pb}"], writes=[f"gfm:{c}"])
        with (
            nc.psum_tensor(f"c_ps1{b}", [1, S], F32) as p_s1,
            nc.psum_tensor(f"c_ps2{b}", [1, S], F32) as p_s2,
        ):
            for c in range(NC_):
                xb = xs_[c % 2]
                sbf = ss_[c % 2]
                sb.op("sp", lambda e, c=c, xb=xb: e.dma_start(
                    out=xb[:], in_=t["x_fm"][b, c * 128:(c + 1) * 128, :]),
                    writes=[f"xs_:{c % 2}"], slot=f"ld{1 + c % 2}")
                sb.op("sp", lambda e, c=c, sbf=sbf: e.dma_start(
                    out=sbf[:], in_=t["s_d"][b, c * 128:(c + 1) * 128, :]),
                    reads=[f"s_d:{b}:{c}"], writes=[f"ss_:{c % 2}"],
                    slot=f"ld{3 + c % 2}")
                x1c = x1b[c % 2][:]
                sb.op("dve", lambda e, c=c, sbf=sbf: e.tensor_mul(
                    t12[:], sbf[:], gfm[:, c * S:(c + 1) * S]),
                    reads=[f"ss_:{c % 2}", f"gfm:{c}"], writes=["t12"])
                sb.op("dve", lambda e, c=c, xb=xb, x1c=x1c: e.tensor_add(
                    x1c, xb[:], t12[:]),
                    reads=[f"xs_:{c % 2}", "t12"], writes=[f"x1:{c % 2}"])
                sb.op("sp", lambda e, c=c, x1c=x1c: e.dma_start(
                    out=t["x1_d"][b, c * 128:(c + 1) * 128, :], in_=x1c),
                    reads=[f"x1:{c % 2}"], writes=[f"x1_d:{b}:{c}"], slot="st0")
                sb.op("pool", lambda e, x1c=x1c: e.tensor_copy(cb[:], x1c),
                      reads=[f"x1:{c % 2}"], writes=["cb"])
                sb.op("act", lambda e, x1c=x1c: e.activation(sq[:], x1c, AF.Square),
                      reads=[f"x1:{c % 2}"], writes=["sq"])
                for tc in range(4):
                    sl = slice(tc * 512, (tc + 1) * 512)
                    sb.op("pe", lambda e, sl=sl, c=c: e.matmul(
                        p_s1[:, sl], ones[:], cb[:, sl],
                        start=(c == 0), stop=(c == NC_ - 1)),
                        reads=["cb", "ones"], writes=[f"Cs1:{tc}"])
                    sb.op("pe", lambda e, sl=sl, c=c: e.matmul(
                        p_s2[:, sl], ones[:], sq[:, sl],
                        start=(c == 0), stop=(c == NC_ - 1)),
                        reads=["sq", "ones"], writes=[f"Cs2:{tc}"])
            emit_ln_rows(p, "C", p_s1, p_s2, rows, p.cst[0:1, 0:1])
        sb.barrier()
        with (
            nc.psum_tensor(f"c_pm{b}", [128, S], F32) as p_mu,
            nc.psum_tensor(f"c_pr{b}", [128, S], F32) as p_rs,
        ):
            for tc in range(4):
                sl = slice(tc * 512, (tc + 1) * 512)
                sb.op("pe", lambda e, sl=sl: e.matmul(
                    p_mu[:, sl], onesr[:], rows[0:1, 0:S][:, sl],
                    start=True, stop=True),
                    reads=["Cmu", "onesr"], writes=["CBmu"])
                sb.op("pe", lambda e, sl=sl: e.matmul(
                    p_rs[:, sl], onesr[:], rows[0:1, S:2 * S][:, sl],
                    start=True, stop=True),
                    reads=["Crs", "onesr"], writes=["CBrs"])
            for c in range(NC_):
                xb = x1b[c % 2]
                sb.op("sp", lambda e, c=c, xb=xb: e.dma_start(
                    out=xb[:], in_=t["x1_d"][b, c * 128:(c + 1) * 128, :]),
                    reads=[f"x1_d:{b}:{c}"], writes=[f"x1:{c % 2}"],
                    slot=f"ld{1 + c % 2}")
                sb.op("dve", lambda e, xb=xb: e.tensor_sub(
                    t12[:], xb[:], p_mu[:]),
                    reads=[f"x1:{c % 2}", "CBmu"], writes=["t12"])
                sb.op("dve", lambda e: e.tensor_mul(t12[:], t12[:], p_rs[:]),
                      reads=["t12", "CBrs"], writes=["t12"])
                sb.op("sp", lambda e, c=c: e.dma_start(
                    out=t["xn1_d"][b, c * 128:(c + 1) * 128, :], in_=t12[:]),
                    reads=["t12"], writes=[f"xn1_d:{b}:{c}"], slot="st1")
                sb.op("pool", lambda e, c=c: e.tensor_copy(cb[:], t12[:]),
                      reads=["t12"], writes=["cb"])
                sb.op("pool", lambda e, c=c: e.dma_start(
                    out=t["xn1bf_d"][b, c * 128:(c + 1) * 128, :], in_=cb[:]),
                    reads=["cb"], writes=[f"xn1bf_d:{b}:{c}"], slot="st2")
    sb.barrier()


def emit_s5(p, b):
    nc, sb, t = p.nc, p.sb, p.t
    sb.barrier()
    HP = 2052
    with ExitStack() as st:
        mixed = st.enter_context(nc.sbuf_tensor(f"r_mx{b}", [128, NC_ * S], BF16))
        biases = st.enter_context(nc.sbuf_tensor(f"r_bs{b}", [128, 64], F32))
        bi2 = st.enter_context(nc.sbuf_tensor(f"r_b2{b}", [128, 40], F32))
        xc = [st.enter_context(nc.sbuf_tensor(f"r_xc{b}{i}", [128, S], F32))
              for i in range(2)]
        t1 = st.enter_context(nc.sbuf_tensor(f"r_t1{b}", [128, 512], F32))
        t2 = st.enter_context(nc.sbuf_tensor(f"r_t2{b}", [128, 512], F32))
        cb = st.enter_context(nc.sbuf_tensor(f"r_cb{b}", [128, S], BF16))
        sq = st.enter_context(nc.sbuf_tensor(f"r_sq{b}", [128, S], BF16))
        ones = st.enter_context(nc.sbuf_tensor(f"r_on{b}", [128, 1], BF16))
        onesr = st.enter_context(nc.sbuf_tensor(f"r_onr{b}", [1, 128], F32))
        rows = st.enter_context(nc.sbuf_tensor(f"r_rw{b}", [1, 3 * S], F32))
        wl = [st.enter_context(nc.sbuf_tensor(f"r_wl{b}{i}", [128, NC_ * 128], BF16))
              for i in range(2)]
        sb.op("dve", lambda e: e.memset(ones[:], 1.0), writes=["ones"])
        sb.op("dve", lambda e: e.memset(onesr[:], 1.0), writes=["onesr"])
        sb.op("sp", lambda e: e.dma_start(out=biases[:, 0:8], in_=t["ln3g"]),
              writes=["bss"], slot="bia")
        sb.op("sp", lambda e: e.dma_start(out=biases[:, 8:16], in_=t["ln3b"]),
              writes=["bss"], slot="bia")
        sb.op("sp", lambda e: e.dma_start(out=biases[:, 16:40], in_=t["taps"]),
              writes=["bss"], slot="bia")
        sb.op("sp", lambda e: e.dma_start(out=biases[:, 40:48], in_=t["mixb"]),
              writes=["bss"], slot="bia")
        for nm, sl0 in (("bi", 0), ("bg", 8), ("bout", 16), ("bso", 24),
                        ("bsin", 32), ("bd", 34), ("bsel", 36)):
            n = 2 if nm in ("bsin", "bd", "bsel") else 8
            sb.op("sp", lambda e, nm=nm, sl0=sl0, n=n: e.dma_start(
                out=bi2[:, sl0:sl0 + n], in_=t[nm]),
                writes=["bss"], slot="bia")
        with (
            nc.psum_tensor(f"r_ps1{b}", [1, S], F32) as p_s1,
            nc.psum_tensor(f"r_ps2{b}", [1, S], F32) as p_s2,
        ):
            for c in range(NC_):
                xb = xc[c % 2]
                sb.op("sp", lambda e, c=c, xb=xb: e.dma_start(
                    out=xb[:], in_=t["xn1_d"][b, c * 128:(c + 1) * 128, :]),
                    reads=[f"xn1_d:{b}:{c}"], writes=[f"xc:{c % 2}"],
                    slot=f"ld{c % 2}"),
                sb.op("pool", lambda e, xb=xb: e.tensor_copy(cb[:], xb[:]),
                      reads=[f"xc:{c % 2}"], writes=["cb"])
                sb.op("act", lambda e, xb=xb: e.activation(sq[:], xb[:], AF.Square),
                      reads=[f"xc:{c % 2}"], writes=["sq"])
                for tc in range(4):
                    sl = slice(tc * 512, (tc + 1) * 512)
                    sb.op("pe", lambda e, sl=sl, c=c: e.matmul(
                        p_s1[:, sl], ones[:], cb[:, sl],
                        start=(c == 0), stop=(c == NC_ - 1)),
                        reads=["cb", "ones"], writes=[f"Rs1:{tc}"])
                    sb.op("pe", lambda e, sl=sl, c=c: e.matmul(
                        p_s2[:, sl], ones[:], sq[:, sl],
                        start=(c == 0), stop=(c == NC_ - 1)),
                        reads=["sq", "ones"], writes=[f"Rs2:{tc}"])
            emit_ln_rows(p, "R", p_s1, p_s2, rows, p.cst[0:1, 0:1])
        sb.barrier()
        with ExitStack() as st2:
            h_sb = st2.enter_context(
                nc.sbuf_tensor(f"r_h{b}", [128, NC_ * HP], BF16))
            acc = st2.enter_context(nc.sbuf_tensor(f"r_ac{b}", [128, S], F32))
            tf = st2.enter_context(nc.sbuf_tensor(f"r_tf{b}", [128, S], F32))
            sb.op("dve", lambda e: e.memset(h_sb[:], 0.0), writes=["hz"])
            with (
                nc.psum_tensor(f"r_pm{b}", [128, S], F32) as p_mu,
                nc.psum_tensor(f"r_pr2{b}", [128, S], F32) as p_rs,
            ):
                for tc in range(4):
                    sl = slice(tc * 512, (tc + 1) * 512)
                    sb.op("pe", lambda e, sl=sl: e.matmul(
                        p_mu[:, sl], onesr[:], rows[0:1, 0:S][:, sl],
                        start=True, stop=True),
                        reads=["Rmu", "onesr"], writes=["RBmu"])
                    sb.op("pe", lambda e, sl=sl: e.matmul(
                        p_rs[:, sl], onesr[:], rows[0:1, S:2 * S][:, sl],
                        start=True, stop=True),
                        reads=["Rrs", "onesr"], writes=["RBrs"])
                for c in range(NC_):
                    xb = xc[c % 2]
                    sb.op("sp", lambda e, c=c, xb=xb: e.dma_start(
                        out=xb[:], in_=t["xn1_d"][b, c * 128:(c + 1) * 128, :]),
                        reads=[f"xn1_d:{b}:{c}"], writes=[f"xc2:{c % 2}"],
                        slot=f"ld{2 + c % 2}")
                    sb.op("dve", lambda e, c=c, xb=xb: e.tensor_sub(
                        tf[:], xb[:], p_mu[:]),
                        reads=[f"xc2:{c % 2}", "RBmu"], writes=["rtf"])
                    sb.op("dve", lambda e: e.tensor_mul(tf[:], tf[:], p_rs[:]),
                          reads=["rtf", "RBrs"], writes=["rtf"])
                    sb.op("dve", lambda e, c=c: e.tensor_scalar(
                        h_sb[:, c * HP + 1:c * HP + 1 + S],
                        tf[:], biases[:, c:c + 1], biases[:, 8 + c:9 + c],
                        ALU.mult, ALU.add),
                        reads=["rtf", "bss", "hz"], writes=[f"h:{c}"])
            for c in range(NC_):
                rd = [f"h:{c}", "bss", "hz"]
                sb.op("dve", lambda e, c=c: e.tensor_scalar(
                    acc[:], h_sb[:, c * HP:c * HP + S],
                    biases[:, 16 + c * 3:17 + c * 3], biases[:, 40 + c:41 + c],
                    ALU.mult, ALU.add),
                    reads=rd, writes=["racc"])
                sb.op("dve", lambda e, c=c: e.scalar_tensor_tensor(
                    acc[:], h_sb[:, c * HP + 1:c * HP + 1 + S],
                    biases[:, 17 + c * 3:18 + c * 3], acc[:],
                    ALU.mult, ALU.add),
                    reads=rd + ["racc"], writes=["racc"])
                sb.op("dve", lambda e, c=c: e.scalar_tensor_tensor(
                    mixed[:, c * S:(c + 1) * S],
                    h_sb[:, c * HP + 2:c * HP + 2 + S],
                    biases[:, 18 + c * 3:19 + c * 3], acc[:],
                    ALU.mult, ALU.add),
                    reads=rd + ["racc"], writes=[f"mx:{c}"])
        sb.barrier()
        with ExitStack() as st3:
            selst = st3.enter_context(nc.sbuf_tensor(f"r_ss{b}", [128, 2 * S], BF16))
            st4 = st3.enter_context(ExitStack())
            dec = st4.enter_context(nc.sbuf_tensor(f"r_dc{b}", [128, 2 * S], F32))
            xst = st4.enter_context(nc.sbuf_tensor(f"r_xs{b}", [128, 2 * S], F32))
            stt = st4.enter_context(nc.sbuf_tensor(f"r_st{b}", [128, 2 * S], F32))
            d1 = st4.enter_context(nc.sbuf_tensor(f"r_d1{b}", [128, S], F32))
            db = st4.enter_context(nc.sbuf_tensor(f"r_db{b}", [128, S], F32))
            with ExitStack() as pst:
                pj = [pst.enter_context(nc.psum_tensor(f"r_pj{b}{i}", [128, 512], F32))
                      for i in range(8)]
                for wn, dst, fn, bcol in (("wd", dec, AF.Sigmoid, 34),
                                          ("wsin", xst, AF.Tanh, 32)):
                    for m in range(2):
                        wb = m % 2
                        sb.op("pool", lambda e, wn=wn, m=m, wb=wb: e.dma_start(
                            out=wl[wb][:].rearrange("p (c n) -> p c n", c=NC_),
                            in_=t[wn][m].rearrange("c p n -> p c n")),
                            writes=[f"wl{wb}"], slot=f"wqk{wb}")
                        for c in range(NC_):
                            for tc in range(4):
                                sb.op("pe", lambda e, c=c, tc=tc, wb=wb, m=m: e.matmul(
                                    pj[(m % 2) * 4 + tc][:],
                                    wl[wb][:, c * 128:(c + 1) * 128],
                                    mixed[:, c * S + tc * 512:c * S + (tc + 1) * 512],
                                    start=(c == 0), stop=(c == NC_ - 1)),
                                    reads=[f"wl{wb}", f"mx:{c}"],
                                    writes=[f"pj:{(m % 2) * 4 + tc}"],
                                    inc=(c == NC_ - 1))
                        for tc in range(4):
                            sb.op("act", lambda e, m=m, tc=tc, dst=dst, fn=fn, bcol=bcol: e.activation(
                                dst[:, m * S + tc * 512:m * S + (tc + 1) * 512],
                                pj[(m % 2) * 4 + tc][:], fn,
                                bias=bi2[:, bcol + m:bcol + m + 1]),
                                reads=[f"pj:{(m % 2) * 4 + tc}", "bss"],
                                writes=[f"{wn}o:{m}"])
                for m in range(2):
                    sb.op("dve", lambda e, m=m: e.tensor_scalar(
                        d1[:], dec[:, m * S:(m + 1) * S], -1.0, 1.0,
                        ALU.mult, ALU.add),
                        reads=[f"wdo:{m}"], writes=["d1"])
                    sb.op("dve", lambda e, m=m: e.tensor_mul(
                        db[:], d1[:], xst[:, m * S:(m + 1) * S]),
                        reads=[f"wsino:{m}", "d1"], writes=["db"])
                    sb.op("dve", lambda e, m=m: e.tensor_tensor_scan(
                        stt[:, m * S:(m + 1) * S], dec[:, m * S:(m + 1) * S],
                        db[:], 0.0, ALU.mult, ALU.add),
                        reads=[f"wdo:{m}", "db"], writes=[f"stt:{m}"])
                for m in range(2):
                    wb = m % 2
                    sb.op("pool", lambda e, m=m, wb=wb: e.dma_start(
                        out=wl[wb][:].rearrange("p (c n) -> p c n", c=NC_),
                        in_=t["wsel"][m].rearrange("c p n -> p c n")),
                        writes=[f"wl{wb}"], slot=f"wqk{wb}")
                    for c in range(NC_):
                        for tc in range(4):
                            sb.op("pe", lambda e, c=c, tc=tc, wb=wb, m=m: e.matmul(
                                pj[(m % 2) * 4 + tc][:],
                                wl[wb][:, c * 128:(c + 1) * 128],
                                mixed[:, c * S + tc * 512:c * S + (tc + 1) * 512],
                                start=(c == 0), stop=(c == NC_ - 1)),
                                reads=[f"wl{wb}", f"mx:{c}"],
                                writes=[f"pj:{(m % 2) * 4 + tc}"],
                                inc=(c == NC_ - 1))
                    for tc in range(4):
                        sb.op("act", lambda e, m=m, tc=tc: e.activation(
                            t1[:], pj[(m % 2) * 4 + tc][:], AF.Sigmoid,
                            bias=bi2[:, 36 + m:37 + m]),
                            reads=[f"pj:{(m % 2) * 4 + tc}", "bss"],
                            writes=["rt1"])
                        sb.op("dve", lambda e, m=m, tc=tc: e.tensor_mul(
                            selst[:, m * S + tc * 512:m * S + (tc + 1) * 512],
                            t1[:], stt[:, m * S + tc * 512:m * S + (tc + 1) * 512]),
                            reads=["rt1", f"stt:{m}"], writes=[f"sst:{m}"])
            sb.barrier()
            st4.close()
            cmb = st3.enter_context(nc.sbuf_tensor(f"r_cm{b}", [128, NC_ * S], BF16))
            yv = st3.enter_context(nc.sbuf_tensor(f"r_yv{b}", [128, 512], F32))
            uv = st3.enter_context(nc.sbuf_tensor(f"r_uv{b}", [128, 512], F32))
            gv = st3.enter_context(nc.sbuf_tensor(f"r_gv{b}", [128, 512], F32))
            wso_t = st3.enter_context(nc.sbuf_tensor(f"r_ws{b}", [128, 2 * 128], BF16))
            with ExitStack() as pst:
                py = pst.enter_context(nc.psum_tensor(f"r_py{b}", [128, 512], F32))
                pu = pst.enter_context(nc.psum_tensor(f"r_pu{b}", [128, 512], F32))
                pg = pst.enter_context(nc.psum_tensor(f"r_pg{b}", [128, 512], F32))
                for m in range(NC_):
                    sb.op("pool", lambda e, m=m: e.dma_start(
                        out=wso_t[:].rearrange("p (c n) -> p c n", c=2),
                        in_=t["wso"][m].rearrange("c p n -> p c n")),
                        writes=["wsot"], slot="wso")
                    sb.op("pool", lambda e, m=m: e.dma_start(
                        out=wl[0][:].rearrange("p (c n) -> p c n", c=NC_),
                        in_=t["wi"][m].rearrange("c p n -> p c n")),
                        writes=["wl0"], slot="wqk0")
                    sb.op("pool", lambda e, m=m: e.dma_start(
                        out=wl[1][:].rearrange("p (c n) -> p c n", c=NC_),
                        in_=t["wg"][m].rearrange("c p n -> p c n")),
                        writes=[f"wl1"], slot="wqk1")
                    for tc in range(4):
                        for c2 in range(2):
                            sb.op("pe", lambda e, c2=c2, tc=tc: e.matmul(
                                py[:], wso_t[:, c2 * 128:(c2 + 1) * 128],
                                selst[:, c2 * S + tc * 512:c2 * S + (tc + 1) * 512],
                                start=(c2 == 0), stop=(c2 == 1)),
                                reads=["wsot", f"sst:{c2}"], writes=["py"],
                                inc=(c2 == 1))
                        for c in range(NC_):
                            sb.op("pe", lambda e, c=c, tc=tc: e.matmul(
                                pu[:], wl[0][:, c * 128:(c + 1) * 128],
                                mixed[:, c * S + tc * 512:c * S + (tc + 1) * 512],
                                start=(c == 0), stop=(c == NC_ - 1)),
                                reads=["wl0", f"mx:{c}"], writes=["pu"],
                                inc=(c == NC_ - 1))
                        for c in range(NC_):
                            sb.op("pe", lambda e, c=c, tc=tc: e.matmul(
                                pg[:], wl[1][:, c * 128:(c + 1) * 128],
                                mixed[:, c * S + tc * 512:c * S + (tc + 1) * 512],
                                start=(c == 0), stop=(c == NC_ - 1)),
                                reads=["wl1", f"mx:{c}"], writes=["pg"],
                                inc=(c == NC_ - 1))
                        sb.op("act", lambda e, m=m: e.activation(
                            yv[:], py[:], AF.Identity, bias=bi2[:, 24 + m:25 + m]),
                            reads=["py", "bss"], writes=["yv"])
                        sb.op("act", lambda e, m=m: e.activation(
                            uv[:], pu[:], AF.Tanh, bias=bi2[:, 0 + m:1 + m]),
                            reads=["pu", "bss"], writes=["uv"])
                        sb.op("act", lambda e, m=m: e.activation(
                            gv[:], pg[:], AF.Sigmoid, bias=bi2[:, 8 + m:9 + m]),
                            reads=["pg", "bss"], writes=["gv"])
                        sb.op("dve", lambda e: e.tensor_sub(t1[:], yv[:], uv[:]),
                              reads=["yv", "uv"], writes=["rt1"])
                        sb.op("dve", lambda e: e.tensor_mul(t2[:], gv[:], t1[:]),
                              reads=["gv", "rt1"], writes=["rt2"])
                        sb.op("dve", lambda e, m=m, tc=tc: e.tensor_add(
                            cmb[:, m * S + tc * 512:m * S + (tc + 1) * 512],
                            t2[:], uv[:]),
                            reads=["rt2", "uv"], writes=[f"cmb:{m}"])
            sb.barrier()
            with ExitStack() as pst:
                pw = [pst.enter_context(nc.psum_tensor(f"r_pw{b}{i}", [128, 512], F32))
                      for i in range(8)]
                s1v = st3.enter_context(nc.sbuf_tensor(f"r_s1{b}", [128, S], F32))
                for m in range(NC_):
                    wb = m % 2
                    pbase = wb * 4
                    sb.op("pool", lambda e, m=m, wb=wb: e.dma_start(
                        out=wl[wb][:].rearrange("p (c n) -> p c n", c=NC_),
                        in_=t["wout"][m].rearrange("c p n -> p c n")),
                        writes=[f"wl{wb}"], slot=f"wqk{wb}")
                    xb = xc[m % 2]
                    sb.op("sp", lambda e, m=m, xb=xb: e.dma_start(
                        out=xb[:], in_=t["xn1_d"][b, m * 128:(m + 1) * 128, :]),
                        reads=[f"xn1_d:{b}:{m}"], writes=[f"xc3:{m % 2}"],
                        slot=f"ld{4 + m % 2}")
                    for tc in range(4):
                        for c in range(NC_):
                            sb.op("pe", lambda e, c=c, tc=tc, wb=wb, pbase=pbase: e.matmul(
                                pw[pbase + tc][:], wl[wb][:, c * 128:(c + 1) * 128],
                                cmb[:, c * S + tc * 512:c * S + (tc + 1) * 512],
                                start=(c == 0), stop=(c == NC_ - 1)),
                                reads=[f"wl{wb}", f"cmb:{c}"],
                                writes=[f"pw:{pbase + tc}"],
                                inc=(c == NC_ - 1))
                        sb.op("act", lambda e, m=m, tc=tc, pbase=pbase: e.activation(
                            t1[:], pw[pbase + tc][:], AF.Identity,
                            bias=bi2[:, 16 + m:17 + m]),
                            reads=[f"pw:{pbase + tc}", "bss"], writes=["rt1"])
                        sb.op("dve", lambda e, m=m, tc=tc, xb=xb: e.tensor_add(
                            s1v[:, tc * 512:(tc + 1) * 512], t1[:],
                            xb[:, tc * 512:(tc + 1) * 512]),
                            reads=["rt1", f"xc3:{m % 2}"], writes=["s1v"])
                    sb.op("sp", lambda e, m=m: e.dma_start(
                        out=t["s1_d"][b, m * 128:(m + 1) * 128, :], in_=s1v[:]),
                        reads=["s1v"], writes=[f"s1_d:{b}:{m}"], slot="sst")
    sb.barrier()


def emit_s6(p):
    nc, sb, t = p.nc, p.sb, p.t
    sb.barrier()
    NTT = 32
    C1 = 1024
    BIG = 8 * C1
    TROW = B_PER_CORE * S
    with ExitStack() as st:
        idn = st.enter_context(nc.sbuf_tensor("g_idn", [128, 128], BF16))
        rst = st.enter_context(ExitStack())
        rt = rst.enter_context(nc.sbuf_tensor("g_rt", [128, 64], BF16))
        ind = rst.enter_context(nc.sbuf_tensor("g_ind", [128, NTT * 8], F32))
        lg = rst.enter_context(nc.sbuf_tensor("g_lg", [128, 8], F32))
        rr = rst.enter_context(nc.sbuf_tensor("g_rr", [128, 8], F32))
        idnf = rst.enter_context(nc.sbuf_tensor("g_idnf", [128, 128], F32))
        indf = rst.enter_context(nc.sbuf_tensor("g_indf", [8, NTT * 128], F32))
        ones8 = rst.enter_context(nc.sbuf_tensor("g_on8", [8, NTT * 128], F32))
        rkf = rst.enter_context(nc.sbuf_tensor("g_rkf", [8, NTT * 128], F32))
        rktm = rst.enter_context(nc.sbuf_tensor("g_rktm", [128, 8], F32))
        ecap = rst.enter_context(nc.sbuf_tensor("g_ecap", [128, 8], F32))
        tids = rst.enter_context(nc.sbuf_tensor("g_tids", [128, 32], I32))
        sli = rst.enter_context(nc.sbuf_tensor("g_sli", [128, 1], I32))
        slf = rst.enter_context(nc.sbuf_tensor("g_slf", [128, 1], F32))
        idxp = rst.enter_context(nc.sbuf_tensor("g_idxp", [128, 64], I32))
        tmt = [rst.enter_context(nc.sbuf_tensor(f"g_tmt{i}", [128, 1024], BF16))
               for i in range(2)]
        xb2 = [rst.enter_context(nc.sbuf_tensor(f"g_xb{i}", [128, NC_ * 128], BF16))
               for i in range(2)]
        sb.op("pool", lambda e: e.dma_start(
            out=rt[:].rearrange("p (c n) -> p c n", c=NC_),
            in_=t["rt1"].rearrange("c p n -> p c n")),
            writes=["rt"], slot="bia2")
        sb.op("sp", lambda e: e.dma_start(out=idn[:], in_=t["idn"]),
              writes=["idn"], slot="bia")
        sb.op("sp", lambda e: e.dma_start(out=idnf[:], in_=t["idnf"]),
              writes=["idnf"], slot="bia")
        sb.op("sp", lambda e: e.dma_start(out=ecap[:], in_=t["ecap1"]),
              writes=["ecap"], slot="bia")
        sb.op("sp", lambda e: e.dma_start(out=tids[:], in_=t["tids"]),
              writes=["tids"], slot="bia")
        sb.op("dve", lambda e: e.memset(idxp[:], TROW), writes=["idxp"])
        sb.op("sp", lambda e: e.dma_start(
            out=t["idx1_d"][0:8 * C1, :].rearrange("(p j) o -> p (j o)", p=128),
            in_=idxp[:]),
            reads=["idxp"], writes=["idx1pre"], slot="st3")
        with (
            nc.psum_tensor("g_pr0", [128, 512], F32) as pr0,
            nc.psum_tensor("g_pr1", [128, 512], F32) as pr1,
            nc.psum_tensor("g_ptr", [128, 2048], BF16) as ptr,
            nc.psum_tensor("g_pti", [8, 512], F32) as pti,
        ):
            for tt in range(NTT):
                b = tt // 16
                tloc = tt % 16
                hf = tt % 2
                psl = (pr0 if hf == 0 else pr1)[:, 0:8]
                xb = xb2[hf]
                for c in range(NC_):
                    sb.op("sp", lambda e, b=b, c=c, tloc=tloc, xb=xb: e.dma_start(
                        out=xb[:, c * 128:(c + 1) * 128],
                        in_=t["xn1bf_d"][b, c * 128:(c + 1) * 128,
                                         tloc * 128:(tloc + 1) * 128]),
                        reads=[f"xn1bf_d:{b}:{c}"], writes=[f"gxt:{hf}:{c}"],
                        slot=f"ld{hf}")
                allx = [f"gxt:{hf}:{i}" for i in range(NC_)]
                for c in range(NC_):
                    sb.op("pe", lambda e, c=c, psl=psl, xb=xb: e.matmul(
                        psl, xb[:, c * 128:(c + 1) * 128],
                        rt[:, c * 8:(c + 1) * 8],
                        start=(c == 0), stop=(c == NC_ - 1)),
                        reads=allx + ["rt"], writes=[f"gpr:{hf}"],
                        inc=(c == NC_ - 1))
                tb_ = tmt[hf]
                for c in range(NC_):
                    pb = c % 2
                    sb.op("pe", lambda e, c=c, xb=xb, pb=pb: e.transpose(
                        ptr[:, pb * 1024:pb * 1024 + 128],
                        xb[:, c * 128:(c + 1) * 128], idn[:]),
                        reads=allx + ["idn"], writes=[f"gptr:{pb}"])
                    sb.op("dve", lambda e, c=c, pb=pb, tb_=tb_: e.tensor_copy(
                        tb_[:, c * 128:(c + 1) * 128],
                        ptr[:, pb * 1024:pb * 1024 + 128]),
                        reads=[f"gptr:{pb}"], writes=[f"gtmt:{hf}"])
                sb.op("sp", lambda e, tt=tt, tb_=tb_: e.dma_start(
                    out=t["xn1tm_d"][tt * 128:(tt + 1) * 128, :], in_=tb_[:]),
                    reads=[f"gtmt:{hf}"], writes=[f"xn1tm:{tt}"],
                    slot=f"st{hf}")
                sb.op("dve", lambda e, psl=psl: e.tensor_copy(lg[:], psl),
                      reads=[f"gpr:{hf}"], writes=["glg"])
                sb.op("dve", lambda e: e.reduce_max(rr[:, 0:1], lg[:], axis=AX.X),
                      reads=["glg"], writes=["grr"])
                sb.op("dve", lambda e, tt=tt: e.tensor_scalar(
                    ind[:, tt * 8:(tt + 1) * 8], lg[:], rr[:, 0:1], None,
                    ALU.is_ge),
                    reads=["glg", "grr"], writes=[f"ind:{tt}"])
                sb.op("pe", lambda e, tt=tt: e.transpose(
                    pti[:, 0:128], ind[:, tt * 8:(tt + 1) * 8], idnf[:]),
                    reads=[f"ind:{tt}", "idnf"], writes=["gpti"])
                sb.op("dve", lambda e, tt=tt: e.tensor_copy(
                    indf[:, tt * 128:(tt + 1) * 128], pti[:, 0:128]),
                    reads=["gpti"], writes=[f"indf:{tt}"])
        sb.barrier()
        with nc.psum_tensor("g_ptm", [128, 512], F32) as ptm:
            sb.op("dve", lambda e: e.memset(ones8[:], 1.0), writes=["on8"])
            sb.op("dve", lambda e: e.tensor_tensor_scan(
                rkf[:], ones8[:], indf[:], 0.0, ALU.mult, ALU.add),
                reads=["on8"] + [f"indf:{i}" for i in range(NTT)],
                writes=["rkf"])
            for tt in range(NTT):
                sb.op("pe", lambda e, tt=tt: e.transpose(
                    ptm[:, 0:8], rkf[:, tt * 128:(tt + 1) * 128], idnf[0:8, 0:8]),
                    reads=["rkf", "idnf"], writes=["gptm"])
                sb.op("dve", lambda e: e.tensor_copy(rktm[:], ptm[:, 0:8]),
                      reads=["gptm"], writes=["rktm"])
                sb.op("dve", lambda e: e.tensor_add(rr[:], rktm[:], ecap[:]),
                      reads=["rktm", "ecap"], writes=["grr"])
                sb.op("dve", lambda e: e.tensor_scalar(
                    rr[:], rr[:], -1.0 - BIG, None, ALU.add),
                    reads=["grr"], writes=["grr"])
                sb.op("dve", lambda e, tt=tt: e.tensor_mul(
                    rr[:], ind[:, tt * 8:(tt + 1) * 8], rr[:]),
                    reads=[f"ind:{tt}", "grr"], writes=["grr"])
                sb.op("dve", lambda e: e.tensor_scalar(
                    rr[:], rr[:], float(BIG), None, ALU.add),
                    reads=["grr"], writes=["grr"])
                sb.op("dve", lambda e: e.tensor_reduce(
                    slf[:], rr[:], AX.X, ALU.min),
                    reads=["grr"], writes=["slf"])
                sb.op("dve", lambda e: e.tensor_copy(sli[:], slf[:]),
                      reads=["slf"], writes=["sli"])
                sb.op("pool", lambda e, tt=tt: e.indirect_dma_start(
                    t["idx1_d"][0:128, :], IoA(sli[:], 0),
                    tids[:, tt:tt + 1], None),
                    reads=["sli", "tids", "idx1pre"], writes=["idxsc"],
                    slot="st3")
        sb.barrier()
        rst.close()
        with ExitStack() as st2:
            fc = st2.enter_context(nc.sbuf_tensor("g_fc", [128, NC_ * 4096], BF16))
            pr_ = st2.enter_context(nc.sbuf_tensor("g_pr", [128, 32 * 1024], BF16))
            hb = st2.enter_context(nc.sbuf_tensor("g_hb", [128, 32 * 512], BF16))
            rt_t = st2.enter_context(nc.sbuf_tensor("g_rtt", [128, 512], BF16))
            xgt = st2.enter_context(nc.sbuf_tensor("g_xgt", [128, 4 * 1024], BF16))
            xgf = st2.enter_context(nc.sbuf_tensor("g_xgf", [128, NC_ * 512], BF16))
            idxj = st2.enter_context(nc.sbuf_tensor("g_idxj", [128, 8], I32))
            gtmp = [st2.enter_context(nc.sbuf_tensor(f"g_gt{i}", [128, 512], F32))
                    for i in range(4)]
            sb.op("dve", lambda e: e.memset(xgt[:], 0.0), writes=["xgt_z"])
            with ExitStack() as pst:
                ph = [pst.enter_context(nc.psum_tensor(f"g_ph{i}", [128, 512], F32))
                      for i in range(2)]
                po = [pst.enter_context(nc.psum_tensor(f"g_po{i}", [128, 512], F32))
                      for i in range(4)]
                ptg = pst.enter_context(nc.psum_tensor("g_ptg", [128, 2048], BF16))
                for j in range(8):
                    sb.op("pool", lambda e, j=j: e.dma_start(
                        out=fc[:].rearrange("p (c n) -> p c n", c=NC_),
                        in_=t["fcs"][j].rearrange("c p n -> p c n")),
                        writes=["gfc"], slot="w1s")
                    for dh in range(2):
                        sb.op("pool", lambda e, j=j, dh=dh: e.dma_start(
                            out=pr_[:, dh * 16384:(dh + 1) * 16384].rearrange(
                                "p (c n) -> p c n", c=32),
                            in_=t["prs"][j, dh].rearrange("c p n -> p c n")),
                            writes=[f"gprj:{dh}"], slot=f"w{2 + dh}s")
                    sb.op("sp", lambda e, j=j: e.dma_start(
                        out=idxj[:],
                        in_=t["idx1_d"][j * C1:(j + 1) * C1, :].rearrange(
                            "(p g) o -> p (g o)", p=128)),
                        reads=["idxsc"], writes=["idxj"], slot="ld4")
                    for blk in range(2):
                        for g in range(4):
                            jj = blk * 4 + g
                            sb.op("pool", lambda e, g=g, jj=jj: e.indirect_dma_start(
                                xgt[:, g * 1024:(g + 1) * 1024], None,
                                t["xn1tm_d"][0:128, :], IoA(idxj[:, jj:jj + 1], 0)),
                                reads=["idxj", "xgt_z"]
                                      + [f"xn1tm:{i}" for i in range(NTT)],
                                writes=[f"xg:{g}"], slot="ld5")
                        for g in range(4):
                            for c in range(NC_):
                                pb = (g * NC_ + c) % 2
                                sb.op("pe", lambda e, g=g, c=c, pb=pb: e.transpose(
                                    ptg[:, pb * 1024:pb * 1024 + 128],
                                    xgt[:, g * 1024 + c * 128:g * 1024 + (c + 1) * 128],
                                    idn[:]),
                                    reads=[f"xg:{g}", "idn"], writes=[f"gptg:{pb}"])
                                sb.op("dve", lambda e, g=g, c=c, pb=pb: e.tensor_copy(
                                    xgf[:, c * 512 + g * 128:c * 512 + (g + 1) * 128],
                                    ptg[:, pb * 1024:pb * 1024 + 128]),
                                    reads=[f"gptg:{pb}"], writes=[f"xgf:{c}"])
                        for mh in range(32):
                            pp = ph[mh % 2]
                            for c in range(NC_):
                                sb.op("pe", lambda e, mh=mh, c=c, pp=pp: e.matmul(
                                    pp[:], fc[:, c * 4096 + mh * 128:c * 4096 + (mh + 1) * 128],
                                    xgf[:, c * 512:(c + 1) * 512],
                                    start=(c == 0), stop=(c == NC_ - 1)),
                                    reads=["gfc", f"xgf:{c}"],
                                    writes=[f"gph:{mh % 2}"],
                                    inc=(c == NC_ - 1))
                            sb.op("act", lambda e, mh=mh, pp=pp: e.activation(
                                rt_t[:], pp[:], AF.Relu),
                                reads=[f"gph:{mh % 2}"], writes=["grt"])
                            sb.op("dve", lambda e, mh=mh: e.tensor_mul(
                                hb[:, mh * 512:(mh + 1) * 512], rt_t[:], rt_t[:]),
                                reads=["grt"], writes=[f"ghb:{mh}"])
                        for dh in range(2):
                            dst = t["g1a_d"] if dh == 0 else t["g1b_d"]
                            for ti in range(4):
                                jj = blk * 4 + ti
                                pp = po[2 * dh + ti % 2]
                                for hc in range(32):
                                    sb.op("pe", lambda e, hc=hc, ti=ti, dh=dh, pp=pp: e.matmul(
                                        pp[:],
                                        hb[:, hc * 512 + ti * 128:hc * 512 + (ti + 1) * 128],
                                        pr_[:, dh * 16384 + hc * 512:dh * 16384 + (hc + 1) * 512],
                                        start=(hc == 0), stop=(hc == 31)),
                                        reads=[f"ghb:{hc}", f"gprj:{dh}"],
                                        writes=[f"gpo:{2 * dh + ti % 2}"],
                                        inc=(hc == 31))
                                gb = gtmp[2 * dh + ti % 2]
                                sb.op("dve", lambda e, gb=gb, pp=pp: e.tensor_copy(
                                    gb[:], pp[:]),
                                    reads=[f"gpo:{2 * dh + ti % 2}"],
                                    writes=[f"ggt:{2 * dh + ti % 2}"])
                                sb.op("pool", lambda e, jj=jj, dh=dh, gb=gb, dst=dst: e.indirect_dma_start(
                                    dst[0:128, :], IoA(idxj[:, jj:jj + 1], 0),
                                    gb[:], None),
                                    reads=[f"ggt:{2 * dh + ti % 2}", "idxj"],
                                    writes=[f"g1sc:{dh}"], slot=f"gm{2 * dh + ti % 2}")
    sb.barrier()


def emit_s7(p, b):
    nc, sb, t = p.nc, p.sb, p.t
    sb.barrier()
    with ExitStack() as st:
        gfm = st.enter_context(nc.sbuf_tensor(f"o_g{b}", [128, NC_ * S], F32))
        gt = st.enter_context(nc.sbuf_tensor(f"o_gt{b}", [128, 1024], F32))
        gs = st.enter_context(nc.sbuf_tensor(f"o_gs{b}", [128, 1024], BF16))
        idn = st.enter_context(nc.sbuf_tensor(f"o_idn{b}", [128, 128], BF16))
        xs_ = [st.enter_context(nc.sbuf_tensor(f"o_xs{b}{i}", [128, S], F32))
               for i in range(2)]
        ss_ = [st.enter_context(nc.sbuf_tensor(f"o_ss{b}{i}", [128, S], F32))
               for i in range(2)]
        ov = st.enter_context(nc.sbuf_tensor(f"o_ov{b}", [128, S], F32))
        t12 = st.enter_context(nc.sbuf_tensor(f"o_t12{b}", [128, S], F32))
        sb.op("sp", lambda e: e.dma_start(out=idn[:], in_=t["idn"]),
              writes=["idn"], slot="bia")
        with nc.psum_tensor(f"o_pt{b}", [128, 2048], BF16) as pt:
            for tloc in range(16):
                tt = b * 16 + tloc
                sb.op("sp", lambda e, tt=tt: e.dma_start(
                    out=gt[:, 0:512], in_=t["g1a_d"][tt * 128:(tt + 1) * 128, :]),
                    reads=["g1sc:0"], writes=["gt"], slot="ld0")
                sb.op("sp", lambda e, tt=tt: e.dma_start(
                    out=gt[:, 512:1024], in_=t["g1b_d"][tt * 128:(tt + 1) * 128, :]),
                    reads=["g1sc:1"], writes=["gt"], slot="ld6")
                sb.op("act", lambda e: e.activation(gs[:], gt[:], AF.Silu),
                      reads=["gt"], writes=["gs"])
                for c in range(NC_):
                    pb = c % 2
                    sb.op("pe", lambda e, c=c, pb=pb: e.transpose(
                        pt[:, pb * 1024:pb * 1024 + 128],
                        gs[:, c * 128:(c + 1) * 128], idn[:]),
                        reads=["gs", "idn"], writes=[f"ptr:{pb}"])
                    sb.op("dve", lambda e, c=c, tloc=tloc, pb=pb: e.tensor_copy(
                        gfm[:, c * S + tloc * 128:c * S + (tloc + 1) * 128],
                        pt[:, pb * 1024:pb * 1024 + 128]),
                        reads=[f"ptr:{pb}"], writes=[f"gfm:{c}"])
        for c in range(NC_):
            xb = xs_[c % 2]
            sbf = ss_[c % 2]
            sb.op("sp", lambda e, c=c, xb=xb: e.dma_start(
                out=xb[:], in_=t["x1_d"][b, c * 128:(c + 1) * 128, :]),
                reads=[f"x1_d:{b}:{c}"], writes=[f"oxs:{c % 2}"],
                slot=f"ld{1 + c % 2}")
            sb.op("sp", lambda e, c=c, sbf=sbf: e.dma_start(
                out=sbf[:], in_=t["s1_d"][b, c * 128:(c + 1) * 128, :]),
                reads=[f"s1_d:{b}:{c}"], writes=[f"oss:{c % 2}"],
                slot=f"ld{3 + c % 2}")
            sb.op("dve", lambda e, c=c, sbf=sbf: e.tensor_mul(
                t12[:], sbf[:], gfm[:, c * S:(c + 1) * S]),
                reads=[f"oss:{c % 2}", f"gfm:{c}"], writes=["t12"])
            sb.op("dve", lambda e, c=c, xb=xb: e.tensor_add(ov[:], xb[:], t12[:]),
                  reads=[f"oxs:{c % 2}", "t12"], writes=["ov"])
            sb.op("sp", lambda e, c=c: e.dma_start(
                out=t["yout"][b, c * 128:(c + 1) * 128, :], in_=ov[:]),
                reads=["ov"], writes=[f"yout:{b}:{c}"], slot="st0")
    sb.barrier()


def build_nc(stages=("s0",), dump=()):
    nc = bass.Bass("TRN2", target_bir_lowering=False)
    t = {}
    for name, shape, dt in _IN_SPECS:
        t[name] = nc.dram_tensor(name, shape, dt, kind="ExternalInput").ap()
    for name, shape, dt in _SCRATCH:
        kind = "ExternalOutput" if name in dump else "Internal"
        t[name] = nc.dram_tensor(name, shape, dt, kind=kind).ap()
    t["yout"] = nc.dram_tensor(
        "yout", [B_PER_CORE, 1024, 2048], F32, kind="ExternalOutput").ap()

    with ExitStack() as stack:
        sb = SB(nc, stack)
        p = P(nc, sb, t)
        p.cst = stack.enter_context(nc.sbuf_tensor("csts", [128, 8], F32))
        sb.op("dve", lambda e: e.memset(p.cst[:, 0:1], float(LN_EPS)), writes=["cst"])
        sb.op("dve", lambda e: e.memset(p.cst[:, 1:2], float(RMS_EPS)), writes=["cst"])
        p.dump = set(dump)
        if "s0" in stages:
            for b in range(B_PER_CORE):
                emit_s0(p, b)
        if "s12" in stages:
            for b in range(B_PER_CORE):
                emit_s12(p, b)
        if "s3" in stages:
            emit_s3(p)
        if "s4" in stages:
            for b in range(B_PER_CORE):
                emit_s4(p, b)
        if "s5" in stages:
            for b in range(B_PER_CORE):
                emit_s5(p, b)
        if "s6" in stages:
            emit_s6(p)
        if "s7" in stages:
            for b in range(B_PER_CORE):
                emit_s7(p, b)
        sb.finalize()
        import os
        if os.environ.get("MK_DEBUG_COUNT"):
            print("instr counts:", {e: len(v) for e, v in sb.prog.items()})

        def run_prog(lst):
            def body(e):
                for it in lst:
                    if isinstance(it, dict):
                        ins = it["fn"](e)
                        if it["inc"] is not None:
                            h, n = it["inc"]
                            ins.then_inc(h, n)
                    else:
                        it(e)
            return body

        with nc.Block() as blk:
            blk.sync(run_prog(sb.prog["sp"]))
            blk.gpsimd(run_prog(sb.prog["pool"]))
            blk.tensor(run_prog(sb.prog["pe"]))
            blk.vector(run_prog(sb.prog["dve"]))
            blk.scalar(run_prog(sb.prog["act"]))
    return nc


def kernel(**inputs):
    H = _prep(inputs)
    x_fm = H.pop("_x_fm")
    nc = build_nc(stages=("s0", "s12", "s3", "s4", "s5", "s6", "s7"))
    in_maps = []
    for c in range(8):
        m = dict(H)
        m["x_fm"] = np.ascontiguousarray(x_fm[2 * c:2 * c + 2])
        in_maps.append(m)
    res = run_bass_kernel_spmd(nc, in_maps, core_ids=list(range(8)))
    out = np.concatenate([r["yout"] for r in res.results], axis=0)
    return np.ascontiguousarray(out.transpose(0, 2, 1))


# version 06 (final)
# speedup vs baseline: 1.3316x; 1.1416x over previous
import numpy as np
import ml_dtypes
from contextlib import ExitStack

import concourse.bass as bass
import concourse.mybir as mybir
from concourse.bass_utils import run_bass_kernel_spmd
from concourse.bass import IndirectOffsetOnAxis as IoA

F32 = mybir.dt.float32
BF16 = mybir.dt.bfloat16
I32 = mybir.dt.int32
AF = mybir.ActivationFunctionType
ALU = mybir.AluOpType
AX = mybir.AxisListType
BFNP = ml_dtypes.bfloat16

B_PER_CORE = 2
S = 2048
D = 1024
NC_ = 8
NT = S // 512
LN_EPS = np.float32(1e-5)
RMS_EPS = np.float32(1.1920929e-07)


class SB:

    ENG = ("pe", "dve", "act", "pool", "sp")

    def __init__(self, nc, stack):
        self.nc = nc
        self.stack = stack
        self.prog = {e: [] for e in self.ENG}
        self.sem = {}
        self.cnt = {}
        for e in self.ENG:
            self.sem[e] = stack.enter_context(nc.semaphore(f"S{e}"))
            self.cnt[e] = 0
        self.slot_of = {}
        self.seen = {e: {} for e in self.ENG}
        self.writer = {}
        self.readers = {}
        self.pending = {e: [] for e in self.ENG}
        self.bar_snapshot = {}

    def _slot_sem(self, slot):
        if slot not in self.slot_of:
            name = f"d_{slot}"
            assert len(self.slot_of) < 96, "too many DMA slots"
            self.sem[name] = self.stack.enter_context(self.nc.semaphore(name))
            self.cnt[name] = 0
            self.slot_of[slot] = name
        return self.slot_of[slot]

    def _force_inc(self, eng):
        ent = None
        for it in reversed(self.prog[eng]):
            if isinstance(it, dict) and it["inc"] is None:
                ent = it
                break
        assert ent is not None, "no quiet op to inc"
        self.cnt[eng] += 1
        c = self.cnt[eng]
        ent["inc"] = (self.sem[eng], 1)
        for buf, kind in self.pending[eng]:
            if kind == "w" and self.writer.get(buf) == ("pend", eng):
                self.writer[buf] = ("done", eng, c)
            if kind == "r":
                rd = self.readers.setdefault(buf, {})
                if rd.get(("pend", eng)):
                    del rd[("pend", eng)]
                rd[eng] = c
        self.pending[eng] = []
        return c

    def _need(self, eng, deps, rec, consumer_is_dma):
        if rec[0] == "done":
            _, sem, c = rec
            deps[sem] = max(deps.get(sem, 0), c)
        else:
            peng = rec[1]
            if peng == eng and not consumer_is_dma:
                return
            c = self._force_inc(peng)
            deps[peng] = max(deps.get(peng, 0), c)

    def op(self, eng, fn, reads=(), writes=(), slot=None, inc=True):
        dma = slot is not None
        deps = {}
        for r in reads:
            w = self.writer.get(r)
            if w:
                self._need(eng, deps, w, dma)
        for wbuf in writes:
            w = self.writer.get(wbuf)
            if w:
                self._need(eng, deps, w, dma)
            for key, c in list(self.readers.get(wbuf, {}).items()):
                if key == ("pend",):
                    continue
                if isinstance(key, tuple):
                    peng = key[1]
                    if peng == eng and not dma:
                        continue
                    cc = self._force_inc(peng)
                    deps[peng] = max(deps.get(peng, 0), cc)
                else:
                    deps[key] = max(deps.get(key, 0), c)
        for sem, c in sorted(deps.items()):
            if self.seen[eng].get(sem, 0) >= c:
                continue
            h = self.sem[sem]
            self.prog[eng].append(lambda e, h=h, c=c: e.wait_ge(h, c))
            self.seen[eng][sem] = c
        if dma:
            semname = self._slot_sem(slot)
            self.cnt[semname] += 16
            c = self.cnt[semname]
            self.prog[eng].append({"fn": fn, "inc": (self.sem[semname], 16)})
            for wbuf in writes:
                self.writer[wbuf] = ("done", semname, c)
                self.readers[wbuf] = {}
            for r in reads:
                self.readers.setdefault(r, {})[semname] = c
        elif inc:
            self.cnt[eng] += 1
            c = self.cnt[eng]
            self.prog[eng].append({"fn": fn, "inc": (self.sem[eng], 1)})
            for buf, kind in self.pending[eng]:
                if kind == "w" and self.writer.get(buf) == ("pend", eng):
                    self.writer[buf] = ("done", eng, c)
                if kind == "r":
                    rd = self.readers.setdefault(buf, {})
                    if ("pend", eng) in rd:
                        del rd[("pend", eng)]
                    rd[eng] = c
            self.pending[eng] = []
            for wbuf in writes:
                self.writer[wbuf] = ("done", eng, c)
                self.readers[wbuf] = {}
            for r in reads:
                self.readers.setdefault(r, {})[eng] = c
        else:
            self.prog[eng].append({"fn": fn, "inc": None})
            for wbuf in writes:
                self.writer[wbuf] = ("pend", eng)
                self.readers[wbuf] = {}
                self.pending[eng].append((wbuf, "w"))
            for r in reads:
                self.readers.setdefault(r, {})[("pend", eng)] = True
                self.pending[eng].append((r, "r"))

    def barrier(self):
        for e in self.ENG:
            if self.pending[e]:
                self._force_inc(e)
        snap = dict(self.cnt)
        for e in self.ENG:
            for sem, c in snap.items():
                if c == 0 or sem == e:
                    continue
                if self.bar_snapshot.get((e, sem), -1) >= c:
                    continue
                if self.seen[e].get(sem, 0) >= c:
                    self.bar_snapshot[(e, sem)] = c
                    continue
                h = self.sem[sem]
                self.prog[e].append(lambda en, h=h, c=c: en.wait_ge(h, c))
                self.seen[e][sem] = c
                self.bar_snapshot[(e, sem)] = c

    def finalize(self):
        for e in self.ENG:
            if self.pending[e]:
                self._force_inc(e)
        self.barrier()


def _prep(ins):
    f = np.float32
    H = {}
    x = ins["L0_attn_x"]
    H["_x_fm"] = np.ascontiguousarray(x.transpose(0, 2, 1))

    perm = np.array([[[h * 192 + d * 3 + s for d in range(64)] for h in range(16)]
                     for s in range(3)]).reshape(-1)
    WT = np.ascontiguousarray(ins["L0_attn_W_qkv"][perm].T)
    bq = ins["L0_attn_b_qkv"][perm]
    H["wqk"] = np.ascontiguousarray(
        WT[:, :2048].reshape(8, 128, 16, 128).transpose(2, 0, 1, 3))
    H["wv"] = np.ascontiguousarray(WT[:, 2048:].reshape(8, 128, 1024))
    H["bqk"] = np.ascontiguousarray(bq[:2048].reshape(16, 128).T)
    H["bv"] = np.ascontiguousarray(bq[2048:].reshape(1, 1024))
    WoT = np.ascontiguousarray(ins["L0_attn_W_o"].T)
    H["wo"] = np.ascontiguousarray(
        WoT.reshape(8, 128, 8, 128).transpose(2, 0, 1, 3))
    H["bo"] = np.ascontiguousarray(ins["L0_attn_b_o"].reshape(8, 128).T)

    H["rt0"] = np.ascontiguousarray(ins["L0_ffn_router_w"].T.reshape(8, 128, 8))
    H["eb0"] = np.ascontiguousarray(ins["L0_ffn_expert_bias"].reshape(1, 8))
    w1 = np.concatenate([ins["L0_ffn_sw1"], ins["L0_ffn_w1"]], 0)
    w3 = np.concatenate([ins["L0_ffn_sw3"], ins["L0_ffn_w3"]], 0)
    w2 = np.concatenate([ins["L0_ffn_sw2"], ins["L0_ffn_w2"]], 0)
    H["w1s"] = np.ascontiguousarray(
        w1.reshape(10, 2048, 8, 128).transpose(0, 3, 2, 1).reshape(10, 128, 16384))
    H["w3s"] = np.ascontiguousarray(
        w3.reshape(10, 2048, 8, 128).transpose(0, 3, 2, 1).reshape(10, 128, 16384))
    H["w2s"] = np.ascontiguousarray(
        w2.reshape(10, 1024, 16, 128).transpose(0, 3, 2, 1).reshape(10, 128, 16384))

    H["ln3g"] = np.ascontiguousarray(ins["L1_attn_ln_g"].reshape(8, 128).T)
    H["ln3b"] = np.ascontiguousarray(ins["L1_attn_ln_b"].reshape(8, 128).T)
    H["taps"] = np.ascontiguousarray(
        ins["L1_attn_mix_w"][:, 0, :].reshape(8, 128, 3).transpose(1, 0, 2).reshape(128, 24))
    H["mixb"] = np.ascontiguousarray(ins["L1_attn_mix_b"].reshape(8, 128).T)

    def blk_io(W, mo, co):
        i, o = W.shape
        return np.ascontiguousarray(
            W.reshape(co, 128, mo, o // mo).transpose(2, 0, 1, 3))

    H["wi"] = blk_io(ins["L1_attn_Wi"], 8, 8)
    H["bi"] = np.ascontiguousarray(ins["L1_attn_bi"].reshape(8, 128).T)
    H["wg"] = blk_io(ins["L1_attn_Wg"], 8, 8)
    H["bg"] = np.ascontiguousarray(ins["L1_attn_bg"].reshape(8, 128).T)
    H["wout"] = blk_io(ins["L1_attn_Wout"], 8, 8)
    H["bout"] = np.ascontiguousarray(ins["L1_attn_bout"].reshape(8, 128).T)
    H["wsin"] = blk_io(ins["L1_attn_Wsin"], 2, 8)
    H["bsin"] = np.ascontiguousarray(ins["L1_attn_bsin"].reshape(2, 128).T)
    H["wd"] = blk_io(ins["L1_attn_Wd"], 2, 8)
    H["bd"] = np.ascontiguousarray(ins["L1_attn_bd"].reshape(2, 128).T)
    H["wsel"] = blk_io(ins["L1_attn_Wsel"], 2, 8)
    H["bsel"] = np.ascontiguousarray(ins["L1_attn_bsel"].reshape(2, 128).T)
    H["wso"] = blk_io(ins["L1_attn_Wso"], 8, 2)
    H["bso"] = np.ascontiguousarray(ins["L1_attn_bso"].reshape(8, 128).T)

    H["rt1"] = np.ascontiguousarray(ins["L1_ffn_router_w"].T.reshape(8, 128, 8))
    H["fcs"] = np.ascontiguousarray(
        ins["L1_ffn_expert_fc"].reshape(8, 8, 128, 4096))
    prT = ins["L1_ffn_expert_proj"].transpose(0, 2, 1)
    H["prs"] = np.ascontiguousarray(
        prT.reshape(8, 32, 128, 2, 512).transpose(0, 3, 1, 2, 4))

    H["idn"] = np.eye(128, dtype=BFNP)
    H["idnf"] = np.eye(128, dtype=np.float32)
    H["tids"] = np.arange(B_PER_CORE * S, dtype=np.int32).reshape(32, 128).T.copy()
    H["ecap1"] = np.broadcast_to(
        (np.arange(8, dtype=np.float32) * 1024.0)[None, :], (128, 8)).copy()
    H["ecap0"] = np.broadcast_to(
        (np.arange(8, dtype=np.float32) * 1280.0)[None, :], (128, 8)).copy()
    H["prange"] = np.arange(128, dtype=np.float32).reshape(128, 1).copy()
    H["ltri8"] = np.triu(np.ones((8, 8), np.float32), k=1)
    masks = np.zeros((4, 128, 512), np.float32)
    for oi, o in enumerate((0, 128, 256, 384)):
        for kk in range(128):
            masks[oi, kk, o + kk:] = 1.0
    H["masks"] = masks.astype(BFNP)
    return H


_IN_SPECS = [
    ("wqk", [16, 8, 128, 128], F32), ("wv", [8, 128, 1024], F32),
    ("bqk", [128, 16], F32), ("bv", [1, 1024], F32),
    ("wo", [8, 8, 128, 128], F32), ("bo", [128, 8], F32),
    ("rt0", [8, 128, 8], F32), ("eb0", [1, 8], F32),
    ("w1s", [10, 128, 16384], F32), ("w3s", [10, 128, 16384], F32),
    ("w2s", [10, 128, 16384], F32),
    ("ln3g", [128, 8], F32), ("ln3b", [128, 8], F32),
    ("taps", [128, 24], F32), ("mixb", [128, 8], F32),
    ("wi", [8, 8, 128, 128], F32), ("bi", [128, 8], F32),
    ("wg", [8, 8, 128, 128], F32), ("bg", [128, 8], F32),
    ("wout", [8, 8, 128, 128], F32), ("bout", [128, 8], F32),
    ("wsin", [2, 8, 128, 128], F32), ("bsin", [128, 2], F32),
    ("wd", [2, 8, 128, 128], F32), ("bd", [128, 2], F32),
    ("wsel", [2, 8, 128, 128], F32), ("bsel", [128, 2], F32),
    ("wso", [8, 2, 128, 128], F32), ("bso", [128, 8], F32),
    ("rt1", [8, 128, 8], F32),
    ("fcs", [8, 8, 128, 4096], F32), ("prs", [8, 2, 32, 128, 512], F32),
    ("idn", [128, 128], BF16), ("idnf", [128, 128], F32),
    ("tids", [128, 32], I32), ("ecap1", [128, 8], F32),
    ("ecap0", [128, 8], F32), ("prange", [128, 1], F32),
    ("ltri8", [8, 8], F32),
    ("masks", [4, 128, 512], BF16),
    ("x_fm", [B_PER_CORE, 1024, 2048], F32),
]

_SCRATCH = [
    ("qdbg", [B_PER_CORE, 1024, 2048], BF16),
    ("kdbg", [B_PER_CORE, 1024, 2048], BF16),
    ("vdbg", [B_PER_CORE, 2048, 1024], BF16),
    ("zdbg", [B_PER_CORE, 1024, 2048], BF16),
    ("xnbf_d", [B_PER_CORE, 1024, 2048], BF16),
    ("rnbf_d", [B_PER_CORE, 1024, 2048], BF16),
    ("s_d", [B_PER_CORE, 1024, 2048], F32),
    ("gmoe_d", [B_PER_CORE * S, 1024], F32),
    ("rout_d", [24 * 512 + 128, 1024], F32),
    ("xntm_d", [B_PER_CORE * S + 1, 1024], BF16),
    ("idx0_d", [24 * 512 + 128, 1], I32),
    ("slab_d", [B_PER_CORE * S, 2], I32),
    ("cwab_d", [B_PER_CORE * S, 2], F32),
    ("x1_d", [B_PER_CORE, 1024, 2048], F32),
    ("xn1_d", [B_PER_CORE, 1024, 2048], F32),
    ("xn1bf_d", [B_PER_CORE, 1024, 2048], BF16),
    ("s1_d", [B_PER_CORE, 1024, 2048], F32),
    ("g1a_d", [B_PER_CORE * S + 1, 512], F32),
    ("g1b_d", [B_PER_CORE * S + 1, 512], F32),
    ("xn1tm_d", [B_PER_CORE * S + 1, 1024], BF16),
    ("idx1_d", [8 * 1024 + 128, 1], I32),
]


class P:

    def __init__(self, nc, sb, t):
        self.nc = nc
        self.sb = sb
        self.t = t
        self.cst = None
        self.dump = set()


def regs(name, *idx_ranges):
    out = [name]
    for r in idx_ranges:
        out = [f"{o}:{i}" for o in out for i in r]
    return out


def emit_ln_rows(p, pfx, p_s1, p_s2, rows, eps_ap):
    sb = p.sb
    sb.op("act", lambda e: e.activation(
        rows[:, 0:S], p_s1[:], AF.Identity, scale=1.0 / D),
        reads=[f"{pfx}s1:{i}" for i in range(4)], writes=[f"{pfx}mu"])
    sb.op("act", lambda e: e.activation(
        rows[:, 2 * S:3 * S], p_s2[:], AF.Identity, scale=1.0 / D),
        reads=[f"{pfx}s2:{i}" for i in range(4)], writes=[f"{pfx}m2"])
    sb.op("dve", lambda e: e.tensor_mul(
        rows[:, S:2 * S], rows[:, 0:S], rows[:, 0:S]),
        reads=[f"{pfx}mu"], writes=[f"{pfx}rs"])
    sb.op("dve", lambda e: e.tensor_sub(
        rows[:, 2 * S:3 * S], rows[:, 2 * S:3 * S], rows[:, S:2 * S]),
        reads=[f"{pfx}m2", f"{pfx}rs"], writes=[f"{pfx}m2"])
    sb.op("act", lambda e: e.activation(
        rows[:, S:2 * S], rows[:, 2 * S:3 * S], AF.Sqrt, bias=eps_ap),
        reads=[f"{pfx}m2", "cst"], writes=[f"{pfx}rs"])
    sb.op("dve", lambda e: e.reciprocal(rows[:, S:2 * S], rows[:, S:2 * S]),
          reads=[f"{pfx}rs"], writes=[f"{pfx}rs"])


def emit_s0(p, b):
    nc, sb, t = p.nc, p.sb, p.t
    with ExitStack() as st:
        x_sb = st.enter_context(nc.sbuf_tensor(f"s0x{b}", [128, NC_ * S], F32))
        xn_sb = st.enter_context(nc.sbuf_tensor(f"s0xn{b}", [128, NC_ * S], F32))
        cb = st.enter_context(nc.sbuf_tensor(f"s0cb{b}", [128, S], BF16))
        sq = st.enter_context(nc.sbuf_tensor(f"s0sq{b}", [128, S], BF16))
        ones = st.enter_context(nc.sbuf_tensor(f"s0on{b}", [128, 1], BF16))
        onesr = st.enter_context(nc.sbuf_tensor(f"s0onr{b}", [1, 128], F32))
        rows = st.enter_context(nc.sbuf_tensor(f"s0rw{b}", [1, 4 * S], F32))
        sb.op("dve", lambda e: e.memset(ones[:], 1.0), writes=["ones"])
        sb.op("dve", lambda e: e.memset(onesr[:], 1.0), writes=["onesr"])
        for c in range(NC_):
            sb.op("sp", lambda e, c=c: e.dma_start(
                out=x_sb[:, c * S:(c + 1) * S],
                in_=t["x_fm"][b, c * 128:(c + 1) * 128, :]),
                writes=[f"x_sb:{c}"], slot=f"ld{c}")
        with (
            nc.psum_tensor(f"p0a{b}", [1, S], F32) as p_s1,
            nc.psum_tensor(f"p0b{b}", [1, S], F32) as p_s2,
        ):
            for c in range(NC_):
                xs = x_sb[:, c * S:(c + 1) * S]
                sb.op("pool", lambda e, xs=xs: e.tensor_copy(cb[:], xs),
                      reads=[f"x_sb:{c}"], writes=["cb"])
                sb.op("act", lambda e, xs=xs: e.activation(sq[:], xs, AF.Square),
                      reads=[f"x_sb:{c}"], writes=["sq"])
                for tc in range(4):
                    sl = slice(tc * 512, (tc + 1) * 512)
                    sb.op("pe", lambda e, sl=sl, c=c: e.matmul(
                        p_s1[:, sl], ones[:], cb[:, sl],
                        start=(c == 0), stop=(c == NC_ - 1)),
                        reads=["cb", "ones"], writes=[f"As1:{tc}"])
                    sb.op("pe", lambda e, sl=sl, c=c: e.matmul(
                        p_s2[:, sl], ones[:], sq[:, sl],
                        start=(c == 0), stop=(c == NC_ - 1)),
                        reads=["sq", "ones"], writes=[f"As2:{tc}"])
            emit_ln_rows(p, "A", p_s1, p_s2, rows, p.cst[0:1, 0:1])
        with nc.psum_tensor(f"p0c{b}", [128, 1024], F32) as p_b:
            for tc in range(4):
                sl = slice(tc * 512, (tc + 1) * 512)
                sb.op("pe", lambda e, sl=sl: e.matmul(
                    p_b[:, 0:512], onesr[:], rows[0:1, 0:S][:, sl],
                    start=True, stop=True),
                    reads=["Amu", "onesr"],
                    writes=["Bmu"] + ([f"As1:{i}" for i in range(4)]
                                      + [f"As2:{i}" for i in range(4)] if tc == 0 else []))
                sb.op("pe", lambda e, sl=sl: e.matmul(
                    p_b[:, 512:1024], onesr[:], rows[0:1, S:2 * S][:, sl],
                    start=True, stop=True),
                    reads=["Ars", "onesr"], writes=["Brs"])
                for c in range(NC_):
                    xs = x_sb[:, c * S + tc * 512:c * S + (tc + 1) * 512]
                    xn = xn_sb[:, c * S + tc * 512:c * S + (tc + 1) * 512]
                    sb.op("dve", lambda e, xs=xs, xn=xn: e.tensor_sub(
                        xn, xs, p_b[:, 0:512]),
                        reads=[f"x_sb:{c}", "Bmu"], writes=[f"xn:{c}:{tc}"])
                    sb.op("dve", lambda e, xn=xn: e.tensor_mul(xn, xn, p_b[:, 512:1024]),
                        reads=[f"xn:{c}:{tc}", "Brs"], writes=[f"xn:{c}:{tc}"])
        with nc.psum_tensor(f"p0d{b}", [1, S], F32) as p_rms:
            for c in range(NC_):
                xn = xn_sb[:, c * S:(c + 1) * S]
                rd = [f"xn:{c}:{tc}" for tc in range(4)]
                sb.op("pool", lambda e, xn=xn: e.tensor_copy(cb[:], xn),
                      reads=rd, writes=["cb"])
                sb.op("pool", lambda e, c=c: e.dma_start(
                    out=t["xnbf_d"][b, c * 128:(c + 1) * 128, :], in_=cb[:]),
                    reads=["cb"], writes=[f"xnbf_d:{b}:{c}"], slot="st0")
                sb.op("act", lambda e, xn=xn: e.activation(sq[:], xn, AF.Square),
                      reads=rd + (["Bmu", "Brs"] if c == 0 else []), writes=["sq"])
                for tc in range(4):
                    sl = slice(tc * 512, (tc + 1) * 512)
                    sb.op("pe", lambda e, sl=sl, c=c: e.matmul(
                        p_rms[:, sl], ones[:], sq[:, sl],
                        start=(c == 0), stop=(c == NC_ - 1)),
                        reads=["sq", "ones"],
                        writes=[f"Rs:{tc}"] + (["Bmu", "Brs"] if c == 0 and tc == 0 else []))
            sb.op("act", lambda e: e.activation(
                rows[:, 3 * S:4 * S], p_rms[:], AF.Sqrt, scale=1.0 / D,
                bias=p.cst[0:1, 1:2]),
                reads=[f"Rs:{i}" for i in range(4)] + ["cst"], writes=["rmsr"])
            sb.op("dve", lambda e: e.reciprocal(
                rows[:, 3 * S:4 * S], rows[:, 3 * S:4 * S]),
                reads=["rmsr"], writes=["rmsr"])
        with nc.psum_tensor(f"p0e{b}", [128, S], F32) as p_c:
            for tc in range(4):
                sl = slice(tc * 512, (tc + 1) * 512)
                sb.op("pe", lambda e, sl=sl: e.matmul(
                    p_c[:, sl], onesr[:], rows[0:1, 3 * S:4 * S][:, sl],
                    start=True, stop=True),
                    reads=["rmsr", "onesr"],
                    writes=["Crm"] + ([f"Rs:{i}" for i in range(4)] if tc == 0 else []))
            for c in range(NC_):
                xn = xn_sb[:, c * S:(c + 1) * S]
                rd = [f"xn:{c}:{tc}" for tc in range(4)]
                sb.op("dve", lambda e, xn=xn: e.tensor_mul(cb[:], xn, p_c[:]),
                      reads=rd + ["Crm"], writes=["cb"])
                sb.op("sp", lambda e, c=c: e.dma_start(
                    out=t["rnbf_d"][b, c * 128:(c + 1) * 128, :], in_=cb[:]),
                    reads=["cb"], writes=[f"rnbf_d:{b}:{c}"], slot="st1")
    p.sb.barrier()


def emit_s12(p, b):
    nc, sb, t = p.nc, p.sb, p.t
    sb.barrier()
    with ExitStack() as st:
        rn = st.enter_context(nc.sbuf_tensor(f"a_rn{b}", [128, NC_ * S], BF16))
        q_sb = st.enter_context(nc.sbuf_tensor(f"a_q{b}", [128, NC_ * S], BF16))
        k_sb = st.enter_context(nc.sbuf_tensor(f"a_k{b}", [128, NC_ * S], BF16))
        v_sb = st.enter_context(nc.sbuf_tensor(f"a_v{b}", [128, 16 * 1024], BF16))
        z_sb = st.enter_context(nc.sbuf_tensor(f"a_z{b}", [128, NC_ * S], BF16))
        wt = [st.enter_context(nc.sbuf_tensor(f"a_w{b}{i}", [128, NC_ * 128], BF16))
              for i in range(2)]
        wv_sb = st.enter_context(nc.sbuf_tensor(f"a_wv{b}", [128, NC_ * 1024], BF16))
        bqk = st.enter_context(nc.sbuf_tensor(f"a_bqk{b}", [128, 16], F32))
        bo_sb = st.enter_context(nc.sbuf_tensor(f"a_bo{b}", [128, 8], F32))
        bv_sb = st.enter_context(nc.sbuf_tensor(f"a_bv{b}", [1, 1024], BF16))
        onesb = st.enter_context(nc.sbuf_tensor(f"a_on{b}", [1, 128], BF16))
        msk = st.enter_context(nc.sbuf_tensor(f"a_msk{b}", [128, 4 * 512], BF16))
        e_bf = st.enter_context(nc.sbuf_tensor(f"a_e{b}", [128, S], BF16))
        etmp = st.enter_context(nc.sbuf_tensor(f"a_et{b}", [128, 512], BF16))
        vv = st.enter_context(nc.sbuf_tensor(f"a_vv{b}", [128, 64], BF16))
        zr = st.enter_context(nc.sbuf_tensor(f"a_zr{b}", [128, 16], F32))
        sv = st.enter_context(nc.sbuf_tensor(f"a_sv{b}", [128, S], F32))
        sb.op("dve", lambda e: e.memset(onesb[:], 1.0), writes=["onesb"])
        for c in range(NC_):
            sb.op("sp", lambda e, c=c: e.dma_start(
                out=rn[:, c * S:(c + 1) * S],
                in_=t["rnbf_d"][b, c * 128:(c + 1) * 128, :]),
                reads=[f"rnbf_d:{b}:{c}"], writes=[f"rn:{c}"], slot=f"ld{c}")
        sb.op("sp", lambda e: e.dma_start(out=bqk[:], in_=t["bqk"]),
              writes=["bqk"], slot="bia")
        sb.op("sp", lambda e: e.dma_start(out=bo_sb[:], in_=t["bo"]),
              writes=["bo"], slot="bia")
        sb.op("pool", lambda e: e.dma_start(out=bv_sb[:], in_=t["bv"]),
              writes=["bv"], slot="bia2")
        sb.op("sp", lambda e: e.dma_start(
            out=msk[:].rearrange("p (o n) -> p o n", o=4),
            in_=t["masks"].rearrange("o p n -> p o n")),
            writes=["msk"], slot="msk")
        sb.op("pool", lambda e: e.dma_start(
            out=wv_sb[:].rearrange("p (c n) -> p c n", c=NC_),
            in_=t["wv"].rearrange("c p n -> p c n")),
            writes=["wv_sb"], slot="wv")
        with ExitStack() as pst:
            pqk = [pst.enter_context(nc.psum_tensor(f"a_pq{b}{i}", [128, 512], F32))
                   for i in range(8)]
            for m in range(16):
                wb = m % 2
                sb.op("pool", lambda e, m=m, wb=wb: e.dma_start(
                    out=wt[wb][:].rearrange("p (c n) -> p c n", c=NC_),
                    in_=t["wqk"][m].rearrange("c p n -> p c n")),
                    writes=[f"wt{wb}"], slot=f"wqk{wb}")
                pbase = (m % 2) * 4
                for c in range(NC_):
                    for tc in range(4):
                        sb.op("pe", lambda e, m=m, c=c, tc=tc, wb=wb, pbase=pbase: e.matmul(
                            pqk[pbase + tc][:],
                            wt[wb][:, c * 128:(c + 1) * 128],
                            rn[:, c * S + tc * 512:c * S + (tc + 1) * 512],
                            start=(c == 0), stop=(c == NC_ - 1)),
                            reads=[f"wt{wb}", f"rn:{c}"],
                            writes=[f"pqk:{pbase + tc}"],
                            inc=(c == NC_ - 1))
                dst = q_sb if m < 8 else k_sb
                mm = m % 8
                for tc in range(4):
                    sb.op("act", lambda e, m=m, mm=mm, tc=tc, dst=dst, pbase=pbase: e.activation(
                        dst[:, mm * S + tc * 512:mm * S + (tc + 1) * 512],
                        pqk[pbase + tc][:], AF.Identity, bias=bqk[:, m:m + 1]),
                        reads=[f"pqk:{pbase + tc}", "bqk"],
                        writes=[f"qk:{m}:{tc}"])
        sb.barrier()
        with ExitStack() as pst:
            pv = [pst.enter_context(nc.psum_tensor(f"a_pv{b}{i}", [128, 1024], F32))
                  for i in range(2)]
            for tt in range(16):
                pb = tt % 2
                for c in range(NC_):
                    for half in range(2):
                        sb.op("pe", lambda e, tt=tt, c=c, half=half, pb=pb: e.matmul(
                            pv[pb][:, half * 512:(half + 1) * 512],
                            rn[:, c * S + tt * 128:c * S + (tt + 1) * 128],
                            wv_sb[:, c * 1024 + half * 512:c * 1024 + (half + 1) * 512],
                            start=(c == 0), stop=False),
                            reads=["wv_sb", f"rn:{c}"], writes=[f"pv:{pb}"],
                            inc=False)
                for half in range(2):
                    sb.op("pe", lambda e, half=half, pb=pb: e.matmul(
                        pv[pb][:, half * 512:(half + 1) * 512],
                        onesb[:], bv_sb[:, half * 512:(half + 1) * 512],
                        start=False, stop=True),
                        reads=["onesb", "bv"], writes=[f"pv:{pb}"],
                        inc=(half == 1))
                sb.op("dve", lambda e, tt=tt, pb=pb: e.tensor_copy(
                    v_sb[:, tt * 1024:(tt + 1) * 1024], pv[pb][:]),
                    reads=[f"pv:{pb}"], writes=[f"v:{tt}"])
        if "qdbg" in p.dump:
            sb.barrier()
            for c in range(NC_):
                for nm, buf in (("qdbg", q_sb), ("kdbg", k_sb)):
                    sb.op("sp", lambda e, c=c, nm=nm, buf=buf: e.dma_start(
                        out=t[nm][b, c * 128:(c + 1) * 128, :],
                        in_=buf[:, c * S:(c + 1) * S]),
                        reads=[], writes=[], slot="dbg")
            for tt in range(16):
                sb.op("sp", lambda e, tt=tt: e.dma_start(
                    out=t["vdbg"][b, tt * 128:(tt + 1) * 128, :],
                    in_=v_sb[:, tt * 1024:(tt + 1) * 1024]),
                    reads=[], writes=[], slot="dbg")
            sb.barrier()
        sb.barrier()
        with (
            nc.psum_tensor(f"a_ps{b}", [128, S], F32) as psc,
            nc.psum_tensor(f"a_pz{b}", [64, S], F32) as pz,
        ):
            for h in range(16):
                ch = h // 2
                r0 = (h % 2) * 64
                qh = q_sb[r0:r0 + 64, ch * S:(ch + 1) * S]
                kh = k_sb[r0:r0 + 64, ch * S:(ch + 1) * S]
                for kt in range(16):
                    c0 = kt // 4
                    for cc in range(c0, 4):
                        sb.op("pe", lambda e, kt=kt, cc=cc, qh=qh, kh=kh: e.matmul(
                            psc[:, cc * 512:(cc + 1) * 512],
                            kh[:, kt * 128:(kt + 1) * 128],
                            qh[:, cc * 512:(cc + 1) * 512],
                            start=True, stop=True),
                            reads=[f"qk:{ch}:{cc}", f"qk:{8 + ch}:{kt // 4}"],
                            writes=[f"psc:{cc}"])
                        sb.op("dve", lambda e, cc=cc: e.reduce_max(
                            zr[:, 8 + cc:9 + cc], psc[:, cc * 512:(cc + 1) * 512],
                            axis=AX.X, negate=True),
                            reads=[f"psc:{cc}"], writes=[f"nmx:{cc}"])
                    for cc in range(c0 + 1, 4):
                        src0 = 8 + c0 if cc == c0 + 1 else 15
                        sb.op("dve", lambda e, cc=cc, src0=src0: e.tensor_tensor(
                            zr[:, 15:16], zr[:, src0:src0 + 1], zr[:, 8 + cc:9 + cc],
                            ALU.min),
                            reads=[f"nmx:{cc}", f"nmx:{c0}" if cc == c0 + 1 else "nmf"],
                            writes=["nmf"])
                    if c0 == 3:
                        sb.op("dve", lambda e: e.tensor_copy(
                            zr[:, 15:16], zr[:, 11:12]),
                            reads=["nmx:3"], writes=["nmf"])
                    o = kt % 4
                    sb.op("act", lambda e, c0=c0: e.activation(
                        etmp[:], psc[:, c0 * 512:(c0 + 1) * 512], AF.Exp,
                        bias=zr[:, 15:16]),
                        reads=[f"psc:{c0}", "nmf"], writes=["etmp"])
                    sb.op("dve", lambda e, c0=c0, o=o: e.tensor_mul(
                        e_bf[:, c0 * 512:(c0 + 1) * 512], etmp[:],
                        msk[:, o * 512:(o + 1) * 512]),
                        reads=["etmp", "msk"], writes=[f"ee:{c0}"])
                    sb.op("dve", lambda e, c0=c0: e.reduce_sum(
                        zr[:, c0:c0 + 1], e_bf[:, c0 * 512:(c0 + 1) * 512], axis=AX.X),
                        reads=[f"ee:{c0}"], writes=[f"zs:{c0}"])
                    for cc in range(c0 + 1, 4):
                        sb.op("act", lambda e, cc=cc: e.activation(
                            e_bf[:, cc * 512:(cc + 1) * 512],
                            psc[:, cc * 512:(cc + 1) * 512], AF.Exp,
                            bias=zr[:, 15:16], accum_out=zr[:, cc:cc + 1]),
                            reads=[f"psc:{cc}", "nmf"],
                            writes=[f"ee:{cc}", f"zs:{cc}"])
                    for cc in range(c0 + 1, 4):
                        src0 = c0 if cc == c0 + 1 else 4
                        sb.op("dve", lambda e, cc=cc, src0=src0: e.tensor_add(
                            zr[:, 4:5], zr[:, src0:src0 + 1], zr[:, cc:cc + 1]),
                            reads=[f"zs:{cc}", f"zs:{c0}" if cc == c0 + 1 else "zt"],
                            writes=["zt"])
                    if c0 == 3:
                        sb.op("dve", lambda e: e.tensor_copy(zr[:, 4:5], zr[:, 3:4]),
                              reads=["zs:3"], writes=["zt"])
                    sb.op("dve", lambda e: e.reciprocal(zr[:, 5:6], zr[:, 4:5]),
                          reads=["zt"], writes=["zrec"])
                    sb.op("dve", lambda e, h=h, kt=kt: e.tensor_scalar(
                        vv[:], v_sb[:, kt * 1024 + h * 64:kt * 1024 + h * 64 + 64],
                        zr[:, 5:6], None, ALU.mult),
                        reads=[f"v:{kt}", "zrec"], writes=["vv"])
                    for cc in range(c0, 4):
                        sb.op("pe", lambda e, cc=cc, kt=kt: e.matmul(
                            pz[:, cc * 512:(cc + 1) * 512], vv[:],
                            e_bf[:, cc * 512:(cc + 1) * 512],
                            start=(kt == 0), stop=(kt == 4 * cc + 3)),
                            reads=["vv", f"ee:{cc}"], writes=[f"pz:{cc}"])
                sb.op("dve", lambda e, r0=r0, ch=ch: e.tensor_copy(
                    z_sb[r0:r0 + 64, ch * S:(ch + 1) * S], pz[:]),
                    reads=[f"pz:{cc}" for cc in range(4)], writes=[f"z:{ch}"])
        if "zdbg" in p.dump:
            sb.barrier()
            for c in range(NC_):
                sb.op("sp", lambda e, c=c: e.dma_start(
                    out=t["zdbg"][b, c * 128:(c + 1) * 128, :],
                    in_=z_sb[:, c * S:(c + 1) * S]),
                    reads=[], writes=[], slot="dbg")
            sb.barrier()
        sb.barrier()
        with ExitStack() as pst:
            po = [pst.enter_context(nc.psum_tensor(f"a_po{b}{i}", [128, 512], F32))
                  for i in range(8)]
            for m in range(8):
                wb = m % 2
                pbase = wb * 4
                sb.op("pool", lambda e, m=m, wb=wb: e.dma_start(
                    out=wt[wb][:].rearrange("p (c n) -> p c n", c=NC_),
                    in_=t["wo"][m].rearrange("c p n -> p c n")),
                    writes=[f"wt{wb}"], slot=f"wqk{wb}")
                for c in range(NC_):
                    for tc in range(4):
                        sb.op("pe", lambda e, c=c, tc=tc, wb=wb, pbase=pbase: e.matmul(
                            po[pbase + tc][:],
                            wt[wb][:, c * 128:(c + 1) * 128],
                            z_sb[:, c * S + tc * 512:c * S + (tc + 1) * 512],
                            start=(c == 0), stop=(c == NC_ - 1)),
                            reads=[f"wt{wb}", f"z:{c}"],
                            writes=[f"po:{pbase + tc}"],
                            inc=(c == NC_ - 1))
                for tc in range(4):
                    sb.op("act", lambda e, tc=tc, m=m, pbase=pbase: e.activation(
                        sv[:, tc * 512:(tc + 1) * 512], po[pbase + tc][:],
                        AF.Identity, bias=bo_sb[:, m:m + 1]),
                        reads=[f"po:{pbase + tc}", "bo"], writes=["sv"])
                sb.op("sp", lambda e, m=m: e.dma_start(
                    out=t["s_d"][b, m * 128:(m + 1) * 128, :], in_=sv[:]),
                    reads=["sv"], writes=[f"s_d:{b}:{m}"], slot="sst")
    sb.barrier()


def emit_s3(p):
    nc, sb, t = p.nc, p.sb, p.t
    sb.barrier()
    NTT = 32
    SG = 24
    BIG0 = SG * 512
    TROW = B_PER_CORE * S
    with ExitStack() as st:
        cw = st.enter_context(nc.sbuf_tensor("f_cw", [128, NTT * 10], F32))
        w1t = st.enter_context(nc.sbuf_tensor("f_w1", [128, NC_ * 2048], BF16))
        w3t = st.enter_context(nc.sbuf_tensor("f_w3", [128, NC_ * 2048], BF16))
        w2t = st.enter_context(nc.sbuf_tensor("f_w2", [128, 16 * 1024], BF16))
        hh = st.enter_context(nc.sbuf_tensor("f_hh", [128, 16 * 512], BF16))
        s1t = st.enter_context(nc.sbuf_tensor("f_s1t", [128, 512], BF16))
        onesb = st.enter_context(nc.sbuf_tensor("f_on", [1, 128], BF16))
        idn = st.enter_context(nc.sbuf_tensor("f_idn", [128, 128], BF16))
        idnf = st.enter_context(nc.sbuf_tensor("f_idnf", [128, 128], F32))
        slab_sb = st.enter_context(nc.sbuf_tensor("f_slab", [128, 2 * NTT], I32))
        cwab_sb = st.enter_context(nc.sbuf_tensor("f_cwab", [128, 2 * NTT], F32))
        wofsI = st.enter_context(nc.sbuf_tensor("f_wofs", [128, SG], I32))
        sb.op("dve", lambda e: e.memset(onesb[:], 1.0), writes=["onesb"])
        sb.op("dve", lambda e: e.memset(cw[:], 1.0), writes=["cwall"])
        sb.op("sp", lambda e: e.dma_start(out=idn[:], in_=t["idn"]),
              writes=["idn"], slot="bia")
        sb.op("sp", lambda e: e.dma_start(out=idnf[:], in_=t["idnf"]),
              writes=["idnf"], slot="bia")
        with ExitStack() as rst:
            xn = rst.enter_context(
                nc.sbuf_tensor("f_xn", [128, NC_ * 4096], BF16))
            for b in range(B_PER_CORE):
                for c in range(NC_):
                    sb.op("sp", lambda e, b=b, c=c: e.dma_start(
                        out=xn[:, c * 4096 + b * S:c * 4096 + (b + 1) * S],
                        in_=t["xnbf_d"][b, c * 128:(c + 1) * 128, :]),
                        reads=[f"xnbf_d:{b}:{c}"], writes=[f"xn:{c}"],
                        slot=f"ld{c}")
            rt = rst.enter_context(nc.sbuf_tensor("f_rt", [128, 64], BF16))
            eb = rst.enter_context(nc.sbuf_tensor("f_eb", [1, 8], BF16))
            lg = rst.enter_context(nc.sbuf_tensor("f_lg", [128, 8], F32))
            rga = rst.enter_context(nc.sbuf_tensor("f_rga", [128, 8], F32))
            rgb = rst.enter_context(nc.sbuf_tensor("f_rgb", [128, 8], F32))
            rr = rst.enter_context(nc.sbuf_tensor("f_rr", [128, 8], F32))
            with ExitStack() as bst:
                inda = bst.enter_context(
                    nc.sbuf_tensor("f_inda", [128, NTT * 8], F32))
                indt = bst.enter_context(
                    nc.sbuf_tensor("f_indt", [128, NTT * 8], F32))
                tids = bst.enter_context(nc.sbuf_tensor("f_tids", [128, 32], I32))
                prange = bst.enter_context(nc.sbuf_tensor("f_prg", [128, 1], F32))
                cnts = bst.enter_context(nc.sbuf_tensor("f_cnts", [8, 32], F32))
                csum = bst.enter_context(nc.sbuf_tensor("f_csum", [8, 32], F32))
                ones8 = bst.enter_context(nc.sbuf_tensor("f_on8", [8, 128], F32))
                rk8 = bst.enter_context(nc.sbuf_tensor("f_rk8", [8, 128], F32))
                padw = bst.enter_context(nc.sbuf_tensor("f_pdw", [8, 128], F32))
                ltri8 = bst.enter_context(nc.sbuf_tensor("f_ltr", [8, 8], F32))
                lds8 = bst.enter_context(nc.sbuf_tensor("f_lds", [8, 2], F32))
                pdc = bst.enter_context(nc.sbuf_tensor("f_pdc", [8, 8], F32))
                pbS = bst.enter_context(nc.sbuf_tensor("f_pbS", [128, 8], F32))
                egsf = bst.enter_context(nc.sbuf_tensor("f_egs", [128, SG], F32))
                base8 = bst.enter_context(nc.sbuf_tensor("f_b8", [128, 8], F32))
                tmp8 = bst.enter_context(nc.sbuf_tensor("f_t8", [128, 8], F32))
                slf = bst.enter_context(nc.sbuf_tensor("f_slf", [128, 1], F32))
                slia = bst.enter_context(nc.sbuf_tensor("f_slia", [128, 1], I32))
                slib = bst.enter_context(nc.sbuf_tensor("f_slib", [128, 1], I32))
                idxp = bst.enter_context(nc.sbuf_tensor("f_idxp", [128, 97], I32))
                xtm = [bst.enter_context(
                    nc.sbuf_tensor(f"f_xtm{i}", [128, 1024], BF16))
                    for i in range(2)]
                sb.op("pool", lambda e: e.dma_start(
                    out=rt[:].rearrange("p (c n) -> p c n", c=NC_),
                    in_=t["rt0"].rearrange("c p n -> p c n")),
                    writes=["rt"], slot="bia2")
                sb.op("pool", lambda e: e.dma_start(out=eb[:], in_=t["eb0"]),
                      writes=["eb"], slot="bia2")
                sb.op("sp", lambda e: e.dma_start(out=tids[:], in_=t["tids"]),
                      writes=["tids"], slot="bia")
                sb.op("sp", lambda e: e.dma_start(out=prange[:], in_=t["prange"]),
                      writes=["prange"], slot="bia")
                sb.op("dve", lambda e: e.memset(idxp[:], TROW), writes=["idxp"])
                sb.op("sp", lambda e: e.dma_start(
                    out=t["idx0_d"].rearrange("(p j) o -> p (j o)", p=128),
                    in_=idxp[:]),
                    reads=["idxp"], writes=["idx0pre"], slot="st3")
                sb.op("dve", lambda e: e.memset(ones8[:], 1.0), writes=["ones8"])
                sb.op("sp", lambda e: e.dma_start(out=ltri8[:], in_=t["ltri8"]),
                      writes=["ltri8"], slot="bia")
                with (
                    nc.psum_tensor("f_pr0", [128, 512], F32) as pr0,
                    nc.psum_tensor("f_pr1", [128, 512], F32) as pr1,
                    nc.psum_tensor("f_pti", [8, 1024], F32) as pti,
                    nc.psum_tensor("f_ptr", [128, 2048], BF16) as ptr,
                ):
                    for tt in range(NTT):
                        hf = tt % 2
                        psl = (pr0 if hf == 0 else pr1)[:, 0:8]
                        for c in range(NC_):
                            sb.op("pe", lambda e, tt=tt, c=c, psl=psl: e.matmul(
                                psl,
                                xn[:, c * 4096 + tt * 128:c * 4096 + (tt + 1) * 128],
                                rt[:, c * 8:(c + 1) * 8],
                                start=(c == 0), stop=False),
                                reads=[f"xn:{c}", "rt"], writes=[f"pr:{hf}"],
                                inc=False)
                        sb.op("pe", lambda e, psl=psl: e.matmul(
                            psl, onesb[:], eb[:], start=False, stop=True),
                            reads=["onesb", "eb"], writes=[f"pr:{hf}"])
                        sb.op("dve", lambda e, psl=psl: e.tensor_copy(lg[:], psl),
                              reads=[f"pr:{hf}"], writes=["lg"])
                        sb.op("dve", lambda e: e.reduce_max(
                            rr[:, 0:1], lg[:], axis=AX.X, negate=True),
                            reads=["lg"], writes=["rr"])
                        sb.op("dve", lambda e: e.reduce_max(
                            rr[:, 1:2], lg[:], axis=AX.X),
                            reads=["lg"], writes=["rr"])
                        sb.op("dve", lambda e: e.tensor_scalar(
                            rga[:], lg[:], rr[:, 1:2], None, ALU.is_ge),
                            reads=["lg", "rr"], writes=["rga"])
                        sb.op("dve", lambda e, tt=tt: e.tensor_copy(
                            inda[:, tt * 8:(tt + 1) * 8], rga[:]),
                            reads=["rga"], writes=[f"inda:{tt}"])
                        sb.op("dve", lambda e: e.scalar_tensor_tensor(
                            rgb[:], rga[:], -1e30, lg[:], ALU.mult, ALU.add),
                            reads=["rga", "lg"], writes=["rgb"])
                        sb.op("dve", lambda e: e.reduce_max(
                            rr[:, 2:3], rgb[:], axis=AX.X),
                            reads=["rgb"], writes=["rr"])
                        sb.op("dve", lambda e: e.tensor_scalar(
                            rga[:], lg[:], rr[:, 2:3], None, ALU.is_ge),
                            reads=["lg", "rr"], writes=["rga"])
                        sb.op("dve", lambda e, tt=tt: e.tensor_copy(
                            indt[:, tt * 8:(tt + 1) * 8], rga[:]),
                            reads=["rga"], writes=[f"ind:{tt}"])
                        sb.op("act", lambda e: e.activation(
                            rgb[:], lg[:], AF.Exp, bias=rr[:, 0:1]),
                            reads=["lg", "rr"], writes=["rgb"])
                        sb.op("act", lambda e: e.activation(
                            rr[:, 3:4], rr[:, 2:3], AF.Exp, bias=rr[:, 0:1]),
                            reads=["rr"], writes=["rr2"])
                        sb.op("dve", lambda e: e.tensor_scalar(
                            rr[:, 4:5], rr[:, 3:4], 1.0, None, ALU.add),
                            reads=["rr2"], writes=["rr2"])
                        sb.op("dve", lambda e: e.reciprocal(rr[:, 5:6], rr[:, 4:5]),
                              reads=["rr2"], writes=["rr2"])
                        sb.op("dve", lambda e, tt=tt: e.scalar_tensor_tensor(
                            cw[:, tt * 10 + 2:tt * 10 + 10], rgb[:], rr[:, 5:6],
                            rga[:], ALU.mult, ALU.mult),
                            reads=["rgb", "rr2", "rga", "cwall"],
                            writes=[f"cw:{tt}"])
                        sb.op("pe", lambda e, tt=tt, hf=hf: e.transpose(
                            pti[:, hf * 512:hf * 512 + 128],
                            indt[:, tt * 8:(tt + 1) * 8], idnf[:]),
                            reads=[f"ind:{tt}", "idnf"], writes=[f"pti:{hf}"])
                        sb.op("dve", lambda e, tt=tt, hf=hf: e.reduce_sum(
                            cnts[:, tt:tt + 1], pti[:, hf * 512:hf * 512 + 128],
                            axis=AX.X),
                            reads=[f"pti:{hf}"], writes=[f"cnt:{tt}"])
                        for c in range(NC_):
                            pb = c % 2
                            sb.op("pe", lambda e, tt=tt, c=c, pb=pb: e.transpose(
                                ptr[:, pb * 1024:pb * 1024 + 128],
                                xn[:, c * 4096 + tt * 128:c * 4096 + (tt + 1) * 128],
                                idn[:]),
                                reads=[f"xn:{c}", "idn"], writes=[f"ptr:{pb}"])
                            sb.op("dve", lambda e, c=c, hf=hf, pb=pb: e.tensor_copy(
                                xtm[hf][:, c * 128:(c + 1) * 128],
                                ptr[:, pb * 1024:pb * 1024 + 128]),
                                reads=[f"ptr:{pb}"], writes=[f"xtm:{hf}"])
                        sb.op("sp", lambda e, tt=tt, hf=hf: e.dma_start(
                            out=t["xntm_d"][tt * 128:(tt + 1) * 128, :],
                            in_=xtm[hf][:]),
                            reads=[f"xtm:{hf}"], writes=[f"xntm:{tt}"],
                            slot=f"st{hf}")
                cntn = [f"cnt:{i}" for i in range(NTT)]
                sb.op("dve", lambda e: e.tensor_tensor_scan(
                    csum[:], ones8[:, 0:32], cnts[:], 0.0, ALU.mult, ALU.add),
                    reads=["ones8"] + cntn, writes=["csum"])
                sb.op("dve", lambda e: e.tensor_sub(csum[:], csum[:], cnts[:]),
                      reads=["csum"] + cntn, writes=["bases"])
                with (
                    nc.psum_tensor("f_pti2", [8, 1024], F32) as pti2,
                    nc.psum_tensor("f_ptm", [128, 1024], F32) as ptm,
                    nc.psum_tensor("f_pbB", [128, 512], F32) as pbB,
                ):
                    sb.op("dve", lambda e: e.reduce_sum(
                        lds8[:, 0:1], cnts[:], axis=AX.X),
                        reads=cntn, writes=["lds"])
                    for k in range(8):
                        sb.op("dve", lambda e, k=k: e.tensor_scalar(
                            pdc[:, k:k + 1], lds8[:, 0:1],
                            float(512 * k) + 0.5, None, ALU.is_ge),
                            reads=["lds"], writes=["pdc"])
                    sb.op("dve", lambda e: e.reduce_sum(
                        lds8[:, 1:2], pdc[:], axis=AX.X),
                        reads=["pdc"], writes=["ldm"])
                    sb.op("dve", lambda e: e.tensor_scalar(
                        lds8[:, 1:2], lds8[:, 1:2], 512.0, None, ALU.mult),
                        reads=["ldm"], writes=["ldp"])
                    sb.op("dve", lambda e: e.tensor_scalar(
                        padw[:], ones8[:], lds8[:, 1:2], None, ALU.mult),
                        reads=["ones8", "ldp"], writes=["pdw"])
                    sb.op("pe", lambda e: e.matmul(
                        pbB[:, 0:8], padw[:], ltri8[:],
                        start=True, stop=True),
                        reads=["pdw", "ltri8"], writes=["pbB"])
                    sb.op("dve", lambda e: e.tensor_copy(pbS[:], pbB[:, 0:8]),
                          reads=["pbB"], writes=["pbS"])
                    for g in range(SG):
                        sb.op("dve", lambda e, g=g: e.tensor_scalar(
                            tmp8[:], pbS[:], float(g * 512), None, ALU.is_le),
                            reads=["pbS"], writes=["t8"])
                        sb.op("dve", lambda e, g=g: e.reduce_sum(
                            egsf[:, g:g + 1], tmp8[:], axis=AX.X),
                            reads=["t8"], writes=["egs"])
                    sb.op("dve", lambda e: e.tensor_scalar(
                        egsf[:], egsf[:], 128.0, 128.0, ALU.mult, ALU.add),
                        reads=["egs"], writes=["egs"])
                    sb.op("dve", lambda e: e.tensor_scalar(
                        egsf[:], egsf[:], prange[:], None, ALU.add),
                        reads=["egs", "prange"], writes=["egs"])
                    sb.op("dve", lambda e: e.tensor_copy(wofsI[:], egsf[:]),
                          reads=["egs"], writes=["wofsI"])
                    for tt in range(NTT):
                        hf = tt % 2
                        sb.op("pe", lambda e, tt=tt, hf=hf: e.transpose(
                            pti2[:, hf * 512:hf * 512 + 128],
                            indt[:, tt * 8:(tt + 1) * 8], idnf[:]),
                            reads=[f"ind:{tt}", "idnf"], writes=[f"pti2:{hf}"])
                        sb.op("dve", lambda e, hf=hf: e.tensor_tensor_scan(
                            rk8[:], ones8[:], pti2[:, hf * 512:hf * 512 + 128],
                            0.0, ALU.mult, ALU.add),
                            reads=["ones8", f"pti2:{hf}"], writes=["rk8"])
                        sb.op("dve", lambda e, tt=tt: e.tensor_scalar(
                            rk8[:], rk8[:], csum[:, tt:tt + 1], None, ALU.add),
                            reads=["rk8", "bases"], writes=["rk8"])
                        sb.op("pe", lambda e, hf=hf: e.transpose(
                            ptm[:, hf * 512:hf * 512 + 8], rk8[:],
                            idnf[0:8, 0:8]),
                            reads=["rk8", "idnf"], writes=[f"ptm:{hf}"])
                        sb.op("dve", lambda e, hf=hf: e.tensor_add(
                            base8[:], ptm[:, hf * 512:hf * 512 + 8], pbS[:]),
                            reads=[f"ptm:{hf}", "pbS"], writes=["b8"])
                        sb.op("dve", lambda e: e.tensor_scalar(
                            base8[:], base8[:], -1.0 - BIG0, None, ALU.add),
                            reads=["b8"], writes=["b8"])
                        sb.op("dve", lambda e, tt=tt: e.tensor_mul(
                            tmp8[:], inda[:, tt * 8:(tt + 1) * 8], base8[:]),
                            reads=[f"inda:{tt}", "b8"], writes=["t8"])
                        sb.op("dve", lambda e: e.tensor_scalar(
                            tmp8[:], tmp8[:], float(BIG0), None, ALU.add),
                            reads=["t8"], writes=["t8"])
                        sb.op("dve", lambda e: e.tensor_reduce(
                            slf[:], tmp8[:], AX.X, ALU.min),
                            reads=["t8"], writes=["slf"])
                        sb.op("dve", lambda e: e.tensor_copy(slia[:], slf[:]),
                              reads=["slf"], writes=["slia"])
                        sb.op("dve", lambda e, tt=tt: e.tensor_copy(
                            slab_sb[:, 2 * tt:2 * tt + 1], slf[:]),
                            reads=["slf"], writes=["slabs"])
                        sb.op("pool", lambda e, tt=tt: e.indirect_dma_start(
                            t["idx0_d"][0:128, :], IoA(slia[:], 0),
                            tids[:, tt:tt + 1], None),
                            reads=["slia", "tids", "idx0pre"],
                            writes=[f"isc:{tt}:0"], slot="sca")
                        sb.op("dve", lambda e, tt=tt: e.tensor_mul(
                            tmp8[:], inda[:, tt * 8:(tt + 1) * 8],
                            cw[:, tt * 10 + 2:tt * 10 + 10]),
                            reads=[f"inda:{tt}", f"cw:{tt}"], writes=["t8"])
                        sb.op("dve", lambda e, tt=tt: e.reduce_sum(
                            cwab_sb[:, 2 * tt:2 * tt + 1], tmp8[:], axis=AX.X),
                            reads=["t8"], writes=["cwabs"])
                        sb.op("dve", lambda e, tt=tt: e.tensor_sub(
                            rga[:], indt[:, tt * 8:(tt + 1) * 8],
                            inda[:, tt * 8:(tt + 1) * 8]),
                            reads=[f"ind:{tt}", f"inda:{tt}"], writes=["rga"])
                        sb.op("dve", lambda e: e.tensor_mul(
                            tmp8[:], rga[:], base8[:]),
                            reads=["rga", "b8"], writes=["t8"])
                        sb.op("dve", lambda e: e.tensor_scalar(
                            tmp8[:], tmp8[:], float(BIG0), None, ALU.add),
                            reads=["t8"], writes=["t8"])
                        sb.op("dve", lambda e: e.tensor_reduce(
                            slf[:], tmp8[:], AX.X, ALU.min),
                            reads=["t8"], writes=["slf"])
                        sb.op("dve", lambda e: e.tensor_copy(slib[:], slf[:]),
                              reads=["slf"], writes=["slib"])
                        sb.op("dve", lambda e, tt=tt: e.tensor_copy(
                            slab_sb[:, 2 * tt + 1:2 * tt + 2], slf[:]),
                            reads=["slf"], writes=["slabs"])
                        sb.op("pool", lambda e, tt=tt: e.indirect_dma_start(
                            t["idx0_d"][0:128, :], IoA(slib[:], 0),
                            tids[:, tt:tt + 1], None),
                            reads=["slib", "tids", "idx0pre"],
                            writes=[f"isc:{tt}:1"], slot="scb")
                        sb.op("dve", lambda e, tt=tt: e.tensor_mul(
                            tmp8[:], rga[:], cw[:, tt * 10 + 2:tt * 10 + 10]),
                            reads=["rga", f"cw:{tt}"], writes=["t8"])
                        sb.op("dve", lambda e, tt=tt: e.reduce_sum(
                            cwab_sb[:, 2 * tt + 1:2 * tt + 2], tmp8[:],
                            axis=AX.X),
                            reads=["t8"], writes=["cwabs"])
                    sb.op("sp", lambda e: e.dma_start(
                        out=t["slab_d"].rearrange("(tt p) k -> p tt k", p=128),
                        in_=slab_sb[:].rearrange("p (tt k) -> p tt k", k=2)),
                        reads=["slabs"], writes=["slabst"], slot="st2")
                    sb.op("sp", lambda e: e.dma_start(
                        out=t["cwab_d"].rearrange("(tt p) k -> p tt k", p=128),
                        in_=cwab_sb[:].rearrange("p (tt k) -> p tt k", k=2)),
                        reads=["cwabs"], writes=["cwabst"], slot="st2")
            with ExitStack() as sst:
                gtmp = [sst.enter_context(
                    nc.sbuf_tensor(f"f_gt{i}", [128, 1024], F32))
                    for i in range(2)]
                ph = [sst.enter_context(nc.psum_tensor(f"f_ph{i}", [128, 512], F32))
                      for i in range(4)]
                po = [sst.enter_context(nc.psum_tensor(f"f_po{i}", [128, 1024], F32))
                      for i in range(2)]
                for j in range(2):
                    for wn, wt_ in (("w1s", w1t), ("w3s", w3t), ("w2s", w2t)):
                        sb.op("pool", lambda e, wn=wn, wt_=wt_, j=j: e.dma_start(
                            out=wt_[:], in_=t[wn][j]),
                            writes=[wn], slot=wn)
                    for tb in range(8):
                        for mh in range(16):
                            pa = ph[2 * (mh % 2)]
                            pb_ = ph[2 * (mh % 2) + 1]
                            for c in range(NC_):
                                sb.op("pe", lambda e, mh=mh, c=c, tb=tb, pa=pa: e.matmul(
                                    pa[:], w1t[:, c * 2048 + mh * 128:c * 2048 + (mh + 1) * 128],
                                    xn[:, c * 4096 + tb * 512:c * 4096 + (tb + 1) * 512],
                                    start=(c == 0), stop=(c == NC_ - 1)),
                                    reads=["w1s", f"xn:{c}"],
                                    writes=[f"ph:{2 * (mh % 2)}"],
                                    inc=(c == NC_ - 1))
                            for c in range(NC_):
                                sb.op("pe", lambda e, mh=mh, c=c, tb=tb, pb_=pb_: e.matmul(
                                    pb_[:], w3t[:, c * 2048 + mh * 128:c * 2048 + (mh + 1) * 128],
                                    xn[:, c * 4096 + tb * 512:c * 4096 + (tb + 1) * 512],
                                    start=(c == 0), stop=(c == NC_ - 1)),
                                    reads=["w3s", f"xn:{c}"],
                                    writes=[f"ph:{2 * (mh % 2) + 1}"],
                                    inc=(c == NC_ - 1))
                            sb.op("act", lambda e, pa=pa: e.activation(
                                s1t[:], pa[:], AF.Silu),
                                reads=[f"ph:{2 * (mh % 2)}"], writes=["s1t"])
                            sb.op("dve", lambda e, mh=mh, pb_=pb_: e.tensor_mul(
                                hh[:, mh * 512:(mh + 1) * 512], s1t[:], pb_[:]),
                                reads=["s1t", f"ph:{2 * (mh % 2) + 1}"],
                                writes=[f"hh:{mh}"])
                        for ti in range(4):
                            tt = tb * 4 + ti
                            pp = po[ti % 2]
                            for hc in range(16):
                                for dh in range(2):
                                    sb.op("pe", lambda e, hc=hc, dh=dh, ti=ti, pp=pp: e.matmul(
                                        pp[:, dh * 512:(dh + 1) * 512],
                                        hh[:, hc * 512 + ti * 128:hc * 512 + (ti + 1) * 128],
                                        w2t[:, hc * 1024 + dh * 512:hc * 1024 + (dh + 1) * 512],
                                        start=(hc == 0), stop=(hc == 15)),
                                        reads=[f"hh:{hc}", "w2s"],
                                        writes=[f"po:{ti % 2}"],
                                        inc=(hc == 15 and dh == 1))
                            sb.op("dve", lambda e, tt=tt, j=j, ti=ti, pp=pp: e.tensor_scalar(
                                gtmp[ti % 2][:], pp[:],
                                cw[:, tt * 10 + j:tt * 10 + j + 1],
                                None, ALU.mult),
                                reads=[f"po:{ti % 2}", f"cw:{tt}", "cwall"],
                                writes=[f"gt:{ti % 2}"])
                            sb.op("pool", lambda e, tt=tt, ti=ti, j=j: e.dma_start(
                                out=t["gmoe_d"][tt * 128:(tt + 1) * 128, :],
                                in_=gtmp[ti % 2][:],
                                accum_op=(ALU.bypass if j == 0 else ALU.add)),
                                reads=[f"gt:{ti % 2}"], writes=[f"gmoe_d:{tt}"],
                                slot=f"gm{tt}")
        with ExitStack() as st2:
            xgt = st2.enter_context(nc.sbuf_tensor("f_xgt", [128, 8 * 1024], BF16))
            xgf = [st2.enter_context(
                nc.sbuf_tensor(f"f_xgf{i}", [128, NC_ * 512], BF16))
                for i in range(2)]
            w2b = [w2t, st2.enter_context(
                nc.sbuf_tensor("f_w2b", [128, 16 * 1024], BF16))]
            rtile = [st2.enter_context(
                nc.sbuf_tensor(f"f_rtl{i}", [128, 1024], F32))
                for i in range(2)]
            idxg = [st2.enter_context(nc.sbuf_tensor(f"f_ixg{i}", [128, 4], I32))
                    for i in range(2)]
            zrt = st2.enter_context(nc.sbuf_tensor("f_zrt", [128, 1024], F32))
            iscn = [f"isc:{i}:{k}" for i in range(NTT) for k in range(2)]
            xtmn = [f"xntm:{i}" for i in range(NTT)]
            w1f = t["w1s"].rearrange("j p n -> (j p) n")
            w3f = t["w3s"].rearrange("j p n -> (j p) n")
            w2f = t["w2s"].rearrange("j p n -> (j p) n")
            sb.op("dve", lambda e: e.memset(zrt[:], 0.0), writes=["zrt"])
            sb.op("sp", lambda e: e.dma_start(
                out=t["rout_d"][BIG0:BIG0 + 128, :], in_=zrt[:]),
                reads=["zrt"], writes=["routz"], slot="rst")
            with ExitStack() as pst:
                ptg = pst.enter_context(nc.psum_tensor("f_ptg", [128, 2048], BF16))
                ph = [pst.enter_context(nc.psum_tensor(f"f_qh{i}", [128, 512], F32))
                      for i in range(4)]
                po = [pst.enter_context(nc.psum_tensor(f"f_qo{i}", [128, 512], F32))
                      for i in range(2)]
                for g in range(SG):
                    hs = g % 2
                    w2c = w2b[hs]
                    sb.op("pool", lambda e, g=g: e.indirect_dma_start(
                        w1t[:], None, w1f[0:128, :],
                        IoA(wofsI[:, g:g + 1], 0)),
                        reads=["wofsI"], writes=["w1s"], slot="w1s")
                    sb.op("pool", lambda e, g=g: e.indirect_dma_start(
                        w3t[:], None, w3f[0:128, :],
                        IoA(wofsI[:, g:g + 1], 0)),
                        reads=["wofsI"], writes=["w3s"], slot="w3s")
                    sb.op("pool", lambda e, g=g, w2c=w2c, hs=hs: e.indirect_dma_start(
                        w2c[:], None, w2f[0:128, :],
                        IoA(wofsI[:, g:g + 1], 0)),
                        reads=["wofsI"], writes=[f"w2s:{hs}"], slot=f"w2s{hs}")
                    sb.op("sp", lambda e, g=g, hs=hs: e.dma_start(
                        out=idxg[hs][:],
                        in_=t["idx0_d"][g * 512:(g + 1) * 512, :].rearrange(
                            "(p q) o -> p (q o)", p=128)),
                        reads=iscn + ["idx0pre"], writes=[f"idxg:{hs}"],
                        slot="idxl")
                    for q in range(4):
                        sb.op("pool", lambda e, q=q, hs=hs: e.indirect_dma_start(
                            xgt[:, (hs * 4 + q) * 1024:(hs * 4 + q + 1) * 1024],
                            None, t["xntm_d"][0:128, :],
                            IoA(idxg[hs][:, q:q + 1], 0)),
                            reads=[f"idxg:{hs}"] + xtmn,
                            writes=[f"xg:{hs * 4 + q}"], slot="xga")
                    for c in range(NC_):
                        for q in range(4):
                            pb = (c * 4 + q) % 2
                            sb.op("pe", lambda e, c=c, q=q, hs=hs, pb=pb: e.transpose(
                                ptg[:, pb * 1024:pb * 1024 + 128],
                                xgt[:, (hs * 4 + q) * 1024 + c * 128:(hs * 4 + q) * 1024 + (c + 1) * 128],
                                idn[:]),
                                reads=[f"xg:{hs * 4 + q}", "idn"],
                                writes=[f"ptg:{pb}"])
                            sb.op("dve", lambda e, c=c, q=q, hs=hs, pb=pb: e.tensor_copy(
                                xgf[hs][:, c * 512 + q * 128:c * 512 + (q + 1) * 128],
                                ptg[:, pb * 1024:pb * 1024 + 128]),
                                reads=[f"ptg:{pb}"], writes=[f"xgf:{hs}:{c}"])
                    for mh in range(16):
                        pa = ph[2 * (mh % 2)]
                        pb_ = ph[2 * (mh % 2) + 1]
                        for c in range(NC_):
                            sb.op("pe", lambda e, mh=mh, c=c, hs=hs, pa=pa: e.matmul(
                                pa[:], w1t[:, c * 2048 + mh * 128:c * 2048 + (mh + 1) * 128],
                                xgf[hs][:, c * 512:(c + 1) * 512],
                                start=(c == 0), stop=(c == NC_ - 1)),
                                reads=["w1s", f"xgf:{hs}:{c}"],
                                writes=[f"ph:{2 * (mh % 2)}"],
                                inc=(c == NC_ - 1))
                        for c in range(NC_):
                            sb.op("pe", lambda e, mh=mh, c=c, hs=hs, pb_=pb_: e.matmul(
                                pb_[:], w3t[:, c * 2048 + mh * 128:c * 2048 + (mh + 1) * 128],
                                xgf[hs][:, c * 512:(c + 1) * 512],
                                start=(c == 0), stop=(c == NC_ - 1)),
                                reads=["w3s", f"xgf:{hs}:{c}"],
                                writes=[f"ph:{2 * (mh % 2) + 1}"],
                                inc=(c == NC_ - 1))
                        sb.op("act", lambda e, pa=pa: e.activation(
                            s1t[:], pa[:], AF.Silu),
                            reads=[f"ph:{2 * (mh % 2)}"], writes=["s1t"])
                        sb.op("dve", lambda e, mh=mh, pb_=pb_: e.tensor_mul(
                            hh[:, mh * 512:(mh + 1) * 512], s1t[:], pb_[:]),
                            reads=["s1t", f"ph:{2 * (mh % 2) + 1}"],
                            writes=[f"hh:{mh}"])
                    for ti in range(4):
                        rb = ti % 2
                        for dh in range(2):
                            pp = po[(2 * ti + dh) % 2]
                            for hc in range(16):
                                sb.op("pe", lambda e, hc=hc, dh=dh, ti=ti, hs=hs, pp=pp, w2c=w2c: e.matmul(
                                    pp[:],
                                    hh[:, hc * 512 + ti * 128:hc * 512 + (ti + 1) * 128],
                                    w2c[:, hc * 1024 + dh * 512:hc * 1024 + (dh + 1) * 512],
                                    start=(hc == 0), stop=(hc == 15)),
                                    reads=[f"hh:{hc}", f"w2s:{hs}"],
                                    writes=[f"po:{(2 * ti + dh) % 2}"],
                                    inc=(hc == 15))
                            sb.op("act", lambda e, dh=dh, rb=rb, pp=pp: e.activation(
                                rtile[rb][:, dh * 512:(dh + 1) * 512], pp[:],
                                AF.Identity),
                                reads=[f"po:{(2 * ti + dh) % 2}"],
                                writes=[f"rtl:{rb}"])
                        sb.op("sp", lambda e, g=g, ti=ti, rb=rb: e.dma_start(
                            out=t["rout_d"][g * 512:(g + 1) * 512, :]
                            .rearrange("(p q) d -> p q d", q=4)[:, ti, :],
                            in_=rtile[rb][:]),
                            reads=[f"rtl:{rb}"], writes=[f"rout:{g}:{ti}"],
                            slot="rst")
    sb.barrier()


def emit_s4(p, b):
    nc, sb, t = p.nc, p.sb, p.t
    sb.barrier()
    with ExitStack() as st:
        gfm = st.enter_context(nc.sbuf_tensor(f"c_g{b}", [128, NC_ * S], F32))
        x1b = [st.enter_context(nc.sbuf_tensor(f"c_x1{b}{i}", [128, S], F32))
               for i in range(2)]
        gt = st.enter_context(nc.sbuf_tensor(f"c_gt{b}", [128, 2 * 1024], F32))
        ga = st.enter_context(nc.sbuf_tensor(f"c_ga{b}", [128, 2 * 1024], F32))
        gb2 = st.enter_context(nc.sbuf_tensor(f"c_gb{b}", [128, 2 * 1024], F32))
        slab2 = st.enter_context(nc.sbuf_tensor(f"c_sl{b}", [128, 4], I32))
        cwab2 = st.enter_context(nc.sbuf_tensor(f"c_cw{b}", [128, 4], F32))
        gs = st.enter_context(nc.sbuf_tensor(f"c_gs{b}", [128, 1024], BF16))
        xs_ = [st.enter_context(nc.sbuf_tensor(f"c_xs{b}{i}", [128, S], F32))
               for i in range(2)]
        ss_ = [st.enter_context(nc.sbuf_tensor(f"c_ss{b}{i}", [128, S], F32))
               for i in range(2)]
        idn = st.enter_context(nc.sbuf_tensor(f"c_idn{b}", [128, 128], BF16))
        cb = st.enter_context(nc.sbuf_tensor(f"c_cb{b}", [128, S], BF16))
        sq = st.enter_context(nc.sbuf_tensor(f"c_sq{b}", [128, S], BF16))
        ones = st.enter_context(nc.sbuf_tensor(f"c_on{b}", [128, 1], BF16))
        onesr = st.enter_context(nc.sbuf_tensor(f"c_onr{b}", [1, 128], F32))
        rows = st.enter_context(nc.sbuf_tensor(f"c_rw{b}", [1, 3 * S], F32))
        t12 = st.enter_context(nc.sbuf_tensor(f"c_t12{b}", [128, S], F32))
        sb.op("dve", lambda e: e.memset(ones[:], 1.0), writes=["ones"])
        sb.op("dve", lambda e: e.memset(onesr[:], 1.0), writes=["onesr"])
        sb.op("sp", lambda e: e.dma_start(out=idn[:], in_=t["idn"]),
              writes=["idn"], slot="bia")
        routnames = ([f"rout:{g}:{q}" for g in range(24) for q in range(4)]
                     + ["routz"])
        with nc.psum_tensor(f"c_pt{b}", [128, 2048], BF16) as pt:
            for tloc in range(16):
                tt = b * 16 + tloc
                hf = tloc % 2
                gth = gt[:, hf * 1024:(hf + 1) * 1024]
                sb.op("sp", lambda e, tt=tt, hf=hf: e.dma_start(
                    out=slab2[:, hf * 2:(hf + 1) * 2],
                    in_=t["slab_d"][tt * 128:(tt + 1) * 128, :]),
                    reads=["slabst"], writes=[f"slab2:{hf}"], slot="ld5")
                sb.op("sp", lambda e, tt=tt, hf=hf: e.dma_start(
                    out=cwab2[:, hf * 2:(hf + 1) * 2],
                    in_=t["cwab_d"][tt * 128:(tt + 1) * 128, :]),
                    reads=["cwabst"], writes=[f"cwab2:{hf}"], slot="ld5")
                sb.op("sp", lambda e, tt=tt, gth=gth: e.dma_start(
                    out=gth, in_=t["gmoe_d"][tt * 128:(tt + 1) * 128, :]),
                    reads=[f"gmoe_d:{tt}"], writes=[f"gt:{hf}"], slot="ld0")
                sb.op("pool", lambda e, hf=hf: e.indirect_dma_start(
                    ga[:, hf * 1024:(hf + 1) * 1024], None,
                    t["rout_d"][0:128, :],
                    IoA(slab2[:, hf * 2:hf * 2 + 1], 0)),
                    reads=[f"slab2:{hf}"] + routnames,
                    writes=[f"ga:{hf}"], slot="ld6")
                sb.op("pool", lambda e, hf=hf: e.indirect_dma_start(
                    gb2[:, hf * 1024:(hf + 1) * 1024], None,
                    t["rout_d"][0:128, :],
                    IoA(slab2[:, hf * 2 + 1:hf * 2 + 2], 0)),
                    reads=[f"slab2:{hf}"] + routnames,
                    writes=[f"gb:{hf}"], slot="ld7")
                sb.op("dve", lambda e, hf=hf, gth=gth: e.scalar_tensor_tensor(
                    gth, ga[:, hf * 1024:(hf + 1) * 1024],
                    cwab2[:, hf * 2:hf * 2 + 1], gth, ALU.mult, ALU.add),
                    reads=[f"ga:{hf}", f"cwab2:{hf}", f"gt:{hf}"],
                    writes=[f"gt:{hf}"])
                sb.op("dve", lambda e, hf=hf, gth=gth: e.scalar_tensor_tensor(
                    gth, gb2[:, hf * 1024:(hf + 1) * 1024],
                    cwab2[:, hf * 2 + 1:hf * 2 + 2], gth, ALU.mult, ALU.add),
                    reads=[f"gb:{hf}", f"cwab2:{hf}", f"gt:{hf}"],
                    writes=[f"gt:{hf}"])
                sb.op("act", lambda e, gth=gth, hf=hf: e.activation(
                    gs[:], gth, AF.Silu),
                    reads=[f"gt:{hf}"], writes=["gs"])
                for c in range(NC_):
                    pb = c % 2
                    sb.op("pe", lambda e, c=c, pb=pb: e.transpose(
                        pt[:, pb * 1024:pb * 1024 + 128],
                        gs[:, c * 128:(c + 1) * 128], idn[:]),
                        reads=["gs", "idn"], writes=[f"ptr:{pb}"])
                    sb.op("dve", lambda e, c=c, tloc=tloc, pb=pb: e.tensor_copy(
                        gfm[:, c * S + tloc * 128:c * S + (tloc + 1) * 128],
                        pt[:, pb * 1024:pb * 1024 + 128]),
                        reads=[f"ptr:{pb}"], writes=[f"gfm:{c}"])
        with (
            nc.psum_tensor(f"c_ps1{b}", [1, S], F32) as p_s1,
            nc.psum_tensor(f"c_ps2{b}", [1, S], F32) as p_s2,
        ):
            for c in range(NC_):
                xb = xs_[c % 2]
                sbf = ss_[c % 2]
                sb.op("sp", lambda e, c=c, xb=xb: e.dma_start(
                    out=xb[:], in_=t["x_fm"][b, c * 128:(c + 1) * 128, :]),
                    writes=[f"xs_:{c % 2}"], slot=f"ld{1 + c % 2}")
                sb.op("sp", lambda e, c=c, sbf=sbf: e.dma_start(
                    out=sbf[:], in_=t["s_d"][b, c * 128:(c + 1) * 128, :]),
                    reads=[f"s_d:{b}:{c}"], writes=[f"ss_:{c % 2}"],
                    slot=f"ld{3 + c % 2}")
                x1c = x1b[c % 2][:]
                sb.op("dve", lambda e, c=c, sbf=sbf: e.tensor_mul(
                    t12[:], sbf[:], gfm[:, c * S:(c + 1) * S]),
                    reads=[f"ss_:{c % 2}", f"gfm:{c}"], writes=["t12"])
                sb.op("dve", lambda e, c=c, xb=xb, x1c=x1c: e.tensor_add(
                    x1c, xb[:], t12[:]),
                    reads=[f"xs_:{c % 2}", "t12"], writes=[f"x1:{c % 2}"])
                sb.op("sp", lambda e, c=c, x1c=x1c: e.dma_start(
                    out=t["x1_d"][b, c * 128:(c + 1) * 128, :], in_=x1c),
                    reads=[f"x1:{c % 2}"], writes=[f"x1_d:{b}:{c}"], slot="st0")
                sb.op("pool", lambda e, x1c=x1c: e.tensor_copy(cb[:], x1c),
                      reads=[f"x1:{c % 2}"], writes=["cb"])
                sb.op("act", lambda e, x1c=x1c: e.activation(sq[:], x1c, AF.Square),
                      reads=[f"x1:{c % 2}"], writes=["sq"])
                for tc in range(4):
                    sl = slice(tc * 512, (tc + 1) * 512)
                    sb.op("pe", lambda e, sl=sl, c=c: e.matmul(
                        p_s1[:, sl], ones[:], cb[:, sl],
                        start=(c == 0), stop=(c == NC_ - 1)),
                        reads=["cb", "ones"], writes=[f"Cs1:{tc}"])
                    sb.op("pe", lambda e, sl=sl, c=c: e.matmul(
                        p_s2[:, sl], ones[:], sq[:, sl],
                        start=(c == 0), stop=(c == NC_ - 1)),
                        reads=["sq", "ones"], writes=[f"Cs2:{tc}"])
            emit_ln_rows(p, "C", p_s1, p_s2, rows, p.cst[0:1, 0:1])
        sb.barrier()
        with (
            nc.psum_tensor(f"c_pm{b}", [128, S], F32) as p_mu,
            nc.psum_tensor(f"c_pr{b}", [128, S], F32) as p_rs,
        ):
            for tc in range(4):
                sl = slice(tc * 512, (tc + 1) * 512)
                sb.op("pe", lambda e, sl=sl: e.matmul(
                    p_mu[:, sl], onesr[:], rows[0:1, 0:S][:, sl],
                    start=True, stop=True),
                    reads=["Cmu", "onesr"], writes=["CBmu"])
                sb.op("pe", lambda e, sl=sl: e.matmul(
                    p_rs[:, sl], onesr[:], rows[0:1, S:2 * S][:, sl],
                    start=True, stop=True),
                    reads=["Crs", "onesr"], writes=["CBrs"])
            for c in range(NC_):
                xb = x1b[c % 2]
                sb.op("sp", lambda e, c=c, xb=xb: e.dma_start(
                    out=xb[:], in_=t["x1_d"][b, c * 128:(c + 1) * 128, :]),
                    reads=[f"x1_d:{b}:{c}"], writes=[f"x1:{c % 2}"],
                    slot=f"ld{1 + c % 2}")
                sb.op("dve", lambda e, xb=xb: e.tensor_sub(
                    t12[:], xb[:], p_mu[:]),
                    reads=[f"x1:{c % 2}", "CBmu"], writes=["t12"])
                sb.op("dve", lambda e: e.tensor_mul(t12[:], t12[:], p_rs[:]),
                      reads=["t12", "CBrs"], writes=["t12"])
                sb.op("sp", lambda e, c=c: e.dma_start(
                    out=t["xn1_d"][b, c * 128:(c + 1) * 128, :], in_=t12[:]),
                    reads=["t12"], writes=[f"xn1_d:{b}:{c}"], slot="st1")
                sb.op("pool", lambda e, c=c: e.tensor_copy(cb[:], t12[:]),
                      reads=["t12"], writes=["cb"])
                sb.op("pool", lambda e, c=c: e.dma_start(
                    out=t["xn1bf_d"][b, c * 128:(c + 1) * 128, :], in_=cb[:]),
                    reads=["cb"], writes=[f"xn1bf_d:{b}:{c}"], slot="st2")
    sb.barrier()


def emit_s5(p, b):
    nc, sb, t = p.nc, p.sb, p.t
    sb.barrier()
    HP = 2052
    with ExitStack() as st:
        mixed = st.enter_context(nc.sbuf_tensor(f"r_mx{b}", [128, NC_ * S], BF16))
        biases = st.enter_context(nc.sbuf_tensor(f"r_bs{b}", [128, 64], F32))
        bi2 = st.enter_context(nc.sbuf_tensor(f"r_b2{b}", [128, 40], F32))
        xc = [st.enter_context(nc.sbuf_tensor(f"r_xc{b}{i}", [128, S], F32))
              for i in range(2)]
        t1 = st.enter_context(nc.sbuf_tensor(f"r_t1{b}", [128, 512], F32))
        t2 = st.enter_context(nc.sbuf_tensor(f"r_t2{b}", [128, 512], F32))
        cb = st.enter_context(nc.sbuf_tensor(f"r_cb{b}", [128, S], BF16))
        sq = st.enter_context(nc.sbuf_tensor(f"r_sq{b}", [128, S], BF16))
        ones = st.enter_context(nc.sbuf_tensor(f"r_on{b}", [128, 1], BF16))
        onesr = st.enter_context(nc.sbuf_tensor(f"r_onr{b}", [1, 128], F32))
        rows = st.enter_context(nc.sbuf_tensor(f"r_rw{b}", [1, 3 * S], F32))
        wl = [st.enter_context(nc.sbuf_tensor(f"r_wl{b}{i}", [128, NC_ * 128], BF16))
              for i in range(2)]
        sb.op("dve", lambda e: e.memset(ones[:], 1.0), writes=["ones"])
        sb.op("dve", lambda e: e.memset(onesr[:], 1.0), writes=["onesr"])
        sb.op("sp", lambda e: e.dma_start(out=biases[:, 0:8], in_=t["ln3g"]),
              writes=["bss"], slot="bia")
        sb.op("sp", lambda e: e.dma_start(out=biases[:, 8:16], in_=t["ln3b"]),
              writes=["bss"], slot="bia")
        sb.op("sp", lambda e: e.dma_start(out=biases[:, 16:40], in_=t["taps"]),
              writes=["bss"], slot="bia")
        sb.op("sp", lambda e: e.dma_start(out=biases[:, 40:48], in_=t["mixb"]),
              writes=["bss"], slot="bia")
        for nm, sl0 in (("bi", 0), ("bg", 8), ("bout", 16), ("bso", 24),
                        ("bsin", 32), ("bd", 34), ("bsel", 36)):
            n = 2 if nm in ("bsin", "bd", "bsel") else 8
            sb.op("sp", lambda e, nm=nm, sl0=sl0, n=n: e.dma_start(
                out=bi2[:, sl0:sl0 + n], in_=t[nm]),
                writes=["bss"], slot="bia")
        with (
            nc.psum_tensor(f"r_ps1{b}", [1, S], F32) as p_s1,
            nc.psum_tensor(f"r_ps2{b}", [1, S], F32) as p_s2,
        ):
            for c in range(NC_):
                xb = xc[c % 2]
                sb.op("sp", lambda e, c=c, xb=xb: e.dma_start(
                    out=xb[:], in_=t["xn1_d"][b, c * 128:(c + 1) * 128, :]),
                    reads=[f"xn1_d:{b}:{c}"], writes=[f"xc:{c % 2}"],
                    slot=f"ld{c % 2}"),
                sb.op("pool", lambda e, xb=xb: e.tensor_copy(cb[:], xb[:]),
                      reads=[f"xc:{c % 2}"], writes=["cb"])
                sb.op("act", lambda e, xb=xb: e.activation(sq[:], xb[:], AF.Square),
                      reads=[f"xc:{c % 2}"], writes=["sq"])
                for tc in range(4):
                    sl = slice(tc * 512, (tc + 1) * 512)
                    sb.op("pe", lambda e, sl=sl, c=c: e.matmul(
                        p_s1[:, sl], ones[:], cb[:, sl],
                        start=(c == 0), stop=(c == NC_ - 1)),
                        reads=["cb", "ones"], writes=[f"Rs1:{tc}"])
                    sb.op("pe", lambda e, sl=sl, c=c: e.matmul(
                        p_s2[:, sl], ones[:], sq[:, sl],
                        start=(c == 0), stop=(c == NC_ - 1)),
                        reads=["sq", "ones"], writes=[f"Rs2:{tc}"])
            emit_ln_rows(p, "R", p_s1, p_s2, rows, p.cst[0:1, 0:1])
        sb.barrier()
        with ExitStack() as st2:
            h_sb = st2.enter_context(
                nc.sbuf_tensor(f"r_h{b}", [128, NC_ * HP], BF16))
            acc = st2.enter_context(nc.sbuf_tensor(f"r_ac{b}", [128, S], F32))
            tf = st2.enter_context(nc.sbuf_tensor(f"r_tf{b}", [128, S], F32))
            sb.op("dve", lambda e: e.memset(h_sb[:], 0.0), writes=["hz"])
            with (
                nc.psum_tensor(f"r_pm{b}", [128, S], F32) as p_mu,
                nc.psum_tensor(f"r_pr2{b}", [128, S], F32) as p_rs,
            ):
                for tc in range(4):
                    sl = slice(tc * 512, (tc + 1) * 512)
                    sb.op("pe", lambda e, sl=sl: e.matmul(
                        p_mu[:, sl], onesr[:], rows[0:1, 0:S][:, sl],
                        start=True, stop=True),
                        reads=["Rmu", "onesr"], writes=["RBmu"])
                    sb.op("pe", lambda e, sl=sl: e.matmul(
                        p_rs[:, sl], onesr[:], rows[0:1, S:2 * S][:, sl],
                        start=True, stop=True),
                        reads=["Rrs", "onesr"], writes=["RBrs"])
                for c in range(NC_):
                    xb = xc[c % 2]
                    sb.op("sp", lambda e, c=c, xb=xb: e.dma_start(
                        out=xb[:], in_=t["xn1_d"][b, c * 128:(c + 1) * 128, :]),
                        reads=[f"xn1_d:{b}:{c}"], writes=[f"xc2:{c % 2}"],
                        slot=f"ld{2 + c % 2}")
                    sb.op("dve", lambda e, c=c, xb=xb: e.tensor_sub(
                        tf[:], xb[:], p_mu[:]),
                        reads=[f"xc2:{c % 2}", "RBmu"], writes=["rtf"])
                    sb.op("dve", lambda e: e.tensor_mul(tf[:], tf[:], p_rs[:]),
                          reads=["rtf", "RBrs"], writes=["rtf"])
                    sb.op("dve", lambda e, c=c: e.tensor_scalar(
                        h_sb[:, c * HP + 1:c * HP + 1 + S],
                        tf[:], biases[:, c:c + 1], biases[:, 8 + c:9 + c],
                        ALU.mult, ALU.add),
                        reads=["rtf", "bss", "hz"], writes=[f"h:{c}"])
            for c in range(NC_):
                rd = [f"h:{c}", "bss", "hz"]
                sb.op("dve", lambda e, c=c: e.tensor_scalar(
                    acc[:], h_sb[:, c * HP:c * HP + S],
                    biases[:, 16 + c * 3:17 + c * 3], biases[:, 40 + c:41 + c],
                    ALU.mult, ALU.add),
                    reads=rd, writes=["racc"])
                sb.op("dve", lambda e, c=c: e.scalar_tensor_tensor(
                    acc[:], h_sb[:, c * HP + 1:c * HP + 1 + S],
                    biases[:, 17 + c * 3:18 + c * 3], acc[:],
                    ALU.mult, ALU.add),
                    reads=rd + ["racc"], writes=["racc"])
                sb.op("dve", lambda e, c=c: e.scalar_tensor_tensor(
                    mixed[:, c * S:(c + 1) * S],
                    h_sb[:, c * HP + 2:c * HP + 2 + S],
                    biases[:, 18 + c * 3:19 + c * 3], acc[:],
                    ALU.mult, ALU.add),
                    reads=rd + ["racc"], writes=[f"mx:{c}"])
        sb.barrier()
        with ExitStack() as st3:
            selst = st3.enter_context(nc.sbuf_tensor(f"r_ss{b}", [128, 2 * S], BF16))
            st4 = st3.enter_context(ExitStack())
            dec = st4.enter_context(nc.sbuf_tensor(f"r_dc{b}", [128, 2 * S], F32))
            xst = st4.enter_context(nc.sbuf_tensor(f"r_xs{b}", [128, 2 * S], F32))
            stt = st4.enter_context(nc.sbuf_tensor(f"r_st{b}", [128, 2 * S], F32))
            d1 = st4.enter_context(nc.sbuf_tensor(f"r_d1{b}", [128, S], F32))
            db = st4.enter_context(nc.sbuf_tensor(f"r_db{b}", [128, S], F32))
            with ExitStack() as pst:
                pj = [pst.enter_context(nc.psum_tensor(f"r_pj{b}{i}", [128, 512], F32))
                      for i in range(8)]
                for wn, dst, fn, bcol in (("wd", dec, AF.Sigmoid, 34),
                                          ("wsin", xst, AF.Tanh, 32)):
                    for m in range(2):
                        wb = m % 2
                        sb.op("pool", lambda e, wn=wn, m=m, wb=wb: e.dma_start(
                            out=wl[wb][:].rearrange("p (c n) -> p c n", c=NC_),
                            in_=t[wn][m].rearrange("c p n -> p c n")),
                            writes=[f"wl{wb}"], slot=f"wqk{wb}")
                        for c in range(NC_):
                            for tc in range(4):
                                sb.op("pe", lambda e, c=c, tc=tc, wb=wb, m=m: e.matmul(
                                    pj[(m % 2) * 4 + tc][:],
                                    wl[wb][:, c * 128:(c + 1) * 128],
                                    mixed[:, c * S + tc * 512:c * S + (tc + 1) * 512],
                                    start=(c == 0), stop=(c == NC_ - 1)),
                                    reads=[f"wl{wb}", f"mx:{c}"],
                                    writes=[f"pj:{(m % 2) * 4 + tc}"],
                                    inc=(c == NC_ - 1))
                        for tc in range(4):
                            sb.op("act", lambda e, m=m, tc=tc, dst=dst, fn=fn, bcol=bcol: e.activation(
                                dst[:, m * S + tc * 512:m * S + (tc + 1) * 512],
                                pj[(m % 2) * 4 + tc][:], fn,
                                bias=bi2[:, bcol + m:bcol + m + 1]),
                                reads=[f"pj:{(m % 2) * 4 + tc}", "bss"],
                                writes=[f"{wn}o:{m}"])
                for m in range(2):
                    sb.op("dve", lambda e, m=m: e.tensor_scalar(
                        d1[:], dec[:, m * S:(m + 1) * S], -1.0, 1.0,
                        ALU.mult, ALU.add),
                        reads=[f"wdo:{m}"], writes=["d1"])
                    sb.op("dve", lambda e, m=m: e.tensor_mul(
                        db[:], d1[:], xst[:, m * S:(m + 1) * S]),
                        reads=[f"wsino:{m}", "d1"], writes=["db"])
                    sb.op("dve", lambda e, m=m: e.tensor_tensor_scan(
                        stt[:, m * S:(m + 1) * S], dec[:, m * S:(m + 1) * S],
                        db[:], 0.0, ALU.mult, ALU.add),
                        reads=[f"wdo:{m}", "db"], writes=[f"stt:{m}"])
                for m in range(2):
                    wb = m % 2
                    sb.op("pool", lambda e, m=m, wb=wb: e.dma_start(
                        out=wl[wb][:].rearrange("p (c n) -> p c n", c=NC_),
                        in_=t["wsel"][m].rearrange("c p n -> p c n")),
                        writes=[f"wl{wb}"], slot=f"wqk{wb}")
                    for c in range(NC_):
                        for tc in range(4):
                            sb.op("pe", lambda e, c=c, tc=tc, wb=wb, m=m: e.matmul(
                                pj[(m % 2) * 4 + tc][:],
                                wl[wb][:, c * 128:(c + 1) * 128],
                                mixed[:, c * S + tc * 512:c * S + (tc + 1) * 512],
                                start=(c == 0), stop=(c == NC_ - 1)),
                                reads=[f"wl{wb}", f"mx:{c}"],
                                writes=[f"pj:{(m % 2) * 4 + tc}"],
                                inc=(c == NC_ - 1))
                    for tc in range(4):
                        sb.op("act", lambda e, m=m, tc=tc: e.activation(
                            t1[:], pj[(m % 2) * 4 + tc][:], AF.Sigmoid,
                            bias=bi2[:, 36 + m:37 + m]),
                            reads=[f"pj:{(m % 2) * 4 + tc}", "bss"],
                            writes=["rt1"])
                        sb.op("dve", lambda e, m=m, tc=tc: e.tensor_mul(
                            selst[:, m * S + tc * 512:m * S + (tc + 1) * 512],
                            t1[:], stt[:, m * S + tc * 512:m * S + (tc + 1) * 512]),
                            reads=["rt1", f"stt:{m}"], writes=[f"sst:{m}"])
            sb.barrier()
            st4.close()
            cmb = st3.enter_context(nc.sbuf_tensor(f"r_cm{b}", [128, NC_ * S], BF16))
            yv = st3.enter_context(nc.sbuf_tensor(f"r_yv{b}", [128, 512], F32))
            uv = st3.enter_context(nc.sbuf_tensor(f"r_uv{b}", [128, 512], F32))
            gv = st3.enter_context(nc.sbuf_tensor(f"r_gv{b}", [128, 512], F32))
            wso_t = st3.enter_context(nc.sbuf_tensor(f"r_ws{b}", [128, 2 * 128], BF16))
            with ExitStack() as pst:
                py = pst.enter_context(nc.psum_tensor(f"r_py{b}", [128, 512], F32))
                pu = pst.enter_context(nc.psum_tensor(f"r_pu{b}", [128, 512], F32))
                pg = pst.enter_context(nc.psum_tensor(f"r_pg{b}", [128, 512], F32))
                for m in range(NC_):
                    sb.op("pool", lambda e, m=m: e.dma_start(
                        out=wso_t[:].rearrange("p (c n) -> p c n", c=2),
                        in_=t["wso"][m].rearrange("c p n -> p c n")),
                        writes=["wsot"], slot="wso")
                    sb.op("pool", lambda e, m=m: e.dma_start(
                        out=wl[0][:].rearrange("p (c n) -> p c n", c=NC_),
                        in_=t["wi"][m].rearrange("c p n -> p c n")),
                        writes=["wl0"], slot="wqk0")
                    sb.op("pool", lambda e, m=m: e.dma_start(
                        out=wl[1][:].rearrange("p (c n) -> p c n", c=NC_),
                        in_=t["wg"][m].rearrange("c p n -> p c n")),
                        writes=[f"wl1"], slot="wqk1")
                    for tc in range(4):
                        for c2 in range(2):
                            sb.op("pe", lambda e, c2=c2, tc=tc: e.matmul(
                                py[:], wso_t[:, c2 * 128:(c2 + 1) * 128],
                                selst[:, c2 * S + tc * 512:c2 * S + (tc + 1) * 512],
                                start=(c2 == 0), stop=(c2 == 1)),
                                reads=["wsot", f"sst:{c2}"], writes=["py"],
                                inc=(c2 == 1))
                        for c in range(NC_):
                            sb.op("pe", lambda e, c=c, tc=tc: e.matmul(
                                pu[:], wl[0][:, c * 128:(c + 1) * 128],
                                mixed[:, c * S + tc * 512:c * S + (tc + 1) * 512],
                                start=(c == 0), stop=(c == NC_ - 1)),
                                reads=["wl0", f"mx:{c}"], writes=["pu"],
                                inc=(c == NC_ - 1))
                        for c in range(NC_):
                            sb.op("pe", lambda e, c=c, tc=tc: e.matmul(
                                pg[:], wl[1][:, c * 128:(c + 1) * 128],
                                mixed[:, c * S + tc * 512:c * S + (tc + 1) * 512],
                                start=(c == 0), stop=(c == NC_ - 1)),
                                reads=["wl1", f"mx:{c}"], writes=["pg"],
                                inc=(c == NC_ - 1))
                        sb.op("act", lambda e, m=m: e.activation(
                            yv[:], py[:], AF.Identity, bias=bi2[:, 24 + m:25 + m]),
                            reads=["py", "bss"], writes=["yv"])
                        sb.op("act", lambda e, m=m: e.activation(
                            uv[:], pu[:], AF.Tanh, bias=bi2[:, 0 + m:1 + m]),
                            reads=["pu", "bss"], writes=["uv"])
                        sb.op("act", lambda e, m=m: e.activation(
                            gv[:], pg[:], AF.Sigmoid, bias=bi2[:, 8 + m:9 + m]),
                            reads=["pg", "bss"], writes=["gv"])
                        sb.op("dve", lambda e: e.tensor_sub(t1[:], yv[:], uv[:]),
                              reads=["yv", "uv"], writes=["rt1"])
                        sb.op("dve", lambda e: e.tensor_mul(t2[:], gv[:], t1[:]),
                              reads=["gv", "rt1"], writes=["rt2"])
                        sb.op("dve", lambda e, m=m, tc=tc: e.tensor_add(
                            cmb[:, m * S + tc * 512:m * S + (tc + 1) * 512],
                            t2[:], uv[:]),
                            reads=["rt2", "uv"], writes=[f"cmb:{m}"])
            sb.barrier()
            with ExitStack() as pst:
                pw = [pst.enter_context(nc.psum_tensor(f"r_pw{b}{i}", [128, 512], F32))
                      for i in range(8)]
                s1v = st3.enter_context(nc.sbuf_tensor(f"r_s1{b}", [128, S], F32))
                for m in range(NC_):
                    wb = m % 2
                    pbase = wb * 4
                    sb.op("pool", lambda e, m=m, wb=wb: e.dma_start(
                        out=wl[wb][:].rearrange("p (c n) -> p c n", c=NC_),
                        in_=t["wout"][m].rearrange("c p n -> p c n")),
                        writes=[f"wl{wb}"], slot=f"wqk{wb}")
                    xb = xc[m % 2]
                    sb.op("sp", lambda e, m=m, xb=xb: e.dma_start(
                        out=xb[:], in_=t["xn1_d"][b, m * 128:(m + 1) * 128, :]),
                        reads=[f"xn1_d:{b}:{m}"], writes=[f"xc3:{m % 2}"],
                        slot=f"ld{4 + m % 2}")
                    for tc in range(4):
                        for c in range(NC_):
                            sb.op("pe", lambda e, c=c, tc=tc, wb=wb, pbase=pbase: e.matmul(
                                pw[pbase + tc][:], wl[wb][:, c * 128:(c + 1) * 128],
                                cmb[:, c * S + tc * 512:c * S + (tc + 1) * 512],
                                start=(c == 0), stop=(c == NC_ - 1)),
                                reads=[f"wl{wb}", f"cmb:{c}"],
                                writes=[f"pw:{pbase + tc}"],
                                inc=(c == NC_ - 1))
                        sb.op("act", lambda e, m=m, tc=tc, pbase=pbase: e.activation(
                            t1[:], pw[pbase + tc][:], AF.Identity,
                            bias=bi2[:, 16 + m:17 + m]),
                            reads=[f"pw:{pbase + tc}", "bss"], writes=["rt1"])
                        sb.op("dve", lambda e, m=m, tc=tc, xb=xb: e.tensor_add(
                            s1v[:, tc * 512:(tc + 1) * 512], t1[:],
                            xb[:, tc * 512:(tc + 1) * 512]),
                            reads=["rt1", f"xc3:{m % 2}"], writes=["s1v"])
                    sb.op("sp", lambda e, m=m: e.dma_start(
                        out=t["s1_d"][b, m * 128:(m + 1) * 128, :], in_=s1v[:]),
                        reads=["s1v"], writes=[f"s1_d:{b}:{m}"], slot="sst")
    sb.barrier()


def emit_s6(p):
    nc, sb, t = p.nc, p.sb, p.t
    sb.barrier()
    NTT = 32
    C1 = 1024
    BIG = 8 * C1
    TROW = B_PER_CORE * S
    with ExitStack() as st:
        idn = st.enter_context(nc.sbuf_tensor("g_idn", [128, 128], BF16))
        rst = st.enter_context(ExitStack())
        rt = rst.enter_context(nc.sbuf_tensor("g_rt", [128, 64], BF16))
        ind = rst.enter_context(nc.sbuf_tensor("g_ind", [128, NTT * 8], F32))
        lg = rst.enter_context(nc.sbuf_tensor("g_lg", [128, 8], F32))
        rr = rst.enter_context(nc.sbuf_tensor("g_rr", [128, 8], F32))
        idnf = rst.enter_context(nc.sbuf_tensor("g_idnf", [128, 128], F32))
        indf = rst.enter_context(nc.sbuf_tensor("g_indf", [8, NTT * 128], F32))
        ones8 = rst.enter_context(nc.sbuf_tensor("g_on8", [8, NTT * 128], F32))
        rkf = rst.enter_context(nc.sbuf_tensor("g_rkf", [8, NTT * 128], F32))
        rktm = rst.enter_context(nc.sbuf_tensor("g_rktm", [128, 8], F32))
        ecap = rst.enter_context(nc.sbuf_tensor("g_ecap", [128, 8], F32))
        tids = rst.enter_context(nc.sbuf_tensor("g_tids", [128, 32], I32))
        sli = rst.enter_context(nc.sbuf_tensor("g_sli", [128, 1], I32))
        slf = rst.enter_context(nc.sbuf_tensor("g_slf", [128, 1], F32))
        idxp = rst.enter_context(nc.sbuf_tensor("g_idxp", [128, 64], I32))
        tmt = [rst.enter_context(nc.sbuf_tensor(f"g_tmt{i}", [128, 1024], BF16))
               for i in range(2)]
        xb2 = [rst.enter_context(nc.sbuf_tensor(f"g_xb{i}", [128, NC_ * 128], BF16))
               for i in range(2)]
        sb.op("pool", lambda e: e.dma_start(
            out=rt[:].rearrange("p (c n) -> p c n", c=NC_),
            in_=t["rt1"].rearrange("c p n -> p c n")),
            writes=["rt"], slot="bia2")
        sb.op("sp", lambda e: e.dma_start(out=idn[:], in_=t["idn"]),
              writes=["idn"], slot="bia")
        sb.op("sp", lambda e: e.dma_start(out=idnf[:], in_=t["idnf"]),
              writes=["idnf"], slot="bia")
        sb.op("sp", lambda e: e.dma_start(out=ecap[:], in_=t["ecap1"]),
              writes=["ecap"], slot="bia")
        sb.op("sp", lambda e: e.dma_start(out=tids[:], in_=t["tids"]),
              writes=["tids"], slot="bia")
        sb.op("dve", lambda e: e.memset(idxp[:], TROW), writes=["idxp"])
        sb.op("sp", lambda e: e.dma_start(
            out=t["idx1_d"][0:8 * C1, :].rearrange("(p j) o -> p (j o)", p=128),
            in_=idxp[:]),
            reads=["idxp"], writes=["idx1pre"], slot="st3")
        with (
            nc.psum_tensor("g_pr0", [128, 512], F32) as pr0,
            nc.psum_tensor("g_pr1", [128, 512], F32) as pr1,
            nc.psum_tensor("g_ptr", [128, 2048], BF16) as ptr,
            nc.psum_tensor("g_pti", [8, 512], F32) as pti,
        ):
            for tt in range(NTT):
                b = tt // 16
                tloc = tt % 16
                hf = tt % 2
                psl = (pr0 if hf == 0 else pr1)[:, 0:8]
                xb = xb2[hf]
                for c in range(NC_):
                    sb.op("sp", lambda e, b=b, c=c, tloc=tloc, xb=xb: e.dma_start(
                        out=xb[:, c * 128:(c + 1) * 128],
                        in_=t["xn1bf_d"][b, c * 128:(c + 1) * 128,
                                         tloc * 128:(tloc + 1) * 128]),
                        reads=[f"xn1bf_d:{b}:{c}"], writes=[f"gxt:{hf}:{c}"],
                        slot=f"ld{hf}")
                allx = [f"gxt:{hf}:{i}" for i in range(NC_)]
                for c in range(NC_):
                    sb.op("pe", lambda e, c=c, psl=psl, xb=xb: e.matmul(
                        psl, xb[:, c * 128:(c + 1) * 128],
                        rt[:, c * 8:(c + 1) * 8],
                        start=(c == 0), stop=(c == NC_ - 1)),
                        reads=allx + ["rt"], writes=[f"gpr:{hf}"],
                        inc=(c == NC_ - 1))
                tb_ = tmt[hf]
                for c in range(NC_):
                    pb = c % 2
                    sb.op("pe", lambda e, c=c, xb=xb, pb=pb: e.transpose(
                        ptr[:, pb * 1024:pb * 1024 + 128],
                        xb[:, c * 128:(c + 1) * 128], idn[:]),
                        reads=allx + ["idn"], writes=[f"gptr:{pb}"])
                    sb.op("dve", lambda e, c=c, pb=pb, tb_=tb_: e.tensor_copy(
                        tb_[:, c * 128:(c + 1) * 128],
                        ptr[:, pb * 1024:pb * 1024 + 128]),
                        reads=[f"gptr:{pb}"], writes=[f"gtmt:{hf}"])
                sb.op("sp", lambda e, tt=tt, tb_=tb_: e.dma_start(
                    out=t["xn1tm_d"][tt * 128:(tt + 1) * 128, :], in_=tb_[:]),
                    reads=[f"gtmt:{hf}"], writes=[f"xn1tm:{tt}"],
                    slot=f"st{hf}")
                sb.op("dve", lambda e, psl=psl: e.tensor_copy(lg[:], psl),
                      reads=[f"gpr:{hf}"], writes=["glg"])
                sb.op("dve", lambda e: e.reduce_max(rr[:, 0:1], lg[:], axis=AX.X),
                      reads=["glg"], writes=["grr"])
                sb.op("dve", lambda e, tt=tt: e.tensor_scalar(
                    ind[:, tt * 8:(tt + 1) * 8], lg[:], rr[:, 0:1], None,
                    ALU.is_ge),
                    reads=["glg", "grr"], writes=[f"ind:{tt}"])
                sb.op("pe", lambda e, tt=tt: e.transpose(
                    pti[:, 0:128], ind[:, tt * 8:(tt + 1) * 8], idnf[:]),
                    reads=[f"ind:{tt}", "idnf"], writes=["gpti"])
                sb.op("dve", lambda e, tt=tt: e.tensor_copy(
                    indf[:, tt * 128:(tt + 1) * 128], pti[:, 0:128]),
                    reads=["gpti"], writes=[f"indf:{tt}"])
        sb.barrier()
        with nc.psum_tensor("g_ptm", [128, 512], F32) as ptm:
            sb.op("dve", lambda e: e.memset(ones8[:], 1.0), writes=["on8"])
            sb.op("dve", lambda e: e.tensor_tensor_scan(
                rkf[:], ones8[:], indf[:], 0.0, ALU.mult, ALU.add),
                reads=["on8"] + [f"indf:{i}" for i in range(NTT)],
                writes=["rkf"])
            for tt in range(NTT):
                sb.op("pe", lambda e, tt=tt: e.transpose(
                    ptm[:, 0:8], rkf[:, tt * 128:(tt + 1) * 128], idnf[0:8, 0:8]),
                    reads=["rkf", "idnf"], writes=["gptm"])
                sb.op("dve", lambda e: e.tensor_copy(rktm[:], ptm[:, 0:8]),
                      reads=["gptm"], writes=["rktm"])
                sb.op("dve", lambda e: e.tensor_add(rr[:], rktm[:], ecap[:]),
                      reads=["rktm", "ecap"], writes=["grr"])
                sb.op("dve", lambda e: e.tensor_scalar(
                    rr[:], rr[:], -1.0 - BIG, None, ALU.add),
                    reads=["grr"], writes=["grr"])
                sb.op("dve", lambda e, tt=tt: e.tensor_mul(
                    rr[:], ind[:, tt * 8:(tt + 1) * 8], rr[:]),
                    reads=[f"ind:{tt}", "grr"], writes=["grr"])
                sb.op("dve", lambda e: e.tensor_scalar(
                    rr[:], rr[:], float(BIG), None, ALU.add),
                    reads=["grr"], writes=["grr"])
                sb.op("dve", lambda e: e.tensor_reduce(
                    slf[:], rr[:], AX.X, ALU.min),
                    reads=["grr"], writes=["slf"])
                sb.op("dve", lambda e: e.tensor_copy(sli[:], slf[:]),
                      reads=["slf"], writes=["sli"])
                sb.op("pool", lambda e, tt=tt: e.indirect_dma_start(
                    t["idx1_d"][0:128, :], IoA(sli[:], 0),
                    tids[:, tt:tt + 1], None),
                    reads=["sli", "tids", "idx1pre"], writes=["idxsc"],
                    slot="st3")
        sb.barrier()
        rst.close()
        with ExitStack() as st2:
            fc = st2.enter_context(nc.sbuf_tensor("g_fc", [128, NC_ * 4096], BF16))
            pr_ = st2.enter_context(nc.sbuf_tensor("g_pr", [128, 32 * 1024], BF16))
            hb = st2.enter_context(nc.sbuf_tensor("g_hb", [128, 32 * 512], BF16))
            rt_t = st2.enter_context(nc.sbuf_tensor("g_rtt", [128, 512], BF16))
            xgt = st2.enter_context(nc.sbuf_tensor("g_xgt", [128, 4 * 1024], BF16))
            xgf = st2.enter_context(nc.sbuf_tensor("g_xgf", [128, NC_ * 512], BF16))
            idxj = st2.enter_context(nc.sbuf_tensor("g_idxj", [128, 8], I32))
            gtmp = [st2.enter_context(nc.sbuf_tensor(f"g_gt{i}", [128, 512], F32))
                    for i in range(4)]
            sb.op("dve", lambda e: e.memset(xgt[:], 0.0), writes=["xgt_z"])
            with ExitStack() as pst:
                ph = [pst.enter_context(nc.psum_tensor(f"g_ph{i}", [128, 512], F32))
                      for i in range(2)]
                po = [pst.enter_context(nc.psum_tensor(f"g_po{i}", [128, 512], F32))
                      for i in range(4)]
                ptg = pst.enter_context(nc.psum_tensor("g_ptg", [128, 2048], BF16))
                for j in range(8):
                    sb.op("pool", lambda e, j=j: e.dma_start(
                        out=fc[:].rearrange("p (c n) -> p c n", c=NC_),
                        in_=t["fcs"][j].rearrange("c p n -> p c n")),
                        writes=["gfc"], slot="w1s")
                    for dh in range(2):
                        sb.op("pool", lambda e, j=j, dh=dh: e.dma_start(
                            out=pr_[:, dh * 16384:(dh + 1) * 16384].rearrange(
                                "p (c n) -> p c n", c=32),
                            in_=t["prs"][j, dh].rearrange("c p n -> p c n")),
                            writes=[f"gprj:{dh}"], slot=f"w{2 + dh}s")
                    sb.op("sp", lambda e, j=j: e.dma_start(
                        out=idxj[:],
                        in_=t["idx1_d"][j * C1:(j + 1) * C1, :].rearrange(
                            "(p g) o -> p (g o)", p=128)),
                        reads=["idxsc"], writes=["idxj"], slot="ld4")
                    for blk in range(2):
                        for g in range(4):
                            jj = blk * 4 + g
                            sb.op("pool", lambda e, g=g, jj=jj: e.indirect_dma_start(
                                xgt[:, g * 1024:(g + 1) * 1024], None,
                                t["xn1tm_d"][0:128, :], IoA(idxj[:, jj:jj + 1], 0)),
                                reads=["idxj", "xgt_z"]
                                      + [f"xn1tm:{i}" for i in range(NTT)],
                                writes=[f"xg:{g}"], slot="ld5")
                        for g in range(4):
                            for c in range(NC_):
                                pb = (g * NC_ + c) % 2
                                sb.op("pe", lambda e, g=g, c=c, pb=pb: e.transpose(
                                    ptg[:, pb * 1024:pb * 1024 + 128],
                                    xgt[:, g * 1024 + c * 128:g * 1024 + (c + 1) * 128],
                                    idn[:]),
                                    reads=[f"xg:{g}", "idn"], writes=[f"gptg:{pb}"])
                                sb.op("dve", lambda e, g=g, c=c, pb=pb: e.tensor_copy(
                                    xgf[:, c * 512 + g * 128:c * 512 + (g + 1) * 128],
                                    ptg[:, pb * 1024:pb * 1024 + 128]),
                                    reads=[f"gptg:{pb}"], writes=[f"xgf:{c}"])
                        for mh in range(32):
                            pp = ph[mh % 2]
                            for c in range(NC_):
                                sb.op("pe", lambda e, mh=mh, c=c, pp=pp: e.matmul(
                                    pp[:], fc[:, c * 4096 + mh * 128:c * 4096 + (mh + 1) * 128],
                                    xgf[:, c * 512:(c + 1) * 512],
                                    start=(c == 0), stop=(c == NC_ - 1)),
                                    reads=["gfc", f"xgf:{c}"],
                                    writes=[f"gph:{mh % 2}"],
                                    inc=(c == NC_ - 1))
                            sb.op("act", lambda e, mh=mh, pp=pp: e.activation(
                                rt_t[:], pp[:], AF.Relu),
                                reads=[f"gph:{mh % 2}"], writes=["grt"])
                            sb.op("dve", lambda e, mh=mh: e.tensor_mul(
                                hb[:, mh * 512:(mh + 1) * 512], rt_t[:], rt_t[:]),
                                reads=["grt"], writes=[f"ghb:{mh}"])
                        for dh in range(2):
                            dst = t["g1a_d"] if dh == 0 else t["g1b_d"]
                            for ti in range(4):
                                jj = blk * 4 + ti
                                pp = po[2 * dh + ti % 2]
                                for hc in range(32):
                                    sb.op("pe", lambda e, hc=hc, ti=ti, dh=dh, pp=pp: e.matmul(
                                        pp[:],
                                        hb[:, hc * 512 + ti * 128:hc * 512 + (ti + 1) * 128],
                                        pr_[:, dh * 16384 + hc * 512:dh * 16384 + (hc + 1) * 512],
                                        start=(hc == 0), stop=(hc == 31)),
                                        reads=[f"ghb:{hc}", f"gprj:{dh}"],
                                        writes=[f"gpo:{2 * dh + ti % 2}"],
                                        inc=(hc == 31))
                                gb = gtmp[2 * dh + ti % 2]
                                sb.op("dve", lambda e, gb=gb, pp=pp: e.tensor_copy(
                                    gb[:], pp[:]),
                                    reads=[f"gpo:{2 * dh + ti % 2}"],
                                    writes=[f"ggt:{2 * dh + ti % 2}"])
                                sb.op("pool", lambda e, jj=jj, dh=dh, gb=gb, dst=dst: e.indirect_dma_start(
                                    dst[0:128, :], IoA(idxj[:, jj:jj + 1], 0),
                                    gb[:], None),
                                    reads=[f"ggt:{2 * dh + ti % 2}", "idxj"],
                                    writes=[f"g1sc:{dh}"], slot=f"gm{2 * dh + ti % 2}")
    sb.barrier()


def emit_s7(p, b):
    nc, sb, t = p.nc, p.sb, p.t
    sb.barrier()
    with ExitStack() as st:
        gfm = st.enter_context(nc.sbuf_tensor(f"o_g{b}", [128, NC_ * S], F32))
        gt = st.enter_context(nc.sbuf_tensor(f"o_gt{b}", [128, 1024], F32))
        gs = st.enter_context(nc.sbuf_tensor(f"o_gs{b}", [128, 1024], BF16))
        idn = st.enter_context(nc.sbuf_tensor(f"o_idn{b}", [128, 128], BF16))
        xs_ = [st.enter_context(nc.sbuf_tensor(f"o_xs{b}{i}", [128, S], F32))
               for i in range(2)]
        ss_ = [st.enter_context(nc.sbuf_tensor(f"o_ss{b}{i}", [128, S], F32))
               for i in range(2)]
        ov = st.enter_context(nc.sbuf_tensor(f"o_ov{b}", [128, S], F32))
        t12 = st.enter_context(nc.sbuf_tensor(f"o_t12{b}", [128, S], F32))
        sb.op("sp", lambda e: e.dma_start(out=idn[:], in_=t["idn"]),
              writes=["idn"], slot="bia")
        with nc.psum_tensor(f"o_pt{b}", [128, 2048], BF16) as pt:
            for tloc in range(16):
                tt = b * 16 + tloc
                sb.op("sp", lambda e, tt=tt: e.dma_start(
                    out=gt[:, 0:512], in_=t["g1a_d"][tt * 128:(tt + 1) * 128, :]),
                    reads=["g1sc:0"], writes=["gt"], slot="ld0")
                sb.op("sp", lambda e, tt=tt: e.dma_start(
                    out=gt[:, 512:1024], in_=t["g1b_d"][tt * 128:(tt + 1) * 128, :]),
                    reads=["g1sc:1"], writes=["gt"], slot="ld6")
                sb.op("act", lambda e: e.activation(gs[:], gt[:], AF.Silu),
                      reads=["gt"], writes=["gs"])
                for c in range(NC_):
                    pb = c % 2
                    sb.op("pe", lambda e, c=c, pb=pb: e.transpose(
                        pt[:, pb * 1024:pb * 1024 + 128],
                        gs[:, c * 128:(c + 1) * 128], idn[:]),
                        reads=["gs", "idn"], writes=[f"ptr:{pb}"])
                    sb.op("dve", lambda e, c=c, tloc=tloc, pb=pb: e.tensor_copy(
                        gfm[:, c * S + tloc * 128:c * S + (tloc + 1) * 128],
                        pt[:, pb * 1024:pb * 1024 + 128]),
                        reads=[f"ptr:{pb}"], writes=[f"gfm:{c}"])
        for c in range(NC_):
            xb = xs_[c % 2]
            sbf = ss_[c % 2]
            sb.op("sp", lambda e, c=c, xb=xb: e.dma_start(
                out=xb[:], in_=t["x1_d"][b, c * 128:(c + 1) * 128, :]),
                reads=[f"x1_d:{b}:{c}"], writes=[f"oxs:{c % 2}"],
                slot=f"ld{1 + c % 2}")
            sb.op("sp", lambda e, c=c, sbf=sbf: e.dma_start(
                out=sbf[:], in_=t["s1_d"][b, c * 128:(c + 1) * 128, :]),
                reads=[f"s1_d:{b}:{c}"], writes=[f"oss:{c % 2}"],
                slot=f"ld{3 + c % 2}")
            sb.op("dve", lambda e, c=c, sbf=sbf: e.tensor_mul(
                t12[:], sbf[:], gfm[:, c * S:(c + 1) * S]),
                reads=[f"oss:{c % 2}", f"gfm:{c}"], writes=["t12"])
            sb.op("dve", lambda e, c=c, xb=xb: e.tensor_add(ov[:], xb[:], t12[:]),
                  reads=[f"oxs:{c % 2}", "t12"], writes=["ov"])
            sb.op("sp", lambda e, c=c: e.dma_start(
                out=t["yout"][b, c * 128:(c + 1) * 128, :], in_=ov[:]),
                reads=["ov"], writes=[f"yout:{b}:{c}"], slot="st0")
    sb.barrier()


def build_nc(stages=("s0",), dump=()):
    nc = bass.Bass("TRN2", target_bir_lowering=False)
    t = {}
    for name, shape, dt in _IN_SPECS:
        t[name] = nc.dram_tensor(name, shape, dt, kind="ExternalInput").ap()
    for name, shape, dt in _SCRATCH:
        kind = "ExternalOutput" if name in dump else "Internal"
        t[name] = nc.dram_tensor(name, shape, dt, kind=kind).ap()
    t["yout"] = nc.dram_tensor(
        "yout", [B_PER_CORE, 1024, 2048], F32, kind="ExternalOutput").ap()

    with ExitStack() as stack:
        sb = SB(nc, stack)
        p = P(nc, sb, t)
        p.cst = stack.enter_context(nc.sbuf_tensor("csts", [128, 8], F32))
        sb.op("dve", lambda e: e.memset(p.cst[:, 0:1], float(LN_EPS)), writes=["cst"])
        sb.op("dve", lambda e: e.memset(p.cst[:, 1:2], float(RMS_EPS)), writes=["cst"])
        p.dump = set(dump)
        if "s0" in stages:
            for b in range(B_PER_CORE):
                emit_s0(p, b)
        if "s12" in stages:
            for b in range(B_PER_CORE):
                emit_s12(p, b)
        if "s3" in stages:
            emit_s3(p)
        if "s4" in stages:
            for b in range(B_PER_CORE):
                emit_s4(p, b)
        if "s5" in stages:
            for b in range(B_PER_CORE):
                emit_s5(p, b)
        if "s6" in stages:
            emit_s6(p)
        if "s7" in stages:
            for b in range(B_PER_CORE):
                emit_s7(p, b)
        sb.finalize()
        import os
        if os.environ.get("MK_DEBUG_COUNT"):
            print("instr counts:", {e: len(v) for e, v in sb.prog.items()})

        def run_prog(lst):
            def body(e):
                for it in lst:
                    if isinstance(it, dict):
                        ins = it["fn"](e)
                        if it["inc"] is not None:
                            h, n = it["inc"]
                            ins.then_inc(h, n)
                    else:
                        it(e)
            return body

        with nc.Block() as blk:
            blk.sync(run_prog(sb.prog["sp"]))
            blk.gpsimd(run_prog(sb.prog["pool"]))
            blk.tensor(run_prog(sb.prog["pe"]))
            blk.vector(run_prog(sb.prog["dve"]))
            blk.scalar(run_prog(sb.prog["act"]))
    return nc


def kernel(**inputs):
    H = _prep(inputs)
    x_fm = H.pop("_x_fm")
    nc = build_nc(stages=("s0", "s12", "s3", "s4", "s5", "s6", "s7"))
    in_maps = []
    for c in range(8):
        m = dict(H)
        m["x_fm"] = np.ascontiguousarray(x_fm[2 * c:2 * c + 2])
        in_maps.append(m)
    res = run_bass_kernel_spmd(nc, in_maps, core_ids=list(range(8)))
    out = np.concatenate([r["yout"] for r in res.results], axis=0)
    return np.ascontiguousarray(out.transpose(0, 2, 1))
